# version 09
# baseline (speedup 1.0000x reference)
import contextlib
import math
import numpy as np
import ml_dtypes
import concourse.bass as bass
import concourse.mybir as mybir
from concourse.bass_utils import run_bass_kernel_spmd
from concourse.alu_op_type import AluOpType

F32 = mybir.dt.float32
BF16 = mybir.dt.bfloat16
I32 = mybir.dt.int32
U8 = mybir.dt.uint8
AF = mybir.ActivationFunctionType
BF = ml_dtypes.bfloat16

QT_EVEN = [0, 3, 4, 7, 8, 11, 12, 15]
QT_ODD = [1, 2, 5, 6, 9, 10, 13, 14]
SCALE = 1.0 / math.sqrt(128.0)
EPS = 1e-5
NH = 32
NHG = 8
CH = 512
NCHUNK = 4096 // CH
VE_DQ_OT = 4 + 1 + NCHUNK * 4
VE1_TT = 5

def q0_of(kt):
    return 128 * (kt // 2)
ATW = [1024 - q0_of(kt) for kt in range(16)]
ATO = [0] * 16
for _kt in range(1, 16):
    ATO[_kt] = ATO[_kt - 1] + ATW[_kt - 1]
ATT_TOT = ATO[15] + ATW[15]

def q_ranges(kt):
    q0 = q0_of(kt)
    return [(q0, 512), (512, 1024)] if q0 < 512 else [(q0, 1024)]


def build_nc():
    nc = bass.Bass("TRN2", target_bir_lowering=False)
    x_own = nc.dram_tensor("x_own", [1024, 4096], F32, kind="ExternalInput").ap()
    wq4 = nc.dram_tensor("wq4", [32, 128, 32, 128], F32, kind="ExternalInput").ap()
    wk4 = nc.dram_tensor("wk4", [32, 128, 32, 128], F32, kind="ExternalInput").ap()
    wv4 = nc.dram_tensor("wv4", [32, 128, 32, 128], F32, kind="ExternalInput").ap()
    wot = nc.dram_tensor("wot", [4096, 4096], F32, kind="ExternalInput").ap()
    bq_t = nc.dram_tensor("bq_t", [128, 32], F32, kind="ExternalInput").ap()
    bk_t = nc.dram_tensor("bk_t", [128, 32], F32, kind="ExternalInput").ap()
    bv_t = nc.dram_tensor("bv_t", [128, 32], F32, kind="ExternalInput").ap()
    bo_row = nc.dram_tensor("bo_row", [1, 4096], F32, kind="ExternalInput").ap()
    bf_row = nc.dram_tensor("bf_row", [1, 4096], F32, kind="ExternalInput").ap()
    qw_own = nc.dram_tensor("qw_own", [4, 512, 128], I32, kind="ExternalInput").ap()
    lut_own = nc.dram_tensor("lut_own", [512, 16], F32, kind="ExternalInput").ap()
    masks_in = nc.dram_tensor("masks_in", [128, 2048], BF16, kind="ExternalInput").ap()
    out_own = nc.dram_tensor("out_own", [1024, 4096], F32, kind="ExternalOutput").ap()
    kag_inA = nc.dram_tensor("kag_inA", [2048, 1024], BF16).ap()
    kag_inB = nc.dram_tensor("kag_inB", [2048, 1024], BF16).ap()
    kag_outA = nc.dram_tensor("kag_outA", [4096, 1024], BF16).ap()
    kag_outB = nc.dram_tensor("kag_outB", [4096, 1024], BF16).ap()
    vag_in = nc.dram_tensor("vag_in", [1024, 4096], BF16).ap()
    vag_out = nc.dram_tensor("vag_out", [2048, 4096], BF16).ap()
    wlag_in = nc.dram_tensor("wlag_in", [4096, 512], BF16).ap()
    wlag_out = nc.dram_tensor("wlag_out", [32768, 512], BF16).ap()
    ao_scr = nc.dram_tensor("ao_scr", [8, 128, 32, 128], BF16).ap()
    x2_scr = nc.dram_tensor("x2_scr", [1024, 4096], F32).ap()

    es = contextlib.ExitStack()
    with es:
        AR = es.enter_context(nc.sbuf_tensor("AR", [128, 204 * 1024], U8))
        arf, arb, ari, aru = AR.bitcast(F32), AR.bitcast(BF16), AR.bitcast(I32), AR.bitcast(U8)

        def fsl(o, n):
            assert o % 64 == 0
            return arf.ap()[:, o // 4:o // 4 + n]

        def bsl(o, n):
            assert o % 64 == 0
            return arb.ap()[:, o // 2:o // 2 + n]

        def isl(o, n):
            assert o % 64 == 0
            return ari.ap()[:, o // 4:o // 4 + n]

        def usl(o, n):
            assert o % 64 == 0
            return aru.ap()[:, o:o + n]

        masks = bsl(0, 2048)
        biasB = bsl(4096, 4096)
        rzb = fsl(12288, 1024)
        rzrow = fsl(16384, 1024)
        ident = fsl(20480, 128)
        identb = bsl(23040, 128)
        iota_i = isl(20992, 128)
        onesr = fsl(21504, 128)
        onesb = bsl(22016, 32)
        bqs = fsl(22080, 32)
        bk_sb = fsl(22208, 32)
        bv_sb = fsl(22336, 32)
        bn6 = fsl(22464, 48)
        mv = fsl(22656, 2)
        sq = fsl(22720, 1)
        rstd = fsl(22784, 1)
        nmr = fsl(22848, 1)
        eps_t = fsl(22912, 1)
        S1 = 24 * 1024
        xn_dm = bsl(S1, 32768)
        S2 = 88 * 1024
        bits = [usl(S2 + 4096 * i, 4096) for i in range(4)]
        qw_i32 = isl(S2 + 16384, 512)
        qw_u8 = usl(S2 + 16384, 2048)
        lut_t = fsl(S2 + 18432, 16)
        dlev = fsl(S2 + 18496, 8)
        tbuf = fsl(S2 + 18560, 8 * CH)
        ubuf = fsl(S2 + 34944, 4 * CH)
        vbuf = fsl(S2 + 43136, 2 * CH)
        wl_f2 = [fsl(S2 + 47232 + 4096 * i, 1024) for i in range(4)]
        wlt_ev = [bsl(4096 + 2048 * i, 1024) for i in range(2)]
        attT = bsl(S2, ATT_TOT)
        Q_hg = [bsl(S2 + 18432 + 8192 * i, 4096) for i in range(2)]
        k_sb = [bsl(S2 + 34816 + 4096 * i, 2048) for i in range(2)]
        v_sb = [bsl(S2 + 43008 + 4096 * i, 2048) for i in range(2)]
        ao_sb = [bsl(S2 + 51200 + 2048 * i, 1024) for i in range(2)]
        wbig = bsl(S2, 32768)
        R3 = 152 * 1024
        vexp = [bsl(R3 + 24576 + 2048 * i, 1024) for i in range(2)]
        xvf = [fsl(R3 + 16384 + 4096 * i, 1024) for i in range(2)]
        x_io = [fsl(R3 + 16384 * i, 4096) for i in range(2)]
        xnf = fsl(R3 + 32768, 4096)
        wcol = [bsl(R3 + 8192 * i, 4096) for i in range(2)]
        kexp = [bsl(R3 + 16384 + 2048 * i, 1024) for i in range(2)]
        aost = [bsl(R3 + 8192 * i, 4096) for i in range(2)]
        xch = [fsl(R3 + 16384 + 4096 * i, 1024) for i in range(2)]
        x2o = [fsl(R3 + 24576 + 4096 * i, 1024) for i in range(2)]
        x2ch = [fsl(R3 + 4096 * i, 1024) for i in range(2)]
        outb = [fsl(R3 + 8192 + 4096 * i, 1024) for i in range(2)]

        PA = es.enter_context(nc.psum_tensor("PA", [128, 1024], F32))
        PB = es.enter_context(nc.psum_tensor("PB", [128, 1024], F32))
        PC = es.enter_context(nc.psum_tensor("PC", [128, 1024], F32))
        PD = es.enter_context(nc.psum_tensor("PD", [128, 1024], F32))

        def sem(name):
            return es.enter_context(nc.semaphore(name))
        ld_misc = sem("ld_misc"); ld_qw = sem("ld_qw")
        ld_x1 = [sem("ld_x1a"), sem("ld_x1b")]
        ld_wv = [sem("ld_wva"), sem("ld_wvb")]
        ld_wk = [sem("ld_wka"), sem("ld_wkb")]
        ld_wq = [sem("ld_wqa"), sem("ld_wqb")]
        ld_wo = sem("ld_wo"); ld_wl = sem("ld_wl")
        ld_kv = [sem("ld_kva"), sem("ld_kvb")]
        ld_ao = [sem("ld_aoa"), sem("ld_aob")]
        ld_x6 = [sem("ld_x6a"), sem("ld_x6b")]
        ld_x27 = [sem("ld_x27a"), sem("ld_x27b")]
        ld_x28 = [sem("ld_x28a"), sem("ld_x28b")]
        exp_v = [sem("exp_va"), sem("exp_vb")]
        exp_k = [sem("exp_ka"), sem("exp_kb"), sem("exp_kc"), sem("exp_kd")]
        exp_wl = [sem("exp_wla"), sem("exp_wlb")]
        cc_k1 = sem("cc_k1"); cc_k2 = sem("cc_k2"); cc_v = sem("cc_v"); cc_wl = sem("cc_wl")
        st_ao = [sem("st_aoa"), sem("st_aob")]
        st_x2 = [sem("st_x2a"), sem("st_x2b")]
        st_out = [sem("st_outa"), sem("st_outb")]
        ge = sem("ge"); ve0 = sem("ve0")
        ve1 = sem("ve1"); se1 = sem("se1"); pe1 = sem("pe1"); ve1b = sem("ve1b")
        pe_bb = sem("pe_bb"); ve_bb = sem("ve_bb")
        pe2a = sem("pe2a"); se2a = sem("se2a"); pe_vt = sem("pe_vt"); se_vx = sem("se_vx")
        pe2b = sem("pe2b"); se2b = sem("se2b")
        ve_dq = sem("ve_dq"); pe_wt = sem("pe_wt"); ve_wt = sem("ve_wt")
        pe_sc = sem("pe_sc"); pe_avz = sem("pe_avz"); pe_rzb = sem("pe_rzb")
        se_exp = sem("se_exp"); se5q = sem("se5q")
        ve_mask = sem("ve_mask"); ve_rz = sem("ve_rz"); ve_fin = sem("ve_fin")
        pe6 = sem("pe6"); ve6 = sem("ve6")
        ve7 = sem("ve7"); se7 = sem("se7"); pe7 = sem("pe7"); ve7b = sem("ve7b")
        pe8 = sem("pe8"); ve8 = sem("ve8"); ve_mid = sem("ve_mid")

        block = es.enter_context(nc.Block())

        def dq_chunk_done(ot, c_local):
            return ot * VE_DQ_OT + 5 + 4 * (c_local + 1)

        @block.sync
        def _(s):
            s.dma_start(out=masks, in_=masks_in).then_inc(ld_misc, 16)
            s.dma_start(out=bqs, in_=bq_t).then_inc(ld_misc, 16)
            s.dma_start(out=bk_sb, in_=bk_t).then_inc(ld_misc, 16)
            s.dma_start(out=bv_sb, in_=bv_t).then_inc(ld_misc, 16)
            for tt in range(8):
                if tt >= 2:
                    s.wait_ge(ve1, VE1_TT * (tt - 1))
                s.dma_start(out=x_io[tt % 2],
                            in_=x_own[128 * tt:128 * (tt + 1), :]).then_inc(ld_x1[tt % 2], 16)
            for mt in range(32):
                s.wait_ge(se_vx, mt + 1)
                for tt in range(8):
                    s.dma_start(out=vag_in[128 * tt:128 * (tt + 1), 128 * mt:128 * (mt + 1)],
                                in_=vexp[mt % 2][:, 128 * tt:128 * (tt + 1)]).then_inc(exp_v[mt % 2], 16)
                if mt % 2 == 1:
                    G = mt // 2
                    s.wait_ge(ve_wt, G + 1)
                    ot, gq = G // 4, G % 4
                    for j in range(8):
                        it = 8 * gq + j
                        s.dma_start(out=wlag_in[128 * it:128 * (it + 1), 128 * ot:128 * (ot + 1)],
                                    in_=wlt_ev[G % 2][:, 128 * j:128 * (j + 1)]).then_inc(exp_wl[G % 2], 16)
            for mt in range(32):
                s.wait_ge(se2b, mt + 1)
                half = mt // 16
                mloc = mt % 16
                kag = kag_inA if half == 0 else kag_inB
                s.dma_start(out=kag[128 * mloc:128 * (mloc + 1), :],
                            in_=kexp[mt % 2]).then_inc(exp_k[2 * half + mt % 2], 16)
            kr2A = kag_outA.rearrange("(r f) (j w t) -> r f j w t", r=2, j=4, w=2, t=128)
            kr2B = kag_outB.rearrange("(r f) (j w t) -> r f j w t", r=2, j=4, w=2, t=128)
            vr = vag_out.rearrange("(r j w p) f -> r p j w f", r=2, j=4, w=2, p=128)
            for H in range(NH):
                par = H % 2
                if H == 0:
                    s.wait_ge(cc_k1, 1)
                    s.wait_ge(cc_v, 1)
                if H == 16:
                    s.wait_ge(cc_k2, 1)
                if H < 2:
                    s.wait_ge(ve_dq, 4 * VE_DQ_OT)
                    s.wait_ge(ve_wt, 16)
                if H >= 2:
                    s.wait_ge(pe_sc, 16 * (H - 1))
                    s.wait_ge(pe_avz, 16 * (H - 1))
                for rp in range(2):
                    for w in range(2):
                        goff = (384 * w) if rp == 0 else (128 + 128 * w)
                        dk = k_sb[par].rearrange("p (j rest) -> p j rest", j=4)[:, :, goff:goff + 128]
                        kr2h = kr2A if H < 16 else kr2B
                        Hl = H % 16
                        s.dma_start(out=dk, in_=kr2h[rp, 128 * Hl:128 * (Hl + 1), :, w, :]).then_inc(ld_kv[par], 16)
                        dv = v_sb[par].rearrange("p (j rest) -> p j rest", j=4)[:, :, goff:goff + 128]
                        s.dma_start(out=dv, in_=vr[rp, :, :, w, 128 * H:128 * (H + 1)]).then_inc(ld_kv[par], 16)
            for c in range(4):
                if c == 0:
                    s.wait_ge(ve_fin, 2 * NH)
                if c > 0:
                    s.wait_ge(pe_bb, c)
                s.dma_start(out=rzrow[0:1, :],
                            in_=bo_row[0:1, 1024 * c:1024 * (c + 1)]).then_inc(ld_misc, 16)
            for oc in range(4):
                for tt in range(8):
                    i = 8 * oc + tt
                    if i == 0:
                        s.wait_ge(st_ao[0], 16 * 8 * NH // 2)
                        s.wait_ge(st_ao[1], 16 * 8 * NH // 2)
                        s.wait_ge(se5q, 4 * NHG)
                    if i >= 2:
                        s.wait_ge(pe6, i - 1)
                    s.dma_start(out=aost[tt % 2],
                                in_=ao_scr[tt].rearrange("p it t -> p (it t)")).then_inc(ld_ao[i % 2], 16)
                    if i >= 2:
                        s.wait_ge(ve6, i - 1)
                    s.dma_start(out=xch[tt % 2],
                                in_=x_own[128 * tt:128 * (tt + 1), 1024 * oc:1024 * (oc + 1)]).then_inc(ld_x6[i % 2], 16)
            for tt in range(8):
                s.wait_ge(st_x2[0], 16 * 16)
                s.wait_ge(st_x2[1], 16 * 16)
                if tt < 2:
                    s.wait_ge(pe6, 32)
                    s.wait_ge(ve6, 32)
                if tt >= 2:
                    s.wait_ge(ve7, VE1_TT * (tt - 1))
                s.dma_start(out=x_io[tt % 2],
                            in_=x2_scr[128 * tt:128 * (tt + 1), :]).then_inc(ld_x27[tt % 2], 16)
            for c in range(4):
                if c == 0:
                    s.wait_ge(ve7b, 32)
                s.wait_ge(pe_bb, 4 + c)
                s.dma_start(out=rzrow[0:1, :],
                            in_=bf_row[0:1, 1024 * c:1024 * (c + 1)]).then_inc(ld_misc, 16)
            for rp in range(4):
                for tt in range(8):
                    i = 8 * rp + tt
                    if i < 2:
                        s.wait_ge(ve7, VE1_TT * 8)
                    if i >= 2:
                        s.wait_ge(ve8, i - 1)
                    s.dma_start(out=x2ch[tt % 2],
                                in_=x2_scr[128 * tt:128 * (tt + 1), 1024 * rp:1024 * (rp + 1)]).then_inc(ld_x28[i % 2], 16)
            s.wait_ge(st_out[0], 16 * 16)
            s.wait_ge(st_out[1], 16 * 16)

        @block.gpsimd
        def _(g):
            g.iota(iota_i, pattern=[[1, 128]], base=0, channel_multiplier=-1).then_inc(ge, 1)
            qw_r = qw_own.rearrange("pl (t r) w -> t r pl w", t=4, r=128)
            g.dma_start(out=qw_i32, in_=qw_r[0]).then_inc(ld_qw, 16)
            g.dma_start(out=lut_t, in_=lut_own[0:128, :]).then_inc(ld_qw, 16)
            for mt in range(32):
                if mt < 2:
                    g.wait_ge(ve1, VE1_TT * 8)
                    g.wait_ge(pe1, 32)
                if mt >= 2:
                    g.wait_ge(pe2a, mt - 1)
                g.dma_start(out=wcol[mt % 2], in_=wv4[mt]).then_inc(ld_wv[mt % 2], 16)
                if mt in (7, 15, 23):
                    ot = (mt + 1) // 8
                    g.wait_ge(ve_dq, ot * VE_DQ_OT)
                    g.dma_start(out=qw_i32, in_=qw_r[ot]).then_inc(ld_qw, 16)
                    g.dma_start(out=lut_t, in_=lut_own[128 * ot:128 * (ot + 1), :]).then_inc(ld_qw, 16)
            g.wait_ge(exp_v[0], 16 * 8 * 16)
            g.wait_ge(exp_v[1], 16 * 8 * 16)
            g.collective_compute("AllGather", AluOpType.bypass,
                                 replica_groups=[[0, 1], [2, 3], [4, 5], [6, 7]],
                                 ins=[vag_in.opt()], outs=[vag_out.opt()]).then_inc(cc_v)
            g.wait_ge(exp_wl[0], 16 * 64)
            g.wait_ge(exp_wl[1], 16 * 64)
            g.collective_compute("AllGather", AluOpType.bypass,
                                 replica_groups=[[0, 1, 2, 3, 4, 5, 6, 7]],
                                 ins=[wlag_in.opt()], outs=[wlag_out.opt()]).then_inc(cc_wl)
            for mt in range(32):
                if mt < 2:
                    g.wait_ge(pe2a, 31 + mt)
                if mt >= 2:
                    g.wait_ge(pe2b, mt - 1)
                g.dma_start(out=wcol[mt % 2], in_=wk4[mt]).then_inc(ld_wk[mt % 2], 16)
                if mt == 19:
                    g.wait_ge(exp_k[0], 16 * 8)
                    g.wait_ge(exp_k[1], 16 * 8)
                    g.collective_compute("AllGather", AluOpType.bypass,
                                         replica_groups=[[0, 1], [2, 3], [4, 5], [6, 7]],
                                         ins=[kag_inA.opt()], outs=[kag_outA.opt()]).then_inc(cc_k1)
            g.wait_ge(exp_k[2], 16 * 8)
            g.wait_ge(exp_k[3], 16 * 8)
            g.collective_compute("AllGather", AluOpType.bypass,
                                 replica_groups=[[0, 1], [2, 3], [4, 5], [6, 7]],
                                 ins=[kag_inB.opt()], outs=[kag_outB.opt()]).then_inc(cc_k2)
            for mt in range(32):
                if mt < 2:
                    g.wait_ge(pe2b, 31 + mt)
                else:
                    g.wait_ge(se5q, mt - 1)
                g.dma_start(out=wcol[mt % 2], in_=wq4[mt]).then_inc(ld_wq[mt % 2], 16)
            wor = wot.rearrange("(it p) f -> it p f", it=32)
            wbr = wbig.rearrange("p (it f) -> p it f", it=32)
            for oc in range(4):
                if oc == 0:
                    g.wait_ge(pe_avz, 16 * NH)
                    g.wait_ge(st_ao[0], 16 * 8 * NH // 2)
                    g.wait_ge(st_ao[1], 16 * 8 * NH // 2)
                else:
                    g.wait_ge(pe6, 8 * oc)
                for it in range(32):
                    g.dma_start(out=wbr[:, it, :],
                                in_=wor[it][:, 1024 * oc:1024 * (oc + 1)]).then_inc(ld_wo, 16)
            wlr = wlag_out.rearrange("(r it p) o -> r it p o", r=8, it=32)
            wbr2 = wbig.rearrange("p (it rb f) -> p it rb f", it=32, rb=2)
            for rp in range(4):
                if rp == 0:
                    g.wait_ge(cc_wl, 1)
                    g.wait_ge(pe6, 32)
                else:
                    g.wait_ge(pe8, 8 * rp)
                for rb in range(2):
                    for it in range(32):
                        g.dma_start(out=wbr2[:, it, rb, :],
                                    in_=wlr[2 * rp + rb, it]).then_inc(ld_wl, 16)

        @block.vector
        def _(v):
            v.wait_ge(ge, 1)
            v.tensor_scalar(out=ident, in0=iota_i, scalar1=0.0, scalar2=None,
                            op0=AluOpType.is_equal)
            v.tensor_scalar(out=identb, in0=iota_i, scalar1=0.0, scalar2=None,
                            op0=AluOpType.is_equal)
            v.memset(onesb[:, 0:1], 1.0)
            v.memset(onesr[0:1, :], 1.0)
            v.memset(eps_t, EPS)
            v.wait_ge(ld_misc, 64)
            v.tensor_scalar(out=bqs, in0=bqs, scalar1=SCALE, scalar2=None,
                            op0=AluOpType.mult).then_inc(ve0, 1)

            def ln_tile(tt, ld_sem, vsem, ssem, psem, x_src):
                v.wait_ge(ld_sem[tt % 2], 16 * (tt // 2 + 1))
                last = None
                for i in range(8):
                    last = v.bn_stats(bn6[:, 6 * i:6 * i + 6], x_src[:, 512 * i:512 * (i + 1)])
                last.then_inc(vsem, 1)
                v.wait_ge(vsem, VE1_TT * tt + 1)
                v.bn_aggr(mv, bn6).then_inc(vsem, 1)
                v.wait_ge(ssem, tt + 1)
                v.reciprocal(rstd, sq).then_inc(vsem, 1)
                v.wait_ge(vsem, VE1_TT * tt + 3)
                v.tensor_scalar(out=nmr, in0=mv[:, 0:1], scalar1=rstd, scalar2=-1.0,
                                op0=AluOpType.mult, op1=AluOpType.mult).then_inc(vsem, 1)
                v.wait_ge(vsem, VE1_TT * tt + 4)
                if tt >= 1:
                    v.wait_ge(psem, 4 * tt)
                v.tensor_scalar(out=xnf, in0=x_src, scalar1=rstd, scalar2=nmr,
                                op0=AluOpType.mult, op1=AluOpType.add).then_inc(vsem, 1)

            xnr = xn_dm.rearrange("p (it t) -> p it t", it=32)
            pdr = PD.rearrange("p (j t) -> p j t", j=8)
            for tt in range(8):
                ln_tile(tt, ld_x1, ve1, se1, pe1, x_io[tt % 2])
                for gq in range(4):
                    G = 4 * tt + gq
                    v.wait_ge(pe1, G + 1)
                    v.tensor_copy(xnr[:, 8 * gq:8 * gq + 8, 128 * tt:128 * (tt + 1)],
                                  pdr).then_inc(ve1b, 1)
            vcnt = 0
            for ot in range(4):
                v.wait_ge(ld_qw, 32 * (ot + 1))
                for pl in range(4):
                    last = None
                    for sh in range(8):
                        last = v.tensor_scalar(
                            out=bits[pl].rearrange("p (b sh) -> p sh b", sh=8)[:, sh, :],
                            in0=qw_u8[:, 512 * pl:512 * (pl + 1)],
                            scalar1=sh, scalar2=1,
                            op0=AluOpType.logical_shift_right,
                            op1=AluOpType.bitwise_and)
                    last.then_inc(ve_dq, 1); vcnt += 1
                v.wait_ge(ve_dq, vcnt)
                v.tensor_tensor(out=dlev,
                                in0=lut_t.rearrange("p (c two) -> p two c", two=2)[:, 1, :],
                                in1=lut_t.rearrange("p (c two) -> p two c", two=2)[:, 0, :],
                                op=AluOpType.subtract).then_inc(ve_dq, 1)
                vcnt += 1
                v.wait_ge(ve_dq, vcnt)
                for chk in range(NCHUNK):
                    C = NCHUNK * ot + chk
                    gidx = C // 2
                    sl = slice(CH * chk, CH * (chk + 1))
                    last = None
                    for c8 in range(8):
                        last = v.tensor_scalar(
                            out=tbuf[:, CH * c8:CH * (c8 + 1)], in0=bits[0][:, sl],
                            scalar1=dlev[:, c8:c8 + 1], scalar2=lut_t[:, 2 * c8:2 * c8 + 1],
                            op0=AluOpType.mult, op1=AluOpType.add)
                    last.then_inc(ve_dq, 1); vcnt += 1
                    v.wait_ge(ve_dq, vcnt)
                    for c4 in range(4):
                        last = v.select(ubuf[:, CH * c4:CH * (c4 + 1)], bits[1][:, sl],
                                        tbuf[:, CH * (2 * c4 + 1):CH * (2 * c4 + 2)],
                                        tbuf[:, CH * 2 * c4:CH * (2 * c4 + 1)], add_drain=True)
                    last.then_inc(ve_dq, 1); vcnt += 1
                    v.wait_ge(ve_dq, vcnt)
                    for c2 in range(2):
                        last = v.select(vbuf[:, CH * c2:CH * (c2 + 1)], bits[2][:, sl],
                                        ubuf[:, CH * (2 * c2 + 1):CH * (2 * c2 + 2)],
                                        ubuf[:, CH * 2 * c2:CH * (2 * c2 + 1)], add_drain=True)
                    last.then_inc(ve_dq, 1); vcnt += 1
                    v.wait_ge(ve_dq, vcnt)
                    if gidx >= 4:
                        v.wait_ge(pe_wt, gidx - 3)
                    v.select(wl_f2[gidx % 4][:, 512 * (C % 2):512 * (C % 2) + 512],
                             bits[3][:, sl], vbuf[:, CH:2 * CH], vbuf[:, 0:CH],
                             add_drain=True).then_inc(ve_dq, 1)
                    vcnt += 1
                    v.wait_ge(ve_dq, vcnt)
                    if C % 2 == 1 and gidx >= 2:
                        G = gidx - 2
                        v.wait_ge(pe_wt, G + 1)
                        if G >= 2:
                            v.wait_ge(exp_wl[G % 2], 16 * 8 * (G // 2))
                        v.tensor_copy(wlt_ev[G % 2], PD[:, 0:1024]).then_inc(ve_wt, 1)
                assert vcnt == (ot + 1) * VE_DQ_OT
            for G in (14, 15):
                v.wait_ge(pe_wt, G + 1)
                v.wait_ge(exp_wl[G % 2], 16 * 8 * (G // 2))
                v.tensor_copy(wlt_ev[G % 2], PD[:, 0:1024]).then_inc(ve_wt, 1)
            for H in range(NH):
                v.wait_ge(pe_avz, 16 * H + 16)
                v.reciprocal(rzrow[0:1, :], PD[0:1, 0:1024]).then_inc(ve_rz, 1)
                v.wait_ge(pe_rzb, H + 1)
                v.tensor_copy(rzb, PD[:, 0:1024]).then_inc(ve_fin, 1)
                v.wait_ge(ve_fin, 2 * H + 1)
                if H >= 2:
                    v.wait_ge(st_ao[H % 2], 16 * 8 * (H // 2))
                v.tensor_tensor(out=ao_sb[H % 2], in0=PC[:, 0:1024], in1=rzb,
                                op=AluOpType.mult).then_inc(ve_mid, 1)
                v.wait_ge(ve_mid, H + 1)
                v.tensor_scalar(out=ao_sb[H % 2], in0=ao_sb[H % 2],
                                scalar1=bv_sb[:, H:H + 1], scalar2=None,
                                op0=AluOpType.add).then_inc(ve_fin, 1)
            for c in range(4):
                v.wait_ge(pe_bb, c + 1)
                v.tensor_copy(biasB[:, 1024 * c:1024 * (c + 1)], PD[:, 0:1024]).then_inc(ve_bb, 1)
            for oc in range(4):
                for tt in range(8):
                    i = 8 * oc + tt
                    v.wait_ge(pe6, i + 1)
                    v.wait_ge(ld_x6[i % 2], 16 * (i // 2 + 1))
                    if i >= 2:
                        v.wait_ge(st_x2[i % 2], 16 * (i // 2))
                    ps = PA if tt % 2 == 0 else PB
                    v.tensor_tensor(out=x2o[tt % 2], in0=ps[:, 0:1024],
                                    in1=biasB[:, 1024 * oc:1024 * (oc + 1)],
                                    op=AluOpType.add).then_inc(ve_mid, 1)
                    v.wait_ge(ve_mid, 32 + i + 1)
                    v.tensor_tensor(out=x2o[tt % 2], in0=x2o[tt % 2], in1=xch[tt % 2],
                                    op=AluOpType.add).then_inc(ve6, 1)
            for tt in range(8):
                ln_tile(tt, ld_x27, ve7, se7, pe7, x_io[tt % 2])
                for gq in range(4):
                    G = 4 * tt + gq
                    v.wait_ge(pe7, G + 1)
                    v.tensor_copy(xnr[:, 8 * gq:8 * gq + 8, 128 * tt:128 * (tt + 1)],
                                  pdr).then_inc(ve7b, 1)
            for c in range(4):
                v.wait_ge(pe_bb, 4 + c + 1)
                v.tensor_copy(biasB[:, 1024 * c:1024 * (c + 1)], PD[:, 0:1024]).then_inc(ve_bb, 1)
            for rp in range(4):
                for tt in range(8):
                    i = 8 * rp + tt
                    v.wait_ge(pe8, i + 1)
                    v.wait_ge(ld_x28[i % 2], 16 * (i // 2 + 1))
                    if i >= 2:
                        v.wait_ge(st_out[i % 2], 16 * (i // 2))
                    ps = PA if tt % 2 == 0 else PB
                    v.tensor_tensor(out=outb[tt % 2], in0=ps[:, 0:1024],
                                    in1=biasB[:, 1024 * rp:1024 * (rp + 1)],
                                    op=AluOpType.add).then_inc(ve_mid, 1)
                    v.wait_ge(ve_mid, 64 + i + 1)
                    v.tensor_tensor(out=outb[tt % 2], in0=outb[tt % 2], in1=x2ch[tt % 2],
                                    op=AluOpType.add).then_inc(ve8, 1)

        @block.scalar
        def _(sc):
            sc.wait_ge(ve0, 1)
            for tt in range(8):
                sc.wait_ge(ve1, VE1_TT * tt + 2)
                sc.activation(sq, mv[:, 1:2], AF.Sqrt, bias=eps_t, scale=1.0).then_inc(se1, 1)
            for mt in range(32):
                sc.wait_ge(pe2a, mt + 1)
                if mt >= 2:
                    sc.wait_ge(pe_vt, mt - 1)
                ps = (PA, PB, PC)[mt % 3]
                sc.activation(xvf[mt % 2], ps[:, 0:1024], AF.Identity).then_inc(se2a, 1)
                sc.wait_ge(pe_vt, mt + 1)
                if mt >= 2:
                    sc.wait_ge(exp_v[mt % 2], 16 * 8 * (mt // 2))
                sc.activation(vexp[mt % 2], PD[:, 0:1024], AF.Identity).then_inc(se_vx, 1)
            for mt in range(32):
                sc.wait_ge(pe2b, mt + 1)
                if mt >= 2:
                    pm = mt - 2
                    sc.wait_ge(exp_k[2 * (pm // 16) + pm % 2], 16 * ((pm % 16) // 2 + 1))
                ps = (PA, PB, PC)[mt % 3]
                sc.activation(kexp[mt % 2], ps[:, 0:1024], AF.Identity,
                              bias=bk_sb[:, mt:mt + 1], scale=1.0).then_inc(se2b, 1)
            for hg in range(NHG):
                for mtl in range(4):
                    mt = 4 * hg + mtl
                    sc.wait_ge(pe2b, 32 + mt + 1)
                    if hg >= 2 and mtl == 0:
                        sc.wait_ge(pe_sc, 16 * 4 * (hg - 1))
                    ps = PA if mt % 2 == 0 else PB
                    sc.activation(Q_hg[hg % 2][:, 1024 * mtl:1024 * (mtl + 1)], ps[:, 0:1024],
                                  AF.Identity, bias=bqs[:, mt:mt + 1], scale=SCALE).then_inc(se5q, 1)
                for hl in range(4):
                    H = 4 * hg + hl
                    if H >= 1:
                        Hp = H - 1
                        sc.wait_ge(ve_fin, 2 * Hp + 2)
                        for tt in range(8):
                            sc.dma_start(out=ao_scr[tt, :, Hp, :],
                                         in_=ao_sb[Hp % 2][:, 128 * tt:128 * (tt + 1)]).then_inc(st_ao[Hp % 2], 16)
                    for kt in range(16):
                        sc.wait_ge(pe_sc, 16 * H + kt + 1)
                        if H >= 1:
                            sc.wait_ge(pe_avz, 16 * (H - 1) + kt + 1)
                        slot = PA if kt % 2 == 0 else PB
                        q0 = q0_of(kt)
                        sc.activation(attT[:, ATO[kt]:ATO[kt] + ATW[kt]],
                                      slot[:, q0:1024], AF.Exp).then_inc(se_exp, 1)
            sc.wait_ge(ve_fin, 2 * NH)
            for tt in range(8):
                sc.dma_start(out=ao_scr[tt, :, NH - 1, :],
                             in_=ao_sb[(NH - 1) % 2][:, 128 * tt:128 * (tt + 1)]).then_inc(st_ao[(NH - 1) % 2], 16)
            for oc in range(4):
                for tt in range(8):
                    i = 8 * oc + tt
                    sc.wait_ge(ve6, i + 1)
                    sc.dma_start(out=x2_scr[128 * tt:128 * (tt + 1), 1024 * oc:1024 * (oc + 1)],
                                 in_=x2o[tt % 2]).then_inc(st_x2[i % 2], 16)
            for tt in range(8):
                sc.wait_ge(ve7, VE1_TT * tt + 2)
                sc.activation(sq, mv[:, 1:2], AF.Sqrt, bias=eps_t, scale=1.0).then_inc(se7, 1)
            for rp in range(4):
                for tt in range(8):
                    i = 8 * rp + tt
                    sc.wait_ge(ve8, i + 1)
                    sc.dma_start(out=out_own[128 * tt:128 * (tt + 1), 1024 * rp:1024 * (rp + 1)],
                                 in_=outb[tt % 2]).then_inc(st_out[i % 2], 16)
            sc.wait_ge(st_out[0], 16 * 16)
            sc.wait_ge(st_out[1], 16 * 16)

        @block.tensor
        def _(te):
            te.wait_ge(ve0, 1)
            for tt in range(8):
                te.wait_ge(ve1, VE1_TT * (tt + 1))
                for gq in range(4):
                    G = 4 * tt + gq
                    if G > 0:
                        te.wait_ge(ve1b, G)
                    last = None
                    for j in range(8):
                        d = 8 * gq + j
                        last = te.transpose(PD[:, 128 * j:128 * (j + 1)],
                                            xnf[:, 128 * d:128 * (d + 1)], ident)
                    last.then_inc(pe1, 1)
            prev_pd = None
            for mt in range(33):
                if mt < 32:
                    te.wait_ge(ld_wv[mt % 2], 16 * (mt // 2 + 1))
                    if mt == 0:
                        te.wait_ge(ve1b, 32)
                    if mt >= 3:
                        te.wait_ge(se2a, mt - 2)
                    ps = (PA, PB, PC)[mt % 3]
                    last = None
                    for it in range(32):
                        for (c0, c1) in ((0, 512), (512, 1024)):
                            last = te.matmul(ps[:, c0:c1], wcol[mt % 2][:, 128 * it:128 * (it + 1)],
                                             xn_dm[:, 1024 * it + c0:1024 * it + c1],
                                             start=(it == 0), stop=(it == 31))
                    last.then_inc(pe2a, 1)
                if mt >= 1:
                    mp = mt - 1
                    te.wait_ge(se2a, mp + 1)
                    if prev_pd is not None:
                        te.wait_ge(*prev_pd)
                    last = None
                    for j in range(8):
                        last = te.transpose(PD[:, 128 * j:128 * (j + 1)],
                                            xvf[mp % 2][:, 128 * j:128 * (j + 1)], ident)
                    last.then_inc(pe_vt, 1)
                    prev_pd = (se_vx, mp + 1)
                    if mp % 2 == 1 and mp // 2 < 16:
                        G = mp // 2
                        te.wait_ge(ve_dq, dq_chunk_done(G // 4, 2 * (G % 4) + 1))
                        te.wait_ge(*prev_pd)
                        last = None
                        for j in range(8):
                            last = te.transpose(PD[:, 128 * j:128 * (j + 1)],
                                                wl_f2[G % 4][:, 128 * j:128 * (j + 1)], ident)
                        last.then_inc(pe_wt, 1)
                        prev_pd = (ve_wt, G + 1)
            for mt in range(32):
                te.wait_ge(ld_wk[mt % 2], 16 * (mt // 2 + 1))
                if mt < 3:
                    te.wait_ge(se2a, (31, 32, 30)[mt])
                if mt >= 3:
                    te.wait_ge(se2b, mt - 2)
                ps = (PA, PB, PC)[mt % 3]
                last = None
                for it in range(32):
                    for (c0, c1) in ((0, 512), (512, 1024)):
                        last = te.matmul(ps[:, c0:c1], wcol[mt % 2][:, 128 * it:128 * (it + 1)],
                                         xn_dm[:, 1024 * it + c0:1024 * it + c1],
                                         start=(it == 0), stop=(it == 31))
                last.then_inc(pe2b, 1)
            for hg in range(NHG):
                for mtl in range(4):
                    mt = 4 * hg + mtl
                    te.wait_ge(ld_wq[mt % 2], 16 * (mt // 2 + 1))
                    if hg == 0 and mtl < 2:
                        te.wait_ge(se2b, 31 + mtl)
                        te.wait_ge(ve_wt, 16)
                    if hg >= 1 and mtl < 2:
                        te.wait_ge(se_exp, 16 * 4 * hg)
                    if mtl >= 2:
                        te.wait_ge(se5q, 4 * hg + mtl - 1)
                    ps = PA if mt % 2 == 0 else PB
                    last = None
                    for it in range(32):
                        for (c0, c1) in ((0, 512), (512, 1024)):
                            last = te.matmul(ps[:, c0:c1], wcol[mt % 2][:, 128 * it:128 * (it + 1)],
                                             xn_dm[:, 1024 * it + c0:1024 * it + c1],
                                             start=(it == 0), stop=(it == 31))
                    last.then_inc(pe2b, 1)
                for hl in range(4):
                    H = 4 * hg + hl
                    te.wait_ge(ld_kv[H % 2], 16 * 8 * (H // 2 + 1))

                    def avz_pair(s_):
                        for kt in (2 * s_, 2 * s_ + 1):
                            te.wait_ge(se_exp, 16 * H + 2 * s_ + 2)
                            lastm = None
                            for (c0, c1) in q_ranges(kt):
                                stop = (kt == 7) if c1 <= 512 else (kt == 15)
                                lastm = te.matmul(
                                    PC[:, c0:c1], v_sb[H % 2][:, 128 * kt:128 * (kt + 1)],
                                    attT[:, ATO[kt] + c0 - q0_of(kt):ATO[kt] + c1 - q0_of(kt)],
                                    start=(kt == 0), stop=stop)
                            for (c0, c1) in q_ranges(kt):
                                stop = (kt == 7) if c1 <= 512 else (kt == 15)
                                lastm = te.matmul(
                                    PD[0:1, c0:c1], onesb[:, 0:1],
                                    attT[:, ATO[kt] + c0 - q0_of(kt):ATO[kt] + c1 - q0_of(kt)],
                                    start=(kt == 0), stop=stop)
                            lastm.then_inc(pe_avz, 1)

                    for s_ in range(8):
                        for kt in (2 * s_, 2 * s_ + 1):
                            tgt = 16 * H + kt - 1
                            if tgt > 0:
                                te.wait_ge(se_exp, tgt)
                            if kt == 0:
                                te.wait_ge(se5q, 4 * (hg + 1))
                                if H >= 1:
                                    te.wait_ge(ve_fin, 2 * (H - 1) + 2)
                                else:
                                    te.wait_ge(ve_wt, 16)
                            slot = PA if kt % 2 == 0 else PB
                            s_of = kt // 2
                            j_of = kt % 2
                            last = None
                            for (c0, c1) in q_ranges(kt):
                                has_mask = (c0 == q0_of(kt))
                                last = te.matmul(slot[:, c0:c1],
                                                 k_sb[H % 2][:, 128 * kt:128 * (kt + 1)],
                                                 Q_hg[hg % 2][:, 1024 * hl + c0:1024 * hl + c1],
                                                 start=True, stop=not has_mask)
                                if has_mask:
                                    last = te.matmul(
                                        slot[:, c0:c0 + 128], identb,
                                        masks[:, 256 * s_of + 128 * j_of:256 * s_of + 128 * (j_of + 1)],
                                        start=False, stop=True)
                            last.then_inc(pe_sc, 1)
                        if s_ >= 1:
                            avz_pair(s_ - 1)
                    avz_pair(7)
                    te.wait_ge(ve_rz, H + 1)
                    last = None
                    for h2 in range(2):
                        last = te.matmul(PD[:, 512 * h2:512 * (h2 + 1)], onesr[0:1, :],
                                         rzrow[0:1, 512 * h2:512 * (h2 + 1)], start=True, stop=True)
                    last.then_inc(pe_rzb, 1)
            for c in range(4):
                if c == 0:
                    te.wait_ge(ve_fin, 2 * NH)
                te.wait_ge(ld_misc, 64 + 16 * (c + 1))
                if c > 0:
                    te.wait_ge(ve_bb, c)
                last = None
                for h2 in range(2):
                    last = te.matmul(PD[:, 512 * h2:512 * (h2 + 1)], onesr[0:1, :],
                                     rzrow[0:1, 512 * h2:512 * (h2 + 1)], start=True, stop=True)
                last.then_inc(pe_bb, 1)
            for oc in range(4):
                te.wait_ge(ld_wo, 16 * 32 * (oc + 1))
                if oc == 0:
                    te.wait_ge(ve_bb, 4)
                for tt in range(8):
                    i = 8 * oc + tt
                    te.wait_ge(ld_ao[i % 2], 16 * (i // 2 + 1))
                    if i < 2:
                        te.wait_ge(se_exp, 16 * NH)
                    if i >= 2:
                        te.wait_ge(ve6, i - 1)
                    ps = PA if tt % 2 == 0 else PB
                    last = None
                    for it in range(32):
                        for (c0, c1) in ((0, 512), (512, 1024)):
                            last = te.matmul(ps[:, c0:c1], aost[tt % 2][:, 128 * it:128 * (it + 1)],
                                             wbig[:, 1024 * it + c0:1024 * it + c1],
                                             start=(it == 0), stop=(it == 31))
                    last.then_inc(pe6, 1)
            for tt in range(8):
                te.wait_ge(ve7, VE1_TT * (tt + 1))
                for gq in range(4):
                    G = 4 * tt + gq
                    if G > 0:
                        te.wait_ge(ve7b, G)
                    else:
                        te.wait_ge(ve_bb, 4)
                    last = None
                    for j in range(8):
                        d = 8 * gq + j
                        last = te.transpose(PD[:, 128 * j:128 * (j + 1)],
                                            xnf[:, 128 * d:128 * (d + 1)], ident)
                    last.then_inc(pe7, 1)
            for c in range(4):
                if c == 0:
                    te.wait_ge(ve7b, 32)
                te.wait_ge(ld_misc, 128 + 16 * (c + 1))
                if c > 0:
                    te.wait_ge(ve_bb, 4 + c)
                last = None
                for h2 in range(2):
                    last = te.matmul(PD[:, 512 * h2:512 * (h2 + 1)], onesr[0:1, :],
                                     rzrow[0:1, 512 * h2:512 * (h2 + 1)], start=True, stop=True)
                last.then_inc(pe_bb, 1)
            wbr2t = wbig.rearrange("p (it rb f) -> p it rb f", it=32, rb=2)
            for rp in range(4):
                te.wait_ge(ld_wl, 16 * 64 * (rp + 1))
                if rp == 0:
                    te.wait_ge(ve_bb, 8)
                    te.wait_ge(ve6, 32)
                for tt in range(8):
                    i = 8 * rp + tt
                    ps = PA if tt % 2 == 0 else PB
                    if i >= 2:
                        te.wait_ge(ve8, i - 1)
                    last = None
                    for it in range(32):
                        for rb in range(2):
                            last = te.matmul(
                                ps[:, 512 * rb:512 * (rb + 1)],
                                xn_dm[:, 1024 * it + 128 * tt:1024 * it + 128 * (tt + 1)],
                                wbr2t[:, it, rb, :],
                                start=(it == 0), stop=(it == 31))
                    last.then_inc(pe8, 1)
    return nc


def kernel(L0_attn_x, L0_attn_Wq, L0_attn_bq, L0_attn_Wk, L0_attn_bk,
           L0_attn_Wv, L0_attn_bv, L0_attn_Wo, L0_attn_bo,
           L0_ffn_qweight, L0_ffn_lut4, L0_ffn_lut8, L0_ffn_bias):
    x = np.asarray(L0_attn_x, dtype=np.float32)
    Wq = np.asarray(L0_attn_Wq, dtype=np.float32)
    Wk = np.asarray(L0_attn_Wk, dtype=np.float32)
    Wv = np.asarray(L0_attn_Wv, dtype=np.float32)
    Wo = np.asarray(L0_attn_Wo, dtype=np.float32)
    bq = np.asarray(L0_attn_bq, dtype=np.float32)
    bk = np.asarray(L0_attn_bk, dtype=np.float32)
    bv = np.asarray(L0_attn_bv, dtype=np.float32)
    bo = np.asarray(L0_attn_bo, dtype=np.float32)
    qw = np.asarray(L0_ffn_qweight)
    lut4 = np.asarray(L0_ffn_lut4, dtype=np.float32)
    bfb = np.asarray(L0_ffn_bias, dtype=np.float32)

    wq4 = np.ascontiguousarray(Wq.reshape(32, 128, 32, 128).transpose(0, 3, 2, 1))
    wk4 = np.ascontiguousarray(Wk.reshape(32, 128, 32, 128).transpose(0, 3, 2, 1))
    wv4 = np.ascontiguousarray(Wv.reshape(32, 128, 32, 128).transpose(0, 3, 2, 1))
    wot = np.ascontiguousarray(Wo.T)
    bq_t = np.ascontiguousarray(bq.reshape(32, 128).T)
    bk_t = np.ascontiguousarray(bk.reshape(32, 128).T)
    bv_t = np.ascontiguousarray(bv.reshape(32, 128).T)
    bo_row = np.ascontiguousarray(bo.reshape(1, 4096))
    bf_row = np.ascontiguousarray(bfb.reshape(1, 4096))

    def build_mask(parity):
        QT = QT_EVEN if parity == 0 else QT_ODD
        m = np.zeros((128, 2048), np.float32)
        for s in range(8):
            qt = QT[s]
            for j in range(2):
                kt = 2 * s + j
                if kt < qt:
                    t = np.zeros((128, 128), np.float32)
                elif kt == qt:
                    t = np.where(np.arange(128)[:, None] <= np.arange(128)[None, :],
                                 0.0, -30000.0).astype(np.float32)
                else:
                    t = np.full((128, 128), -30000.0, np.float32)
                m[:, 256 * s + 128 * j:256 * s + 128 * (j + 1)] = t
        return m.astype(BF)

    mask_by_par = [build_mask(0), build_mask(1)]

    nc = build_nc()
    in_maps = []
    for core in range(8):
        b = core // 2
        parity = core % 2
        QT = QT_EVEN if parity == 0 else QT_ODD
        rows = np.concatenate([128 * qt + np.arange(128) for qt in QT])
        in_maps.append({
            "x_own": np.ascontiguousarray(x[b][rows]),
            "wq4": wq4, "wk4": wk4, "wv4": wv4, "wot": wot,
            "bq_t": bq_t, "bk_t": bk_t, "bv_t": bv_t,
            "bo_row": bo_row, "bf_row": bf_row,
            "qw_own": np.ascontiguousarray(qw[0:4, 512 * core:512 * (core + 1), :]),
            "lut_own": np.ascontiguousarray(lut4[512 * core:512 * (core + 1), :]),
            "masks_in": mask_by_par[parity],
        })
    res = run_bass_kernel_spmd(nc, in_maps, core_ids=list(range(8)))
    out = np.empty((4, 2048, 4096), dtype=np.float32)
    for core in range(8):
        b = core // 2
        parity = core % 2
        QT = QT_EVEN if parity == 0 else QT_ODD
        rows = np.concatenate([128 * qt + np.arange(128) for qt in QT])
        out[b][rows] = res.results[core]["out_own"]
    return out


# version 11
# speedup vs baseline: 1.0032x; 1.0032x over previous
import contextlib
import math
import numpy as np
import ml_dtypes
import concourse.bass as bass
import concourse.mybir as mybir
from concourse.bass_utils import run_bass_kernel_spmd
from concourse.alu_op_type import AluOpType

F32 = mybir.dt.float32
BF16 = mybir.dt.bfloat16
I32 = mybir.dt.int32
U8 = mybir.dt.uint8
AF = mybir.ActivationFunctionType
BF = ml_dtypes.bfloat16

QT_EVEN = [0, 3, 4, 7, 8, 11, 12, 15]
QT_ODD = [1, 2, 5, 6, 9, 10, 13, 14]
SCALE = 1.0 / math.sqrt(128.0)
EPS = 1e-5
NH = 32
NHG = 8
CH = 512
NCHUNK = 4096 // CH
VE_DQ_OT = 4 + 1 + NCHUNK * 4
VE1_TT = 5

def q0_of(kt):
    return 128 * (kt // 2)
ATW = [1024 - q0_of(kt) for kt in range(16)]
ATO = [0] * 16
for _kt in range(1, 16):
    ATO[_kt] = ATO[_kt - 1] + ATW[_kt - 1]
ATT_TOT = ATO[15] + ATW[15]

def q_ranges(kt):
    q0 = q0_of(kt)
    return [(q0, 512), (512, 1024)] if q0 < 512 else [(q0, 1024)]


def build_nc():
    nc = bass.Bass("TRN2", target_bir_lowering=False)
    x_own = nc.dram_tensor("x_own", [1024, 4096], F32, kind="ExternalInput").ap()
    wq4 = nc.dram_tensor("wq4", [32, 128, 32, 128], F32, kind="ExternalInput").ap()
    wk4 = nc.dram_tensor("wk4", [32, 128, 32, 128], F32, kind="ExternalInput").ap()
    wv4 = nc.dram_tensor("wv4", [32, 128, 32, 128], F32, kind="ExternalInput").ap()
    wot = nc.dram_tensor("wot", [4096, 4096], F32, kind="ExternalInput").ap()
    bq_t = nc.dram_tensor("bq_t", [128, 32], F32, kind="ExternalInput").ap()
    bk_t = nc.dram_tensor("bk_t", [128, 32], F32, kind="ExternalInput").ap()
    bv_t = nc.dram_tensor("bv_t", [128, 32], F32, kind="ExternalInput").ap()
    bo_row = nc.dram_tensor("bo_row", [1, 4096], F32, kind="ExternalInput").ap()
    bf_row = nc.dram_tensor("bf_row", [1, 4096], F32, kind="ExternalInput").ap()
    qw_own = nc.dram_tensor("qw_own", [4, 512, 128], I32, kind="ExternalInput").ap()
    lut_own = nc.dram_tensor("lut_own", [512, 16], F32, kind="ExternalInput").ap()
    masks_in = nc.dram_tensor("masks_in", [128, 2048], BF16, kind="ExternalInput").ap()
    out_own = nc.dram_tensor("out_own", [1024, 4096], F32, kind="ExternalOutput").ap()
    kag_inA = nc.dram_tensor("kag_inA", [2048, 1024], BF16).ap()
    kag_inB = nc.dram_tensor("kag_inB", [2048, 1024], BF16).ap()
    kag_outA = nc.dram_tensor("kag_outA", [4096, 1024], BF16).ap()
    kag_outB = nc.dram_tensor("kag_outB", [4096, 1024], BF16).ap()
    vag_in = nc.dram_tensor("vag_in", [1024, 4096], BF16).ap()
    vag_out = nc.dram_tensor("vag_out", [2048, 4096], BF16).ap()
    wlag_in = nc.dram_tensor("wlag_in", [4096, 512], BF16).ap()
    wlag_out = nc.dram_tensor("wlag_out", [32768, 512], BF16).ap()
    ao_scr = nc.dram_tensor("ao_scr", [8, 128, 32, 128], BF16).ap()
    x2_scr = nc.dram_tensor("x2_scr", [1024, 4096], F32).ap()

    es = contextlib.ExitStack()
    with es:
        AR = es.enter_context(nc.sbuf_tensor("AR", [128, 204 * 1024], U8))
        arf, arb, ari, aru = AR.bitcast(F32), AR.bitcast(BF16), AR.bitcast(I32), AR.bitcast(U8)

        def fsl(o, n):
            assert o % 64 == 0
            return arf.ap()[:, o // 4:o // 4 + n]

        def bsl(o, n):
            assert o % 64 == 0
            return arb.ap()[:, o // 2:o // 2 + n]

        def isl(o, n):
            assert o % 64 == 0
            return ari.ap()[:, o // 4:o // 4 + n]

        def usl(o, n):
            assert o % 64 == 0
            return aru.ap()[:, o:o + n]

        masks = bsl(0, 2048)
        biasB = bsl(4096, 4096)
        rzb = fsl(12288, 1024)
        rzrow = fsl(16384, 1024)
        ident = fsl(20480, 128)
        identb = bsl(23040, 128)
        iota_i = isl(20992, 128)
        onesr = fsl(21504, 128)
        onesb = bsl(22016, 32)
        bqs = fsl(22080, 32)
        bk_sb = fsl(22208, 32)
        bv_sb = fsl(22336, 32)
        bn6 = fsl(22464, 48)
        mv = fsl(22656, 2)
        sq = fsl(22720, 1)
        rstd = fsl(22784, 1)
        nmr = fsl(22848, 1)
        eps_t = fsl(22912, 1)
        S1 = 24 * 1024
        xn_dm = bsl(S1, 32768)
        S2 = 88 * 1024
        bits = [usl(S2 + 4096 * i, 4096) for i in range(4)]
        qw_i32 = isl(S2 + 16384, 512)
        qw_u8 = usl(S2 + 16384, 2048)
        lut_t = fsl(S2 + 18432, 16)
        dlev = fsl(S2 + 18496, 8)
        tbuf = fsl(S2 + 18560, 8 * CH)
        ubuf = fsl(S2 + 34944, 4 * CH)
        vbuf = fsl(S2 + 43136, 2 * CH)
        wl_f2 = [fsl(S2 + 47232 + 4096 * i, 1024) for i in range(4)]
        wlt_ev = [bsl(4096 + 2048 * i, 1024) for i in range(2)]
        attT = bsl(S2, ATT_TOT)
        Q_hg = [bsl(S2 + 18432 + 8192 * i, 4096) for i in range(2)]
        k_sb = [bsl(S2 + 34816 + 4096 * i, 2048) for i in range(2)]
        v_sb = [bsl(S2 + 43008 + 4096 * i, 2048) for i in range(2)]
        ao_sb = [bsl(S2 + 51200 + 2048 * i, 1024) for i in range(2)]
        wbig = bsl(S2, 32768)
        R3 = 152 * 1024
        vexp = [bsl(R3 + 24576 + 2048 * i, 1024) for i in range(2)]
        xvf = [fsl(R3 + 16384 + 4096 * i, 1024) for i in range(2)]
        x_io = [fsl(R3 + 16384 * i, 4096) for i in range(2)]
        xnf = fsl(R3 + 32768, 4096)
        wcol = [bsl(R3 + 8192 * i, 4096) for i in range(2)]
        kexp = [bsl(R3 + 16384 + 2048 * i, 1024) for i in range(2)]
        aost = [bsl(R3 + 8192 * i, 4096) for i in range(2)]
        xch = [fsl(R3 + 16384 + 4096 * i, 1024) for i in range(2)]
        x2o = [fsl(R3 + 24576 + 4096 * i, 1024) for i in range(2)]
        x2ch = [fsl(R3 + 4096 * i, 1024) for i in range(2)]
        outb = [fsl(R3 + 8192 + 4096 * i, 1024) for i in range(2)]

        PA = es.enter_context(nc.psum_tensor("PA", [128, 1024], F32))
        PB = es.enter_context(nc.psum_tensor("PB", [128, 1024], F32))
        PC = es.enter_context(nc.psum_tensor("PC", [128, 1024], F32))
        PD = es.enter_context(nc.psum_tensor("PD", [128, 1024], F32))

        def sem(name):
            return es.enter_context(nc.semaphore(name))
        ld_misc = sem("ld_misc"); ld_qw = sem("ld_qw")
        ld_x1 = [sem("ld_x1a"), sem("ld_x1b")]
        ld_wv = [sem("ld_wva"), sem("ld_wvb")]
        ld_wk = [sem("ld_wka"), sem("ld_wkb")]
        ld_wq = [sem("ld_wqa"), sem("ld_wqb")]
        ld_wo = sem("ld_wo"); ld_wl = sem("ld_wl")
        ld_kv = [sem("ld_kva"), sem("ld_kvb")]
        ld_ao = [sem("ld_aoa"), sem("ld_aob")]
        ld_x6 = [sem("ld_x6a"), sem("ld_x6b")]
        ld_x27 = [sem("ld_x27a"), sem("ld_x27b")]
        ld_x28 = [sem("ld_x28a"), sem("ld_x28b")]
        exp_v = [sem("exp_va"), sem("exp_vb")]
        exp_k = [sem("exp_ka"), sem("exp_kb"), sem("exp_kc"), sem("exp_kd")]
        exp_wl = [sem("exp_wla"), sem("exp_wlb")]
        cc_k1 = sem("cc_k1"); cc_k2 = sem("cc_k2"); cc_v = sem("cc_v"); cc_wl = sem("cc_wl")
        st_ao = [sem("st_aoa"), sem("st_aob")]
        st_x2 = [sem("st_x2a"), sem("st_x2b")]
        st_out = [sem("st_outa"), sem("st_outb")]
        ge = sem("ge"); ve0 = sem("ve0")
        ve1 = sem("ve1"); se1 = sem("se1"); pe1 = sem("pe1"); ve1b = sem("ve1b")
        pe_bb = sem("pe_bb"); ve_bb = sem("ve_bb")
        pe2a = sem("pe2a"); se2a = sem("se2a"); pe_vt = sem("pe_vt"); se_vx = sem("se_vx")
        pe2b = sem("pe2b"); se2b = sem("se2b")
        ve_dq = sem("ve_dq"); pe_wt = sem("pe_wt"); ve_wt = sem("ve_wt")
        pe_sc = sem("pe_sc"); pe_avz = sem("pe_avz"); pe_rzb = sem("pe_rzb")
        se_exp = sem("se_exp"); se5q = sem("se5q")
        ve_mask = sem("ve_mask"); ve_rz = sem("ve_rz"); ve_fin = sem("ve_fin")
        pe6 = sem("pe6"); ve6 = sem("ve6")
        ve7 = sem("ve7"); se7 = sem("se7"); pe7 = sem("pe7"); ve7b = sem("ve7b")
        pe8 = sem("pe8"); ve8 = sem("ve8"); ve_mid = sem("ve_mid")

        block = es.enter_context(nc.Block())

        def dq_chunk_done(ot, c_local):
            return ot * VE_DQ_OT + 5 + 4 * (c_local + 1)

        @block.sync
        def _(s):
            s.dma_start(out=masks, in_=masks_in).then_inc(ld_misc, 16)
            s.dma_start(out=bqs, in_=bq_t).then_inc(ld_misc, 16)
            s.dma_start(out=bk_sb, in_=bk_t).then_inc(ld_misc, 16)
            s.dma_start(out=bv_sb, in_=bv_t).then_inc(ld_misc, 16)
            for tt in range(8):
                if tt >= 2:
                    s.wait_ge(ve1, VE1_TT * (tt - 1))
                s.dma_start(out=x_io[tt % 2],
                            in_=x_own[128 * tt:128 * (tt + 1), :]).then_inc(ld_x1[tt % 2], 16)
            for mt in range(32):
                s.wait_ge(se_vx, mt + 1)
                for tt in range(8):
                    s.dma_start(out=vag_in[128 * tt:128 * (tt + 1), 128 * mt:128 * (mt + 1)],
                                in_=vexp[mt % 2][:, 128 * tt:128 * (tt + 1)]).then_inc(exp_v[mt % 2], 16)
                if mt % 2 == 1:
                    G = mt // 2
                    s.wait_ge(ve_wt, G + 1)
                    ot, gq = G // 4, G % 4
                    for j in range(8):
                        it = 8 * gq + j
                        s.dma_start(out=wlag_in[128 * it:128 * (it + 1), 128 * ot:128 * (ot + 1)],
                                    in_=wlt_ev[G % 2][:, 128 * j:128 * (j + 1)]).then_inc(exp_wl[G % 2], 16)
            for mt in range(32):
                s.wait_ge(se2b, mt + 1)
                half = mt // 16
                mloc = mt % 16
                kag = kag_inA if half == 0 else kag_inB
                s.dma_start(out=kag[128 * mloc:128 * (mloc + 1), :],
                            in_=kexp[mt % 2]).then_inc(exp_k[2 * half + mt % 2], 16)
            kr2A = kag_outA.rearrange("(r f) (j w t) -> r f j w t", r=2, j=4, w=2, t=128)
            kr2B = kag_outB.rearrange("(r f) (j w t) -> r f j w t", r=2, j=4, w=2, t=128)
            vr = vag_out.rearrange("(r j w p) f -> r p j w f", r=2, j=4, w=2, p=128)
            for H in range(NH):
                par = H % 2
                if H == 0:
                    s.wait_ge(cc_k1, 1)
                    s.wait_ge(cc_v, 1)
                if H == 16:
                    s.wait_ge(cc_k2, 1)
                if H < 2:
                    s.wait_ge(ve_dq, 4 * VE_DQ_OT)
                    s.wait_ge(ve_wt, 16)
                if H >= 2:
                    s.wait_ge(pe_sc, 16 * (H - 1))
                    s.wait_ge(pe_avz, 16 * (H - 1))
                for rp in range(2):
                    for w in range(2):
                        goff = (384 * w) if rp == 0 else (128 + 128 * w)
                        dk = k_sb[par].rearrange("p (j rest) -> p j rest", j=4)[:, :, goff:goff + 128]
                        kr2h = kr2A if H < 16 else kr2B
                        Hl = H % 16
                        s.dma_start(out=dk, in_=kr2h[rp, 128 * Hl:128 * (Hl + 1), :, w, :]).then_inc(ld_kv[par], 16)
                        dv = v_sb[par].rearrange("p (j rest) -> p j rest", j=4)[:, :, goff:goff + 128]
                        s.dma_start(out=dv, in_=vr[rp, :, :, w, 128 * H:128 * (H + 1)]).then_inc(ld_kv[par], 16)
            for c in range(4):
                if c == 0:
                    s.wait_ge(ve_fin, 2 * NH)
                if c > 0:
                    s.wait_ge(pe_bb, c)
                s.dma_start(out=rzrow[0:1, :],
                            in_=bo_row[0:1, 1024 * c:1024 * (c + 1)]).then_inc(ld_misc, 16)
            for oc in range(4):
                for tt in range(8):
                    i = 8 * oc + tt
                    if i == 0:
                        s.wait_ge(st_ao[0], 16 * 8 * NH // 2)
                        s.wait_ge(st_ao[1], 16 * 8 * NH // 2)
                        s.wait_ge(se5q, 4 * NHG)
                    if i >= 2:
                        s.wait_ge(pe6, i - 1)
                    s.dma_start(out=aost[tt % 2],
                                in_=ao_scr[tt].rearrange("p it t -> p (it t)")).then_inc(ld_ao[i % 2], 16)
                    if i >= 2:
                        s.wait_ge(ve6, i - 1)
                    s.dma_start(out=xch[tt % 2],
                                in_=x_own[128 * tt:128 * (tt + 1), 1024 * oc:1024 * (oc + 1)]).then_inc(ld_x6[i % 2], 16)
            for tt in range(8):
                s.wait_ge(st_x2[0], 16 * 16)
                s.wait_ge(st_x2[1], 16 * 16)
                if tt < 2:
                    s.wait_ge(pe6, 32)
                    s.wait_ge(ve6, 32)
                if tt >= 2:
                    s.wait_ge(ve7, VE1_TT * (tt - 1))
                s.dma_start(out=x_io[tt % 2],
                            in_=x2_scr[128 * tt:128 * (tt + 1), :]).then_inc(ld_x27[tt % 2], 16)
            for c in range(4):
                if c == 0:
                    s.wait_ge(ve7b, 32)
                s.wait_ge(pe_bb, 4 + c)
                s.dma_start(out=rzrow[0:1, :],
                            in_=bf_row[0:1, 1024 * c:1024 * (c + 1)]).then_inc(ld_misc, 16)
            for rp in range(4):
                for tt in range(8):
                    i = 8 * rp + tt
                    if i < 2:
                        s.wait_ge(ve7, VE1_TT * 8)
                    if i >= 2:
                        s.wait_ge(ve8, i - 1)
                    s.dma_start(out=x2ch[tt % 2],
                                in_=x2_scr[128 * tt:128 * (tt + 1), 1024 * rp:1024 * (rp + 1)]).then_inc(ld_x28[i % 2], 16)
            s.wait_ge(st_out[0], 16 * 16)
            s.wait_ge(st_out[1], 16 * 16)

        @block.gpsimd
        def _(g):
            g.iota(iota_i, pattern=[[1, 128]], base=0, channel_multiplier=-1).then_inc(ge, 1)
            qw_r = qw_own.rearrange("pl (t r) w -> t r pl w", t=4, r=128)
            g.dma_start(out=qw_i32, in_=qw_r[0]).then_inc(ld_qw, 16)
            g.dma_start(out=lut_t, in_=lut_own[0:128, :]).then_inc(ld_qw, 16)
            for mt in range(32):
                if mt < 2:
                    g.wait_ge(ve1, VE1_TT * 8)
                    g.wait_ge(pe1, 32)
                if mt >= 2:
                    g.wait_ge(pe2a, mt - 1)
                g.dma_start(out=wcol[mt % 2], in_=wv4[mt]).then_inc(ld_wv[mt % 2], 16)
                if mt in (7, 15, 23):
                    ot = (mt + 1) // 8
                    g.wait_ge(ve_dq, ot * VE_DQ_OT)
                    g.dma_start(out=qw_i32, in_=qw_r[ot]).then_inc(ld_qw, 16)
                    g.dma_start(out=lut_t, in_=lut_own[128 * ot:128 * (ot + 1), :]).then_inc(ld_qw, 16)
            g.wait_ge(exp_v[0], 16 * 8 * 16)
            g.wait_ge(exp_v[1], 16 * 8 * 16)
            g.collective_compute("AllGather", AluOpType.bypass,
                                 replica_groups=[[0, 1], [2, 3], [4, 5], [6, 7]],
                                 ins=[vag_in.opt()], outs=[vag_out.opt()]).then_inc(cc_v)
            g.wait_ge(exp_wl[0], 16 * 64)
            g.wait_ge(exp_wl[1], 16 * 64)
            g.collective_compute("AllGather", AluOpType.bypass,
                                 replica_groups=[[0, 1, 2, 3, 4, 5, 6, 7]],
                                 ins=[wlag_in.opt()], outs=[wlag_out.opt()]).then_inc(cc_wl)
            for mt in range(32):
                if mt < 2:
                    g.wait_ge(pe2a, 31 + mt)
                if mt >= 2:
                    g.wait_ge(pe2b, mt - 1)
                g.dma_start(out=wcol[mt % 2], in_=wk4[mt]).then_inc(ld_wk[mt % 2], 16)
                if mt == 19:
                    g.wait_ge(exp_k[0], 16 * 8)
                    g.wait_ge(exp_k[1], 16 * 8)
                    g.collective_compute("AllGather", AluOpType.bypass,
                                         replica_groups=[[0, 1], [2, 3], [4, 5], [6, 7]],
                                         ins=[kag_inA.opt()], outs=[kag_outA.opt()]).then_inc(cc_k1)
            g.wait_ge(exp_k[2], 16 * 8)
            g.wait_ge(exp_k[3], 16 * 8)
            g.collective_compute("AllGather", AluOpType.bypass,
                                 replica_groups=[[0, 1], [2, 3], [4, 5], [6, 7]],
                                 ins=[kag_inB.opt()], outs=[kag_outB.opt()]).then_inc(cc_k2)
            for mt in range(32):
                if mt < 2:
                    g.wait_ge(pe2b, 31 + mt)
                else:
                    g.wait_ge(se5q, mt - 1)
                g.dma_start(out=wcol[mt % 2], in_=wq4[mt]).then_inc(ld_wq[mt % 2], 16)
            wor = wot.rearrange("(it p) f -> it p f", it=32)
            wbr = wbig.rearrange("p (it f) -> p it f", it=32)
            for oc in range(4):
                if oc == 0:
                    g.wait_ge(pe_avz, 16 * NH)
                    g.wait_ge(st_ao[0], 16 * 8 * NH // 2)
                    g.wait_ge(st_ao[1], 16 * 8 * NH // 2)
                else:
                    g.wait_ge(pe6, 8 * oc)
                for it in range(32):
                    g.dma_start(out=wbr[:, it, :],
                                in_=wor[it][:, 1024 * oc:1024 * (oc + 1)]).then_inc(ld_wo, 16)
            wlr = wlag_out.rearrange("(r it p) o -> r it p o", r=8, it=32)
            wbr2 = wbig.rearrange("p (it rb f) -> p it rb f", it=32, rb=2)
            for rp in range(4):
                if rp == 0:
                    g.wait_ge(cc_wl, 1)
                    g.wait_ge(pe6, 32)
                else:
                    g.wait_ge(pe8, 8 * rp)
                for rb in range(2):
                    for it in range(32):
                        g.dma_start(out=wbr2[:, it, rb, :],
                                    in_=wlr[2 * rp + rb, it]).then_inc(ld_wl, 16)

        @block.vector
        def _(v):
            v.wait_ge(ge, 1)
            v.tensor_scalar(out=ident, in0=iota_i, scalar1=0.0, scalar2=None,
                            op0=AluOpType.is_equal)
            v.tensor_scalar(out=identb, in0=iota_i, scalar1=0.0, scalar2=None,
                            op0=AluOpType.is_equal)
            v.memset(onesb[:, 0:1], 1.0)
            v.memset(onesr[0:1, :], 1.0)
            v.memset(eps_t, EPS)
            v.wait_ge(ld_misc, 64)
            v.tensor_scalar(out=bqs, in0=bqs, scalar1=SCALE, scalar2=None,
                            op0=AluOpType.mult).then_inc(ve0, 1)

            def ln_tile(tt, ld_sem, vsem, ssem, psem, x_src):
                v.wait_ge(ld_sem[tt % 2], 16 * (tt // 2 + 1))
                last = None
                for i in range(8):
                    last = v.bn_stats(bn6[:, 6 * i:6 * i + 6], x_src[:, 512 * i:512 * (i + 1)])
                last.then_inc(vsem, 1)
                v.wait_ge(vsem, VE1_TT * tt + 1)
                v.bn_aggr(mv, bn6).then_inc(vsem, 1)
                v.wait_ge(ssem, tt + 1)
                v.reciprocal(rstd, sq).then_inc(vsem, 1)
                v.wait_ge(vsem, VE1_TT * tt + 3)
                v.tensor_scalar(out=nmr, in0=mv[:, 0:1], scalar1=rstd, scalar2=-1.0,
                                op0=AluOpType.mult, op1=AluOpType.mult).then_inc(vsem, 1)
                v.wait_ge(vsem, VE1_TT * tt + 4)
                if tt >= 1:
                    v.wait_ge(psem, 4 * tt)
                v.tensor_scalar(out=xnf, in0=x_src, scalar1=rstd, scalar2=nmr,
                                op0=AluOpType.mult, op1=AluOpType.add).then_inc(vsem, 1)

            xnr = xn_dm.rearrange("p (it t) -> p it t", it=32)
            pdr = PD.rearrange("p (j t) -> p j t", j=8)
            for tt in range(8):
                ln_tile(tt, ld_x1, ve1, se1, pe1, x_io[tt % 2])
                for gq in range(4):
                    G = 4 * tt + gq
                    v.wait_ge(pe1, G + 1)
                    v.tensor_copy(xnr[:, 8 * gq:8 * gq + 8, 128 * tt:128 * (tt + 1)],
                                  pdr).then_inc(ve1b, 1)
            vcnt = 0
            for ot in range(4):
                v.wait_ge(ld_qw, 32 * (ot + 1))
                for pl in range(4):
                    last = None
                    for sh in range(8):
                        last = v.tensor_scalar(
                            out=bits[pl].rearrange("p (b sh) -> p sh b", sh=8)[:, sh, :],
                            in0=qw_u8[:, 512 * pl:512 * (pl + 1)],
                            scalar1=sh, scalar2=1,
                            op0=AluOpType.logical_shift_right,
                            op1=AluOpType.bitwise_and)
                    last.then_inc(ve_dq, 1); vcnt += 1
                v.wait_ge(ve_dq, vcnt)
                v.tensor_tensor(out=dlev,
                                in0=lut_t.rearrange("p (c two) -> p two c", two=2)[:, 1, :],
                                in1=lut_t.rearrange("p (c two) -> p two c", two=2)[:, 0, :],
                                op=AluOpType.subtract).then_inc(ve_dq, 1)
                vcnt += 1
                v.wait_ge(ve_dq, vcnt)
                for chk in range(NCHUNK):
                    C = NCHUNK * ot + chk
                    gidx = C // 2
                    sl = slice(CH * chk, CH * (chk + 1))
                    last = None
                    for c8 in range(8):
                        last = v.tensor_scalar(
                            out=tbuf[:, CH * c8:CH * (c8 + 1)], in0=bits[0][:, sl],
                            scalar1=dlev[:, c8:c8 + 1], scalar2=lut_t[:, 2 * c8:2 * c8 + 1],
                            op0=AluOpType.mult, op1=AluOpType.add)
                    last.then_inc(ve_dq, 1); vcnt += 1
                    v.wait_ge(ve_dq, vcnt)
                    for c4 in range(4):
                        last = v.select(ubuf[:, CH * c4:CH * (c4 + 1)], bits[1][:, sl],
                                        tbuf[:, CH * (2 * c4 + 1):CH * (2 * c4 + 2)],
                                        tbuf[:, CH * 2 * c4:CH * (2 * c4 + 1)], add_drain=True)
                    last.then_inc(ve_dq, 1); vcnt += 1
                    v.wait_ge(ve_dq, vcnt)
                    for c2 in range(2):
                        last = v.select(vbuf[:, CH * c2:CH * (c2 + 1)], bits[2][:, sl],
                                        ubuf[:, CH * (2 * c2 + 1):CH * (2 * c2 + 2)],
                                        ubuf[:, CH * 2 * c2:CH * (2 * c2 + 1)], add_drain=True)
                    last.then_inc(ve_dq, 1); vcnt += 1
                    v.wait_ge(ve_dq, vcnt)
                    if gidx >= 4:
                        v.wait_ge(pe_wt, gidx - 3)
                    v.select(wl_f2[gidx % 4][:, 512 * (C % 2):512 * (C % 2) + 512],
                             bits[3][:, sl], vbuf[:, CH:2 * CH], vbuf[:, 0:CH],
                             add_drain=True).then_inc(ve_dq, 1)
                    vcnt += 1
                    v.wait_ge(ve_dq, vcnt)
                    if C % 2 == 1 and gidx >= 2:
                        G = gidx - 2
                        v.wait_ge(pe_wt, G + 1)
                        if G >= 2:
                            v.wait_ge(exp_wl[G % 2], 16 * 8 * (G // 2))
                        v.tensor_copy(wlt_ev[G % 2], PD[:, 0:1024]).then_inc(ve_wt, 1)
                assert vcnt == (ot + 1) * VE_DQ_OT
            for G in (14, 15):
                v.wait_ge(pe_wt, G + 1)
                v.wait_ge(exp_wl[G % 2], 16 * 8 * (G // 2))
                v.tensor_copy(wlt_ev[G % 2], PD[:, 0:1024]).then_inc(ve_wt, 1)
            for H in range(NH):
                v.wait_ge(pe_avz, 16 * H + 16)
                v.reciprocal(rzrow[0:1, :], PD[0:1, 0:1024]).then_inc(ve_rz, 1)
                v.wait_ge(pe_rzb, H + 1)
                v.tensor_copy(rzb, PD[:, 0:1024]).then_inc(ve_fin, 1)
                v.wait_ge(ve_fin, 2 * H + 1)
                if H >= 2:
                    v.wait_ge(st_ao[H % 2], 16 * 8 * (H // 2))
                v.tensor_tensor(out=ao_sb[H % 2], in0=PC[:, 0:1024], in1=rzb,
                                op=AluOpType.mult).then_inc(ve_mid, 1)
                v.wait_ge(ve_mid, H + 1)
                v.tensor_scalar(out=ao_sb[H % 2], in0=ao_sb[H % 2],
                                scalar1=bv_sb[:, H:H + 1], scalar2=None,
                                op0=AluOpType.add).then_inc(ve_fin, 1)
            for c in range(4):
                v.wait_ge(pe_bb, c + 1)
                v.tensor_copy(biasB[:, 1024 * c:1024 * (c + 1)], PD[:, 0:1024]).then_inc(ve_bb, 1)
            for oc in range(4):
                for tt in range(8):
                    i = 8 * oc + tt
                    v.wait_ge(pe6, i + 1)
                    v.wait_ge(ld_x6[i % 2], 16 * (i // 2 + 1))
                    if i >= 2:
                        v.wait_ge(st_x2[i % 2], 16 * (i // 2))
                    ps = PA if tt % 2 == 0 else PB
                    v.tensor_tensor(out=x2o[tt % 2], in0=ps[:, 0:1024],
                                    in1=biasB[:, 1024 * oc:1024 * (oc + 1)],
                                    op=AluOpType.add).then_inc(ve_mid, 1)
                    v.wait_ge(ve_mid, 32 + i + 1)
                    v.tensor_tensor(out=x2o[tt % 2], in0=x2o[tt % 2], in1=xch[tt % 2],
                                    op=AluOpType.add).then_inc(ve6, 1)
            for tt in range(8):
                ln_tile(tt, ld_x27, ve7, se7, pe7, x_io[tt % 2])
                for gq in range(4):
                    G = 4 * tt + gq
                    v.wait_ge(pe7, G + 1)
                    v.tensor_copy(xnr[:, 8 * gq:8 * gq + 8, 128 * tt:128 * (tt + 1)],
                                  pdr).then_inc(ve7b, 1)
            for c in range(4):
                v.wait_ge(pe_bb, 4 + c + 1)
                v.tensor_copy(biasB[:, 1024 * c:1024 * (c + 1)], PD[:, 0:1024]).then_inc(ve_bb, 1)
            for rp in range(4):
                for tt in range(8):
                    i = 8 * rp + tt
                    v.wait_ge(pe8, i + 1)
                    v.wait_ge(ld_x28[i % 2], 16 * (i // 2 + 1))
                    if i >= 2:
                        v.wait_ge(st_out[i % 2], 16 * (i // 2))
                    ps = PA if tt % 2 == 0 else PB
                    v.tensor_tensor(out=outb[tt % 2], in0=ps[:, 0:1024],
                                    in1=biasB[:, 1024 * rp:1024 * (rp + 1)],
                                    op=AluOpType.add).then_inc(ve_mid, 1)
                    v.wait_ge(ve_mid, 64 + i + 1)
                    v.tensor_tensor(out=outb[tt % 2], in0=outb[tt % 2], in1=x2ch[tt % 2],
                                    op=AluOpType.add).then_inc(ve8, 1)

        @block.scalar
        def _(sc):
            sc.wait_ge(ve0, 1)
            for tt in range(8):
                sc.wait_ge(ve1, VE1_TT * tt + 2)
                sc.activation(sq, mv[:, 1:2], AF.Sqrt, bias=eps_t, scale=1.0).then_inc(se1, 1)
            for mt in range(32):
                sc.wait_ge(pe2a, mt + 1)
                if mt >= 2:
                    sc.wait_ge(pe_vt, mt - 1)
                ps = (PA, PB, PC)[mt % 3]
                sc.activation(xvf[mt % 2], ps[:, 0:1024], AF.Identity).then_inc(se2a, 1)
                sc.wait_ge(pe_vt, mt + 1)
                if mt >= 2:
                    sc.wait_ge(exp_v[mt % 2], 16 * 8 * (mt // 2))
                sc.activation(vexp[mt % 2], PD[:, 0:1024], AF.Identity).then_inc(se_vx, 1)
            for mt in range(32):
                sc.wait_ge(pe2b, mt + 1)
                if mt >= 2:
                    pm = mt - 2
                    sc.wait_ge(exp_k[2 * (pm // 16) + pm % 2], 16 * ((pm % 16) // 2 + 1))
                ps = (PA, PB, PC)[mt % 3]
                sc.activation(kexp[mt % 2], ps[:, 0:1024], AF.Identity,
                              bias=bk_sb[:, mt:mt + 1], scale=1.0).then_inc(se2b, 1)
            for hg in range(NHG):
                for mtl in range(4):
                    mt = 4 * hg + mtl
                    sc.wait_ge(pe2b, 32 + mt + 1)
                    if hg >= 2 and mtl == 0:
                        sc.wait_ge(pe_sc, 16 * 4 * (hg - 1))
                    ps = PA if mt % 2 == 0 else PB
                    sc.activation(Q_hg[hg % 2][:, 1024 * mtl:1024 * (mtl + 1)], ps[:, 0:1024],
                                  AF.Identity, bias=bqs[:, mt:mt + 1], scale=SCALE).then_inc(se5q, 1)
                for hl in range(4):
                    H = 4 * hg + hl
                    for kt in range(16):
                        if kt == 4 and H >= 1:
                            Hp = H - 1
                            sc.wait_ge(ve_fin, 2 * Hp + 2)
                            for tt in range(8):
                                sc.dma_start(out=ao_scr[tt, :, Hp, :],
                                             in_=ao_sb[Hp % 2][:, 128 * tt:128 * (tt + 1)]).then_inc(st_ao[Hp % 2], 16)
                        sc.wait_ge(pe_sc, 16 * H + kt + 1)
                        if H >= 1:
                            sc.wait_ge(pe_avz, 16 * (H - 1) + kt + 1)
                        slot = PA if kt % 2 == 0 else PB
                        q0 = q0_of(kt)
                        sc.activation(attT[:, ATO[kt]:ATO[kt] + ATW[kt]],
                                      slot[:, q0:1024], AF.Exp).then_inc(se_exp, 1)
            sc.wait_ge(ve_fin, 2 * NH)
            for tt in range(8):
                sc.dma_start(out=ao_scr[tt, :, NH - 1, :],
                             in_=ao_sb[(NH - 1) % 2][:, 128 * tt:128 * (tt + 1)]).then_inc(st_ao[(NH - 1) % 2], 16)
            for oc in range(4):
                for tt in range(8):
                    i = 8 * oc + tt
                    sc.wait_ge(ve6, i + 1)
                    sc.dma_start(out=x2_scr[128 * tt:128 * (tt + 1), 1024 * oc:1024 * (oc + 1)],
                                 in_=x2o[tt % 2]).then_inc(st_x2[i % 2], 16)
            for tt in range(8):
                sc.wait_ge(ve7, VE1_TT * tt + 2)
                sc.activation(sq, mv[:, 1:2], AF.Sqrt, bias=eps_t, scale=1.0).then_inc(se7, 1)
            for rp in range(4):
                for tt in range(8):
                    i = 8 * rp + tt
                    sc.wait_ge(ve8, i + 1)
                    sc.dma_start(out=out_own[128 * tt:128 * (tt + 1), 1024 * rp:1024 * (rp + 1)],
                                 in_=outb[tt % 2]).then_inc(st_out[i % 2], 16)
            sc.wait_ge(st_out[0], 16 * 16)
            sc.wait_ge(st_out[1], 16 * 16)

        @block.tensor
        def _(te):
            te.wait_ge(ve0, 1)
            for tt in range(8):
                te.wait_ge(ve1, VE1_TT * (tt + 1))
                for gq in range(4):
                    G = 4 * tt + gq
                    if G > 0:
                        te.wait_ge(ve1b, G)
                    last = None
                    for j in range(8):
                        d = 8 * gq + j
                        last = te.transpose(PD[:, 128 * j:128 * (j + 1)],
                                            xnf[:, 128 * d:128 * (d + 1)], ident)
                    last.then_inc(pe1, 1)
            prev_pd = None
            for mt in range(33):
                if mt < 32:
                    te.wait_ge(ld_wv[mt % 2], 16 * (mt // 2 + 1))
                    if mt == 0:
                        te.wait_ge(ve1b, 32)
                    if mt >= 3:
                        te.wait_ge(se2a, mt - 2)
                    ps = (PA, PB, PC)[mt % 3]
                    last = None
                    for it in range(32):
                        for (c0, c1) in ((0, 512), (512, 1024)):
                            last = te.matmul(ps[:, c0:c1], wcol[mt % 2][:, 128 * it:128 * (it + 1)],
                                             xn_dm[:, 1024 * it + c0:1024 * it + c1],
                                             start=(it == 0), stop=(it == 31))
                    last.then_inc(pe2a, 1)
                if mt >= 1:
                    mp = mt - 1
                    te.wait_ge(se2a, mp + 1)
                    if prev_pd is not None:
                        te.wait_ge(*prev_pd)
                    last = None
                    for j in range(8):
                        last = te.transpose(PD[:, 128 * j:128 * (j + 1)],
                                            xvf[mp % 2][:, 128 * j:128 * (j + 1)], ident)
                    last.then_inc(pe_vt, 1)
                    prev_pd = (se_vx, mp + 1)
                    if mp % 2 == 1 and mp // 2 < 16:
                        G = mp // 2
                        te.wait_ge(ve_dq, dq_chunk_done(G // 4, 2 * (G % 4) + 1))
                        te.wait_ge(*prev_pd)
                        last = None
                        for j in range(8):
                            last = te.transpose(PD[:, 128 * j:128 * (j + 1)],
                                                wl_f2[G % 4][:, 128 * j:128 * (j + 1)], ident)
                        last.then_inc(pe_wt, 1)
                        prev_pd = (ve_wt, G + 1)
            for mt in range(32):
                te.wait_ge(ld_wk[mt % 2], 16 * (mt // 2 + 1))
                if mt < 3:
                    te.wait_ge(se2a, (31, 32, 30)[mt])
                if mt >= 3:
                    te.wait_ge(se2b, mt - 2)
                ps = (PA, PB, PC)[mt % 3]
                last = None
                for it in range(32):
                    for (c0, c1) in ((0, 512), (512, 1024)):
                        last = te.matmul(ps[:, c0:c1], wcol[mt % 2][:, 128 * it:128 * (it + 1)],
                                         xn_dm[:, 1024 * it + c0:1024 * it + c1],
                                         start=(it == 0), stop=(it == 31))
                last.then_inc(pe2b, 1)
            for hg in range(NHG):
                for mtl in range(4):
                    mt = 4 * hg + mtl
                    te.wait_ge(ld_wq[mt % 2], 16 * (mt // 2 + 1))
                    if hg == 0 and mtl < 2:
                        te.wait_ge(se2b, 31 + mtl)
                        te.wait_ge(ve_wt, 16)
                    if hg >= 1 and mtl < 2:
                        te.wait_ge(se_exp, 16 * 4 * hg)
                    if mtl >= 2:
                        te.wait_ge(se5q, 4 * hg + mtl - 1)
                    ps = PA if mt % 2 == 0 else PB
                    last = None
                    for it in range(32):
                        for (c0, c1) in ((0, 512), (512, 1024)):
                            last = te.matmul(ps[:, c0:c1], wcol[mt % 2][:, 128 * it:128 * (it + 1)],
                                             xn_dm[:, 1024 * it + c0:1024 * it + c1],
                                             start=(it == 0), stop=(it == 31))
                    last.then_inc(pe2b, 1)
                def mk_avz(H):
                    def avz_pair(s_):
                        if s_ == 0:
                            if H >= 1:
                                te.wait_ge(ve_fin, 2 * (H - 1) + 2)
                            else:
                                te.wait_ge(ve_wt, 16)
                        for kt in (2 * s_, 2 * s_ + 1):
                            te.wait_ge(se_exp, 16 * H + 2 * s_ + 2)
                            lastm = None
                            for (c0, c1) in q_ranges(kt):
                                stop = (kt == 7) if c1 <= 512 else (kt == 15)
                                lastm = te.matmul(
                                    PC[:, c0:c1], v_sb[H % 2][:, 128 * kt:128 * (kt + 1)],
                                    attT[:, ATO[kt] + c0 - q0_of(kt):ATO[kt] + c1 - q0_of(kt)],
                                    start=(kt == 0), stop=stop)
                            for (c0, c1) in q_ranges(kt):
                                stop = (kt == 7) if c1 <= 512 else (kt == 15)
                                lastm = te.matmul(
                                    PD[0:1, c0:c1], onesb[:, 0:1],
                                    attT[:, ATO[kt] + c0 - q0_of(kt):ATO[kt] + c1 - q0_of(kt)],
                                    start=(kt == 0), stop=stop)
                            lastm.then_inc(pe_avz, 1)
                    return avz_pair

                def rz_bcast(Hp):
                        te.wait_ge(ve_rz, Hp + 1)
                        last = None
                        for h2 in range(2):
                            last = te.matmul(PD[:, 512 * h2:512 * (h2 + 1)], onesr[0:1, :],
                                             rzrow[0:1, 512 * h2:512 * (h2 + 1)],
                                             start=True, stop=True)
                        last.then_inc(pe_rzb, 1)

                for hl in range(4):
                    H = 4 * hg + hl
                    te.wait_ge(ld_kv[H % 2], 16 * 8 * (H // 2 + 1))
                    avz_pair = mk_avz(H)
                    for s_ in range(8):
                        for kt in (2 * s_, 2 * s_ + 1):
                            tgt = 16 * H + kt - 1
                            if tgt > 0:
                                te.wait_ge(se_exp, tgt)
                            if kt == 0:
                                te.wait_ge(se5q, 4 * (hg + 1))
                            slot = PA if kt % 2 == 0 else PB
                            s_of = kt // 2
                            j_of = kt % 2
                            last = None
                            for (c0, c1) in q_ranges(kt):
                                has_mask = (c0 == q0_of(kt))
                                last = te.matmul(slot[:, c0:c1],
                                                 k_sb[H % 2][:, 128 * kt:128 * (kt + 1)],
                                                 Q_hg[hg % 2][:, 1024 * hl + c0:1024 * hl + c1],
                                                 start=True, stop=not has_mask)
                                if has_mask:
                                    last = te.matmul(
                                        slot[:, c0:c0 + 128], identb,
                                        masks[:, 256 * s_of + 128 * j_of:256 * s_of + 128 * (j_of + 1)],
                                        start=False, stop=True)
                            last.then_inc(pe_sc, 1)
                        if s_ == 0 and H >= 1:
                            mk_avz(H - 1)(7)
                        if s_ == 1 and H >= 1:
                            rz_bcast(H - 1)
                        if s_ >= 2:
                            avz_pair(s_ - 2)
                    avz_pair(6)
                    if H == NH - 1:
                        avz_pair(7)
                        rz_bcast(H)
            for c in range(4):
                if c == 0:
                    te.wait_ge(ve_fin, 2 * NH)
                te.wait_ge(ld_misc, 64 + 16 * (c + 1))
                if c > 0:
                    te.wait_ge(ve_bb, c)
                last = None
                for h2 in range(2):
                    last = te.matmul(PD[:, 512 * h2:512 * (h2 + 1)], onesr[0:1, :],
                                     rzrow[0:1, 512 * h2:512 * (h2 + 1)], start=True, stop=True)
                last.then_inc(pe_bb, 1)
            for oc in range(4):
                te.wait_ge(ld_wo, 16 * 32 * (oc + 1))
                if oc == 0:
                    te.wait_ge(ve_bb, 4)
                for tt in range(8):
                    i = 8 * oc + tt
                    te.wait_ge(ld_ao[i % 2], 16 * (i // 2 + 1))
                    if i < 2:
                        te.wait_ge(se_exp, 16 * NH)
                    if i >= 2:
                        te.wait_ge(ve6, i - 1)
                    ps = PA if tt % 2 == 0 else PB
                    last = None
                    for it in range(32):
                        for (c0, c1) in ((0, 512), (512, 1024)):
                            last = te.matmul(ps[:, c0:c1], aost[tt % 2][:, 128 * it:128 * (it + 1)],
                                             wbig[:, 1024 * it + c0:1024 * it + c1],
                                             start=(it == 0), stop=(it == 31))
                    last.then_inc(pe6, 1)
            for tt in range(8):
                te.wait_ge(ve7, VE1_TT * (tt + 1))
                for gq in range(4):
                    G = 4 * tt + gq
                    if G > 0:
                        te.wait_ge(ve7b, G)
                    else:
                        te.wait_ge(ve_bb, 4)
                    last = None
                    for j in range(8):
                        d = 8 * gq + j
                        last = te.transpose(PD[:, 128 * j:128 * (j + 1)],
                                            xnf[:, 128 * d:128 * (d + 1)], ident)
                    last.then_inc(pe7, 1)
            for c in range(4):
                if c == 0:
                    te.wait_ge(ve7b, 32)
                te.wait_ge(ld_misc, 128 + 16 * (c + 1))
                if c > 0:
                    te.wait_ge(ve_bb, 4 + c)
                last = None
                for h2 in range(2):
                    last = te.matmul(PD[:, 512 * h2:512 * (h2 + 1)], onesr[0:1, :],
                                     rzrow[0:1, 512 * h2:512 * (h2 + 1)], start=True, stop=True)
                last.then_inc(pe_bb, 1)
            wbr2t = wbig.rearrange("p (it rb f) -> p it rb f", it=32, rb=2)
            for rp in range(4):
                te.wait_ge(ld_wl, 16 * 64 * (rp + 1))
                if rp == 0:
                    te.wait_ge(ve_bb, 8)
                    te.wait_ge(ve6, 32)
                for tt in range(8):
                    i = 8 * rp + tt
                    ps = PA if tt % 2 == 0 else PB
                    if i >= 2:
                        te.wait_ge(ve8, i - 1)
                    last = None
                    for it in range(32):
                        for rb in range(2):
                            last = te.matmul(
                                ps[:, 512 * rb:512 * (rb + 1)],
                                xn_dm[:, 1024 * it + 128 * tt:1024 * it + 128 * (tt + 1)],
                                wbr2t[:, it, rb, :],
                                start=(it == 0), stop=(it == 31))
                    last.then_inc(pe8, 1)
    return nc


def kernel(L0_attn_x, L0_attn_Wq, L0_attn_bq, L0_attn_Wk, L0_attn_bk,
           L0_attn_Wv, L0_attn_bv, L0_attn_Wo, L0_attn_bo,
           L0_ffn_qweight, L0_ffn_lut4, L0_ffn_lut8, L0_ffn_bias):
    x = np.asarray(L0_attn_x, dtype=np.float32)
    Wq = np.asarray(L0_attn_Wq, dtype=np.float32)
    Wk = np.asarray(L0_attn_Wk, dtype=np.float32)
    Wv = np.asarray(L0_attn_Wv, dtype=np.float32)
    Wo = np.asarray(L0_attn_Wo, dtype=np.float32)
    bq = np.asarray(L0_attn_bq, dtype=np.float32)
    bk = np.asarray(L0_attn_bk, dtype=np.float32)
    bv = np.asarray(L0_attn_bv, dtype=np.float32)
    bo = np.asarray(L0_attn_bo, dtype=np.float32)
    qw = np.asarray(L0_ffn_qweight)
    lut4 = np.asarray(L0_ffn_lut4, dtype=np.float32)
    bfb = np.asarray(L0_ffn_bias, dtype=np.float32)

    wq4 = np.ascontiguousarray(Wq.reshape(32, 128, 32, 128).transpose(0, 3, 2, 1))
    wk4 = np.ascontiguousarray(Wk.reshape(32, 128, 32, 128).transpose(0, 3, 2, 1))
    wv4 = np.ascontiguousarray(Wv.reshape(32, 128, 32, 128).transpose(0, 3, 2, 1))
    wot = np.ascontiguousarray(Wo.T)
    bq_t = np.ascontiguousarray(bq.reshape(32, 128).T)
    bk_t = np.ascontiguousarray(bk.reshape(32, 128).T)
    bv_t = np.ascontiguousarray(bv.reshape(32, 128).T)
    bo_row = np.ascontiguousarray(bo.reshape(1, 4096))
    bf_row = np.ascontiguousarray(bfb.reshape(1, 4096))

    def build_mask(parity):
        QT = QT_EVEN if parity == 0 else QT_ODD
        m = np.zeros((128, 2048), np.float32)
        for s in range(8):
            qt = QT[s]
            for j in range(2):
                kt = 2 * s + j
                if kt < qt:
                    t = np.zeros((128, 128), np.float32)
                elif kt == qt:
                    t = np.where(np.arange(128)[:, None] <= np.arange(128)[None, :],
                                 0.0, -30000.0).astype(np.float32)
                else:
                    t = np.full((128, 128), -30000.0, np.float32)
                m[:, 256 * s + 128 * j:256 * s + 128 * (j + 1)] = t
        return m.astype(BF)

    mask_by_par = [build_mask(0), build_mask(1)]

    nc = build_nc()
    in_maps = []
    for core in range(8):
        b = core // 2
        parity = core % 2
        QT = QT_EVEN if parity == 0 else QT_ODD
        rows = np.concatenate([128 * qt + np.arange(128) for qt in QT])
        in_maps.append({
            "x_own": np.ascontiguousarray(x[b][rows]),
            "wq4": wq4, "wk4": wk4, "wv4": wv4, "wot": wot,
            "bq_t": bq_t, "bk_t": bk_t, "bv_t": bv_t,
            "bo_row": bo_row, "bf_row": bf_row,
            "qw_own": np.ascontiguousarray(qw[0:4, 512 * core:512 * (core + 1), :]),
            "lut_own": np.ascontiguousarray(lut4[512 * core:512 * (core + 1), :]),
            "masks_in": mask_by_par[parity],
        })
    res = run_bass_kernel_spmd(nc, in_maps, core_ids=list(range(8)))
    out = np.empty((4, 2048, 4096), dtype=np.float32)
    for core in range(8):
        b = core // 2
        parity = core % 2
        QT = QT_EVEN if parity == 0 else QT_ODD
        rows = np.concatenate([128 * qt + np.arange(128) for qt in QT])
        out[b][rows] = res.results[core]["out_own"]
    return out


# version 12
# speedup vs baseline: 1.0611x; 1.0577x over previous
import contextlib
import math
import numpy as np
import ml_dtypes
import concourse.bass as bass
import concourse.mybir as mybir
from concourse.bass_utils import run_bass_kernel_spmd
from concourse.alu_op_type import AluOpType

F32 = mybir.dt.float32
BF16 = mybir.dt.bfloat16
I32 = mybir.dt.int32
U8 = mybir.dt.uint8
AF = mybir.ActivationFunctionType
BF = ml_dtypes.bfloat16

QT_EVEN = [0, 3, 4, 7, 8, 11, 12, 15]
QT_ODD = [1, 2, 5, 6, 9, 10, 13, 14]
SCALE = 1.0 / math.sqrt(128.0)
EPS = 1e-5
NH = 32
NHG = 8
CH = 512
NCHUNK = 4096 // CH
VE_DQ_OT = 4 + 1 + NCHUNK * 4
VE1_TT = 5

def q0_of(kt):
    return 128 * (kt // 2)
ATW = [1024 - q0_of(kt) for kt in range(16)]
ATO = [0] * 16
for _kt in range(1, 16):
    ATO[_kt] = ATO[_kt - 1] + ATW[_kt - 1]
ATT_TOT = ATO[15] + ATW[15]

def q_ranges(kt):
    q0 = q0_of(kt)
    return [(q0, 512), (512, 1024)] if q0 < 512 else [(q0, 1024)]


def build_nc():
    nc = bass.Bass("TRN2", target_bir_lowering=False)
    x_own = nc.dram_tensor("x_own", [1024, 4096], F32, kind="ExternalInput").ap()
    wq4 = nc.dram_tensor("wq4", [32, 128, 32, 128], F32, kind="ExternalInput").ap()
    wk4 = nc.dram_tensor("wk4", [32, 128, 32, 128], F32, kind="ExternalInput").ap()
    wv4 = nc.dram_tensor("wv4", [32, 128, 32, 128], F32, kind="ExternalInput").ap()
    wot = nc.dram_tensor("wot", [4096, 4096], F32, kind="ExternalInput").ap()
    bq_t = nc.dram_tensor("bq_t", [128, 32], F32, kind="ExternalInput").ap()
    bk_t = nc.dram_tensor("bk_t", [128, 32], F32, kind="ExternalInput").ap()
    bv_t = nc.dram_tensor("bv_t", [128, 32], F32, kind="ExternalInput").ap()
    bo_row = nc.dram_tensor("bo_row", [1, 4096], F32, kind="ExternalInput").ap()
    bf_row = nc.dram_tensor("bf_row", [1, 4096], F32, kind="ExternalInput").ap()
    qw_own = nc.dram_tensor("qw_own", [4, 512, 128], I32, kind="ExternalInput").ap()
    lut_own = nc.dram_tensor("lut_own", [512, 16], F32, kind="ExternalInput").ap()
    masks_in = nc.dram_tensor("masks_in", [128, 2048], BF16, kind="ExternalInput").ap()
    out_own = nc.dram_tensor("out_own", [1024, 4096], F32, kind="ExternalOutput").ap()
    kag_inA = nc.dram_tensor("kag_inA", [2048, 1024], BF16).ap()
    kag_inB = nc.dram_tensor("kag_inB", [2048, 1024], BF16).ap()
    kag_outA = nc.dram_tensor("kag_outA", [4096, 1024], BF16).ap()
    kag_outB = nc.dram_tensor("kag_outB", [4096, 1024], BF16).ap()
    vag_in = nc.dram_tensor("vag_in", [1024, 4096], BF16).ap()
    vag_out = nc.dram_tensor("vag_out", [2048, 4096], BF16).ap()
    wlag_in = nc.dram_tensor("wlag_in", [4096, 512], BF16).ap()
    wlag_out = nc.dram_tensor("wlag_out", [32768, 512], BF16).ap()
    ao_scr = nc.dram_tensor("ao_scr", [8, 128, 32, 128], BF16).ap()
    x2_scr = nc.dram_tensor("x2_scr", [1024, 4096], F32).ap()

    es = contextlib.ExitStack()
    with es:
        AR = es.enter_context(nc.sbuf_tensor("AR", [128, 204 * 1024], U8))
        arf, arb, ari, aru = AR.bitcast(F32), AR.bitcast(BF16), AR.bitcast(I32), AR.bitcast(U8)

        def fsl(o, n):
            assert o % 64 == 0
            return arf.ap()[:, o // 4:o // 4 + n]

        def bsl(o, n):
            assert o % 64 == 0
            return arb.ap()[:, o // 2:o // 2 + n]

        def isl(o, n):
            assert o % 64 == 0
            return ari.ap()[:, o // 4:o // 4 + n]

        def usl(o, n):
            assert o % 64 == 0
            return aru.ap()[:, o:o + n]

        masks = bsl(0, 2048)
        biasB = bsl(4096, 4096)
        rzb = fsl(12288, 1024)
        rzrow = fsl(16384, 1024)
        ident = fsl(20480, 128)
        identb = bsl(23040, 128)
        iota_i = isl(20992, 128)
        onesr = fsl(21504, 128)
        onesb = bsl(22016, 32)
        bqs = fsl(22080, 32)
        bk_sb = fsl(22208, 32)
        bv_sb = fsl(22336, 32)
        bn6 = fsl(22464, 48)
        mv = fsl(22656, 2)
        sq = fsl(22720, 1)
        rstd = fsl(22784, 1)
        nmr = fsl(22848, 1)
        eps_t = fsl(22912, 1)
        S1 = 24 * 1024
        xn_dm = bsl(S1, 32768)
        S2 = 88 * 1024
        bits = [usl(S2 + 4096 * i, 4096) for i in range(4)]
        qw_i32 = isl(S2 + 16384, 512)
        qw_u8 = usl(S2 + 16384, 2048)
        lut_t = fsl(S2 + 18432, 16)
        dlev = fsl(S2 + 18496, 8)
        tbuf = fsl(S2 + 18560, 8 * CH)
        ubuf = fsl(S2 + 34944, 4 * CH)
        vbuf = fsl(S2 + 43136, 2 * CH)
        wl_f2 = [fsl(S2 + 47232 + 4096 * i, 1024) for i in range(4)]
        wlt_ev = [bsl(4096 + 2048 * i, 1024) for i in range(2)]
        attT = bsl(S2, ATT_TOT)
        Q_hg = [bsl(S2 + 18432 + 8192 * i, 4096) for i in range(2)]
        k_sb = [bsl(S2 + 34816 + 4096 * i, 2048) for i in range(2)]
        v_sb = [bsl(S2 + 43008 + 4096 * i, 2048) for i in range(2)]
        ao_sb = [bsl(S2 + 51200 + 2048 * i, 1024) for i in range(2)]
        wbig = bsl(S2, 32768)
        R3 = 152 * 1024
        vexp = [bsl(R3 + 24576 + 2048 * i, 1024) for i in range(2)]
        xvf = [fsl(R3 + 16384 + 4096 * i, 1024) for i in range(2)]
        x_io = [fsl(R3 + 16384 * i, 4096) for i in range(2)]
        xnf = fsl(R3 + 32768, 4096)
        wcol = [bsl(R3 + 8192 * i, 4096) for i in range(2)]
        kexp = [bsl(R3 + 16384 + 2048 * i, 1024) for i in range(2)]
        aost = [bsl(R3 + 8192 * i, 4096) for i in range(2)]
        xch = [fsl(R3 + 16384 + 4096 * i, 1024) for i in range(2)]
        x2o = [fsl(R3 + 24576 + 4096 * i, 1024) for i in range(2)]
        x2ch = [fsl(R3 + 4096 * i, 1024) for i in range(2)]
        outb = [fsl(R3 + 8192 + 4096 * i, 1024) for i in range(2)]

        PA = es.enter_context(nc.psum_tensor("PA", [128, 1024], F32))
        PB = es.enter_context(nc.psum_tensor("PB", [128, 1024], F32))
        PC = es.enter_context(nc.psum_tensor("PC", [128, 1024], F32))
        PD = es.enter_context(nc.psum_tensor("PD", [128, 1024], F32))

        def sem(name):
            return es.enter_context(nc.semaphore(name))
        ld_misc = sem("ld_misc"); ld_qw = sem("ld_qw")
        ld_x1 = [sem("ld_x1a"), sem("ld_x1b")]
        ld_wv = [sem("ld_wva"), sem("ld_wvb")]
        ld_wk = [sem("ld_wka"), sem("ld_wkb")]
        ld_wq = [sem("ld_wqa"), sem("ld_wqb")]
        ld_wo = sem("ld_wo"); ld_wl = sem("ld_wl")
        ld_kv = [sem("ld_kva"), sem("ld_kvb")]
        ld_ao = [sem("ld_aoa"), sem("ld_aob")]
        ld_x6 = [sem("ld_x6a"), sem("ld_x6b")]
        ld_x27 = [sem("ld_x27a"), sem("ld_x27b")]
        ld_x28 = [sem("ld_x28a"), sem("ld_x28b")]
        exp_v = [sem("exp_va"), sem("exp_vb")]
        exp_k = [sem("exp_ka"), sem("exp_kb"), sem("exp_kc"), sem("exp_kd")]
        exp_wl = [sem("exp_wla"), sem("exp_wlb")]
        cc_k1 = sem("cc_k1"); cc_k2 = sem("cc_k2"); cc_v = sem("cc_v"); cc_wl = sem("cc_wl")
        st_ao = [sem("st_aoa"), sem("st_aob")]
        st_x2 = [sem("st_x2a"), sem("st_x2b")]
        st_out = [sem("st_outa"), sem("st_outb")]
        ge = sem("ge"); ve0 = sem("ve0")
        ve1 = sem("ve1"); se1 = sem("se1"); pe1 = sem("pe1"); ve1b = sem("ve1b")
        pe_bb = sem("pe_bb"); ve_bb = sem("ve_bb")
        pe2a = sem("pe2a"); se2a = sem("se2a"); pe_vt = sem("pe_vt"); se_vx = sem("se_vx")
        pe2b = sem("pe2b"); se2b = sem("se2b")
        ve_dq = sem("ve_dq"); pe_wt = sem("pe_wt"); ve_wt = sem("ve_wt")
        pe_sc = sem("pe_sc"); pe_avz = sem("pe_avz"); pe_rzb = sem("pe_rzb")
        se_exp = sem("se_exp"); se5q = sem("se5q")
        ve_mask = sem("ve_mask"); ve_rz = sem("ve_rz"); ve_fin = sem("ve_fin")
        pe6 = sem("pe6"); ve6 = sem("ve6")
        ve7 = sem("ve7"); se7 = sem("se7"); pe7 = sem("pe7"); ve7b = sem("ve7b")
        pe8 = sem("pe8"); ve8 = sem("ve8"); ve_mid = sem("ve_mid")

        block = es.enter_context(nc.Block())

        def dq_chunk_done(ot, c_local):
            return ot * VE_DQ_OT + 5 + 4 * (c_local + 1)

        @block.sync
        def _(s):
            s.dma_start(out=masks, in_=masks_in).then_inc(ld_misc, 16)
            s.dma_start(out=bqs, in_=bq_t).then_inc(ld_misc, 16)
            s.dma_start(out=bk_sb, in_=bk_t).then_inc(ld_misc, 16)
            s.dma_start(out=bv_sb, in_=bv_t).then_inc(ld_misc, 16)
            for tt in range(8):
                if tt >= 2:
                    s.wait_ge(ve1, VE1_TT * (tt - 1))
                s.dma_start(out=x_io[tt % 2],
                            in_=x_own[128 * tt:128 * (tt + 1), :]).then_inc(ld_x1[tt % 2], 16)
            for mt in range(32):
                s.wait_ge(se_vx, mt + 1)
                for tt in range(8):
                    s.dma_start(out=vag_in[128 * tt:128 * (tt + 1), 128 * mt:128 * (mt + 1)],
                                in_=vexp[mt % 2][:, 128 * tt:128 * (tt + 1)]).then_inc(exp_v[mt % 2], 16)
                if mt % 2 == 1:
                    G = mt // 2
                    s.wait_ge(ve_wt, G + 1)
                    ot, gq = G // 4, G % 4
                    for j in range(8):
                        it = 8 * gq + j
                        s.dma_start(out=wlag_in[128 * it:128 * (it + 1), 128 * ot:128 * (ot + 1)],
                                    in_=wlt_ev[G % 2][:, 128 * j:128 * (j + 1)]).then_inc(exp_wl[G % 2], 16)
            for mt in range(32):
                s.wait_ge(se2b, mt + 1)
                half = mt // 16
                mloc = mt % 16
                kag = kag_inA if half == 0 else kag_inB
                s.dma_start(out=kag[128 * mloc:128 * (mloc + 1), :],
                            in_=kexp[mt % 2]).then_inc(exp_k[2 * half + mt % 2], 16)
            kr2A = kag_outA.rearrange("(r f) (j w t) -> r f j w t", r=2, j=4, w=2, t=128)
            kr2B = kag_outB.rearrange("(r f) (j w t) -> r f j w t", r=2, j=4, w=2, t=128)
            vr = vag_out.rearrange("(r j w p) f -> r p j w f", r=2, j=4, w=2, p=128)
            for H in range(NH):
                par = H % 2
                if H == 0:
                    s.wait_ge(cc_k1, 1)
                    s.wait_ge(cc_v, 1)
                if H == 16:
                    s.wait_ge(cc_k2, 1)
                if H < 2:
                    s.wait_ge(ve_dq, 4 * VE_DQ_OT)
                    s.wait_ge(ve_wt, 16)
                if H >= 2:
                    s.wait_ge(pe_sc, 16 * (H - 1))
                    s.wait_ge(pe_avz, 16 * (H - 1))
                for rp in range(2):
                    for w in range(2):
                        goff = (384 * w) if rp == 0 else (128 + 128 * w)
                        dk = k_sb[par].rearrange("p (j rest) -> p j rest", j=4)[:, :, goff:goff + 128]
                        kr2h = kr2A if H < 16 else kr2B
                        Hl = H % 16
                        s.dma_start(out=dk, in_=kr2h[rp, 128 * Hl:128 * (Hl + 1), :, w, :]).then_inc(ld_kv[par], 16)
                        dv = v_sb[par].rearrange("p (j rest) -> p j rest", j=4)[:, :, goff:goff + 128]
                        s.dma_start(out=dv, in_=vr[rp, :, :, w, 128 * H:128 * (H + 1)]).then_inc(ld_kv[par], 16)
            for c in range(4):
                if c == 0:
                    s.wait_ge(ve_fin, 2 * NH)
                if c > 0:
                    s.wait_ge(pe_bb, c)
                s.dma_start(out=rzrow[0:1, :],
                            in_=bo_row[0:1, 1024 * c:1024 * (c + 1)]).then_inc(ld_misc, 16)
            for oc in range(4):
                for tt in range(8):
                    i = 8 * oc + tt
                    if i == 0:
                        s.wait_ge(st_ao[0], 16 * 8 * NH // 2)
                        s.wait_ge(st_ao[1], 16 * 8 * NH // 2)
                        s.wait_ge(se5q, 4 * NHG)
                    if i >= 2:
                        s.wait_ge(pe6, i - 1)
                    s.dma_start(out=aost[tt % 2],
                                in_=ao_scr[tt].rearrange("p it t -> p (it t)")).then_inc(ld_ao[i % 2], 16)
                    if i >= 2:
                        s.wait_ge(ve6, i - 1)
                    s.dma_start(out=xch[tt % 2],
                                in_=x_own[128 * tt:128 * (tt + 1), 1024 * oc:1024 * (oc + 1)]).then_inc(ld_x6[i % 2], 16)
            for tt in range(8):
                s.wait_ge(st_x2[0], 16 * 16)
                s.wait_ge(st_x2[1], 16 * 16)
                if tt < 2:
                    s.wait_ge(pe6, 32)
                    s.wait_ge(ve6, 32)
                if tt >= 2:
                    s.wait_ge(ve7, VE1_TT * (tt - 1))
                s.dma_start(out=x_io[tt % 2],
                            in_=x2_scr[128 * tt:128 * (tt + 1), :]).then_inc(ld_x27[tt % 2], 16)
            for c in range(4):
                if c == 0:
                    s.wait_ge(ve7b, 32)
                s.wait_ge(pe_bb, 4 + c)
                s.dma_start(out=rzrow[0:1, :],
                            in_=bf_row[0:1, 1024 * c:1024 * (c + 1)]).then_inc(ld_misc, 16)
            for rp in range(4):
                for tt in range(8):
                    i = 8 * rp + tt
                    if i < 2:
                        s.wait_ge(ve7, VE1_TT * 8)
                    if i >= 2:
                        s.wait_ge(ve8, i - 1)
                    s.dma_start(out=x2ch[tt % 2],
                                in_=x2_scr[128 * tt:128 * (tt + 1), 1024 * rp:1024 * (rp + 1)]).then_inc(ld_x28[i % 2], 16)
            s.wait_ge(st_out[0], 16 * 16)
            s.wait_ge(st_out[1], 16 * 16)

        @block.gpsimd
        def _(g):
            g.iota(iota_i, pattern=[[1, 128]], base=0, channel_multiplier=-1).then_inc(ge, 1)
            qw_r = qw_own.rearrange("pl (t r) w -> t r pl w", t=4, r=128)
            g.dma_start(out=qw_i32, in_=qw_r[0]).then_inc(ld_qw, 16)
            g.dma_start(out=lut_t, in_=lut_own[0:128, :]).then_inc(ld_qw, 16)
            for mt in range(32):
                if mt < 2:
                    g.wait_ge(ve1, VE1_TT * 8)
                    g.wait_ge(pe1, 32)
                if mt >= 2:
                    g.wait_ge(pe2a, mt - 1)
                g.dma_start(out=wcol[mt % 2], in_=wv4[mt]).then_inc(ld_wv[mt % 2], 16)
                if mt in (7, 15, 23):
                    ot = (mt + 1) // 8
                    g.wait_ge(ve_dq, ot * VE_DQ_OT)
                    g.dma_start(out=qw_i32, in_=qw_r[ot]).then_inc(ld_qw, 16)
                    g.dma_start(out=lut_t, in_=lut_own[128 * ot:128 * (ot + 1), :]).then_inc(ld_qw, 16)
            g.wait_ge(exp_v[0], 16 * 8 * 16)
            g.wait_ge(exp_v[1], 16 * 8 * 16)
            g.collective_compute("AllGather", AluOpType.bypass,
                                 replica_groups=[[0, 1], [2, 3], [4, 5], [6, 7]],
                                 ins=[vag_in.opt()], outs=[vag_out.opt()]).then_inc(cc_v)
            for mt in range(32):
                if mt < 2:
                    g.wait_ge(pe2a, 31 + mt)
                if mt >= 2:
                    g.wait_ge(pe2b, mt - 1)
                g.dma_start(out=wcol[mt % 2], in_=wk4[mt]).then_inc(ld_wk[mt % 2], 16)
                if mt == 19:
                    g.wait_ge(exp_k[0], 16 * 8)
                    g.wait_ge(exp_k[1], 16 * 8)
                    g.collective_compute("AllGather", AluOpType.bypass,
                                         replica_groups=[[0, 1], [2, 3], [4, 5], [6, 7]],
                                         ins=[kag_inA.opt()], outs=[kag_outA.opt()]).then_inc(cc_k1)
            g.wait_ge(exp_k[2], 16 * 8)
            g.wait_ge(exp_k[3], 16 * 8)
            g.collective_compute("AllGather", AluOpType.bypass,
                                 replica_groups=[[0, 1], [2, 3], [4, 5], [6, 7]],
                                 ins=[kag_inB.opt()], outs=[kag_outB.opt()]).then_inc(cc_k2)
            for mt in range(32):
                if mt < 2:
                    g.wait_ge(pe2b, 31 + mt)
                else:
                    g.wait_ge(se5q, mt - 1)
                g.dma_start(out=wcol[mt % 2], in_=wq4[mt]).then_inc(ld_wq[mt % 2], 16)
                if mt == 8:
                    g.wait_ge(exp_wl[0], 16 * 64)
                    g.wait_ge(exp_wl[1], 16 * 64)
                    g.collective_compute("AllGather", AluOpType.bypass,
                                         replica_groups=[[0, 1, 2, 3, 4, 5, 6, 7]],
                                         ins=[wlag_in.opt()], outs=[wlag_out.opt()]).then_inc(cc_wl)
            wor = wot.rearrange("(it p) f -> it p f", it=32)
            wbr = wbig.rearrange("p (it f) -> p it f", it=32)
            for oc in range(4):
                if oc == 0:
                    g.wait_ge(pe_avz, 16 * NH)
                    g.wait_ge(st_ao[0], 16 * 8 * NH // 2)
                    g.wait_ge(st_ao[1], 16 * 8 * NH // 2)
                else:
                    g.wait_ge(pe6, 8 * oc)
                for it in range(32):
                    g.dma_start(out=wbr[:, it, :],
                                in_=wor[it][:, 1024 * oc:1024 * (oc + 1)]).then_inc(ld_wo, 16)
            wlr = wlag_out.rearrange("(r it p) o -> r it p o", r=8, it=32)
            wbr2 = wbig.rearrange("p (it rb f) -> p it rb f", it=32, rb=2)
            for rp in range(4):
                if rp == 0:
                    g.wait_ge(cc_wl, 1)
                    g.wait_ge(pe6, 32)
                else:
                    g.wait_ge(pe8, 8 * rp)
                for rb in range(2):
                    for it in range(32):
                        g.dma_start(out=wbr2[:, it, rb, :],
                                    in_=wlr[2 * rp + rb, it]).then_inc(ld_wl, 16)

        @block.vector
        def _(v):
            v.wait_ge(ge, 1)
            v.tensor_scalar(out=ident, in0=iota_i, scalar1=0.0, scalar2=None,
                            op0=AluOpType.is_equal)
            v.tensor_scalar(out=identb, in0=iota_i, scalar1=0.0, scalar2=None,
                            op0=AluOpType.is_equal)
            v.memset(onesb[:, 0:1], 1.0)
            v.memset(onesr[0:1, :], 1.0)
            v.memset(eps_t, EPS)
            v.wait_ge(ld_misc, 64)
            v.tensor_scalar(out=bqs, in0=bqs, scalar1=SCALE, scalar2=None,
                            op0=AluOpType.mult).then_inc(ve0, 1)

            def ln_tile(tt, ld_sem, vsem, ssem, psem, x_src):
                v.wait_ge(ld_sem[tt % 2], 16 * (tt // 2 + 1))
                last = None
                for i in range(8):
                    last = v.bn_stats(bn6[:, 6 * i:6 * i + 6], x_src[:, 512 * i:512 * (i + 1)])
                last.then_inc(vsem, 1)
                v.wait_ge(vsem, VE1_TT * tt + 1)
                v.bn_aggr(mv, bn6).then_inc(vsem, 1)
                v.wait_ge(ssem, tt + 1)
                v.reciprocal(rstd, sq).then_inc(vsem, 1)
                v.wait_ge(vsem, VE1_TT * tt + 3)
                v.tensor_scalar(out=nmr, in0=mv[:, 0:1], scalar1=rstd, scalar2=-1.0,
                                op0=AluOpType.mult, op1=AluOpType.mult).then_inc(vsem, 1)
                v.wait_ge(vsem, VE1_TT * tt + 4)
                if tt >= 1:
                    v.wait_ge(psem, 4 * tt)
                v.tensor_scalar(out=xnf, in0=x_src, scalar1=rstd, scalar2=nmr,
                                op0=AluOpType.mult, op1=AluOpType.add).then_inc(vsem, 1)

            xnr = xn_dm.rearrange("p (it t) -> p it t", it=32)
            pdr = PD.rearrange("p (j t) -> p j t", j=8)
            for tt in range(8):
                ln_tile(tt, ld_x1, ve1, se1, pe1, x_io[tt % 2])
                for gq in range(4):
                    G = 4 * tt + gq
                    v.wait_ge(pe1, G + 1)
                    v.tensor_copy(xnr[:, 8 * gq:8 * gq + 8, 128 * tt:128 * (tt + 1)],
                                  pdr).then_inc(ve1b, 1)
            vcnt = 0
            for ot in range(4):
                v.wait_ge(ld_qw, 32 * (ot + 1))
                for pl in range(4):
                    last = None
                    for sh in range(8):
                        last = v.tensor_scalar(
                            out=bits[pl].rearrange("p (b sh) -> p sh b", sh=8)[:, sh, :],
                            in0=qw_u8[:, 512 * pl:512 * (pl + 1)],
                            scalar1=sh, scalar2=1,
                            op0=AluOpType.logical_shift_right,
                            op1=AluOpType.bitwise_and)
                    last.then_inc(ve_dq, 1); vcnt += 1
                v.wait_ge(ve_dq, vcnt)
                v.tensor_tensor(out=dlev,
                                in0=lut_t.rearrange("p (c two) -> p two c", two=2)[:, 1, :],
                                in1=lut_t.rearrange("p (c two) -> p two c", two=2)[:, 0, :],
                                op=AluOpType.subtract).then_inc(ve_dq, 1)
                vcnt += 1
                v.wait_ge(ve_dq, vcnt)
                for chk in range(NCHUNK):
                    C = NCHUNK * ot + chk
                    gidx = C // 2
                    sl = slice(CH * chk, CH * (chk + 1))
                    last = None
                    for c8 in range(8):
                        last = v.tensor_scalar(
                            out=tbuf[:, CH * c8:CH * (c8 + 1)], in0=bits[0][:, sl],
                            scalar1=dlev[:, c8:c8 + 1], scalar2=lut_t[:, 2 * c8:2 * c8 + 1],
                            op0=AluOpType.mult, op1=AluOpType.add)
                    last.then_inc(ve_dq, 1); vcnt += 1
                    v.wait_ge(ve_dq, vcnt)
                    for c4 in range(4):
                        last = v.select(ubuf[:, CH * c4:CH * (c4 + 1)], bits[1][:, sl],
                                        tbuf[:, CH * (2 * c4 + 1):CH * (2 * c4 + 2)],
                                        tbuf[:, CH * 2 * c4:CH * (2 * c4 + 1)], add_drain=True)
                    last.then_inc(ve_dq, 1); vcnt += 1
                    v.wait_ge(ve_dq, vcnt)
                    for c2 in range(2):
                        last = v.select(vbuf[:, CH * c2:CH * (c2 + 1)], bits[2][:, sl],
                                        ubuf[:, CH * (2 * c2 + 1):CH * (2 * c2 + 2)],
                                        ubuf[:, CH * 2 * c2:CH * (2 * c2 + 1)], add_drain=True)
                    last.then_inc(ve_dq, 1); vcnt += 1
                    v.wait_ge(ve_dq, vcnt)
                    if gidx >= 4:
                        v.wait_ge(pe_wt, gidx - 3)
                    v.select(wl_f2[gidx % 4][:, 512 * (C % 2):512 * (C % 2) + 512],
                             bits[3][:, sl], vbuf[:, CH:2 * CH], vbuf[:, 0:CH],
                             add_drain=True).then_inc(ve_dq, 1)
                    vcnt += 1
                    v.wait_ge(ve_dq, vcnt)
                    if C % 2 == 1 and gidx >= 2:
                        G = gidx - 2
                        v.wait_ge(pe_wt, G + 1)
                        if G >= 2:
                            v.wait_ge(exp_wl[G % 2], 16 * 8 * (G // 2))
                        v.tensor_copy(wlt_ev[G % 2], PD[:, 0:1024]).then_inc(ve_wt, 1)
                assert vcnt == (ot + 1) * VE_DQ_OT
            for G in (14, 15):
                v.wait_ge(pe_wt, G + 1)
                v.wait_ge(exp_wl[G % 2], 16 * 8 * (G // 2))
                v.tensor_copy(wlt_ev[G % 2], PD[:, 0:1024]).then_inc(ve_wt, 1)
            for H in range(NH):
                v.wait_ge(pe_avz, 16 * H + 8)
                v.reciprocal(rzrow[0:1, 0:512], PD[0:1, 0:512]).then_inc(ve_rz, 1)
                v.wait_ge(pe_avz, 16 * H + 16)
                v.reciprocal(rzrow[0:1, 512:1024], PD[0:1, 512:1024]).then_inc(ve_rz, 1)
                v.wait_ge(pe_rzb, 2 * H + 2)
                v.tensor_copy(rzb, PD[:, 0:1024]).then_inc(ve_fin, 1)
                v.wait_ge(ve_fin, 2 * H + 1)
                if H >= 2:
                    v.wait_ge(st_ao[H % 2], 16 * 8 * (H // 2))
                v.tensor_tensor(out=ao_sb[H % 2], in0=PC[:, 0:1024], in1=rzb,
                                op=AluOpType.mult).then_inc(ve_mid, 1)
                v.wait_ge(ve_mid, H + 1)
                v.tensor_scalar(out=ao_sb[H % 2], in0=ao_sb[H % 2],
                                scalar1=bv_sb[:, H:H + 1], scalar2=None,
                                op0=AluOpType.add).then_inc(ve_fin, 1)
            for c in range(4):
                v.wait_ge(pe_bb, c + 1)
                v.tensor_copy(biasB[:, 1024 * c:1024 * (c + 1)], PD[:, 0:1024]).then_inc(ve_bb, 1)
            for oc in range(4):
                for tt in range(8):
                    i = 8 * oc + tt
                    v.wait_ge(pe6, i + 1)
                    v.wait_ge(ld_x6[i % 2], 16 * (i // 2 + 1))
                    if i >= 2:
                        v.wait_ge(st_x2[i % 2], 16 * (i // 2))
                    ps = PA if tt % 2 == 0 else PB
                    v.tensor_tensor(out=x2o[tt % 2], in0=ps[:, 0:1024],
                                    in1=biasB[:, 1024 * oc:1024 * (oc + 1)],
                                    op=AluOpType.add).then_inc(ve_mid, 1)
                    v.wait_ge(ve_mid, 32 + i + 1)
                    v.tensor_tensor(out=x2o[tt % 2], in0=x2o[tt % 2], in1=xch[tt % 2],
                                    op=AluOpType.add).then_inc(ve6, 1)
            for tt in range(8):
                ln_tile(tt, ld_x27, ve7, se7, pe7, x_io[tt % 2])
                for gq in range(4):
                    G = 4 * tt + gq
                    v.wait_ge(pe7, G + 1)
                    v.tensor_copy(xnr[:, 8 * gq:8 * gq + 8, 128 * tt:128 * (tt + 1)],
                                  pdr).then_inc(ve7b, 1)
            for c in range(4):
                v.wait_ge(pe_bb, 4 + c + 1)
                v.tensor_copy(biasB[:, 1024 * c:1024 * (c + 1)], PD[:, 0:1024]).then_inc(ve_bb, 1)
            for rp in range(4):
                for tt in range(8):
                    i = 8 * rp + tt
                    v.wait_ge(pe8, i + 1)
                    v.wait_ge(ld_x28[i % 2], 16 * (i // 2 + 1))
                    if i >= 2:
                        v.wait_ge(st_out[i % 2], 16 * (i // 2))
                    ps = PA if tt % 2 == 0 else PB
                    v.tensor_tensor(out=outb[tt % 2], in0=ps[:, 0:1024],
                                    in1=biasB[:, 1024 * rp:1024 * (rp + 1)],
                                    op=AluOpType.add).then_inc(ve_mid, 1)
                    v.wait_ge(ve_mid, 64 + i + 1)
                    v.tensor_tensor(out=outb[tt % 2], in0=outb[tt % 2], in1=x2ch[tt % 2],
                                    op=AluOpType.add).then_inc(ve8, 1)

        @block.scalar
        def _(sc):
            sc.wait_ge(ve0, 1)
            for tt in range(8):
                sc.wait_ge(ve1, VE1_TT * tt + 2)
                sc.activation(sq, mv[:, 1:2], AF.Sqrt, bias=eps_t, scale=1.0).then_inc(se1, 1)
            for mt in range(32):
                sc.wait_ge(pe2a, mt + 1)
                if mt >= 2:
                    sc.wait_ge(pe_vt, mt - 1)
                ps = (PA, PB, PC)[mt % 3]
                sc.activation(xvf[mt % 2], ps[:, 0:1024], AF.Identity).then_inc(se2a, 1)
                sc.wait_ge(pe_vt, mt + 1)
                if mt >= 2:
                    sc.wait_ge(exp_v[mt % 2], 16 * 8 * (mt // 2))
                sc.activation(vexp[mt % 2], PD[:, 0:1024], AF.Identity).then_inc(se_vx, 1)
            for mt in range(32):
                sc.wait_ge(pe2b, mt + 1)
                if mt >= 2:
                    pm = mt - 2
                    sc.wait_ge(exp_k[2 * (pm // 16) + pm % 2], 16 * ((pm % 16) // 2 + 1))
                ps = (PA, PB, PC)[mt % 3]
                sc.activation(kexp[mt % 2], ps[:, 0:1024], AF.Identity,
                              bias=bk_sb[:, mt:mt + 1], scale=1.0).then_inc(se2b, 1)
            for hg in range(NHG):
                for mtl in range(4):
                    mt = 4 * hg + mtl
                    sc.wait_ge(pe2b, 32 + mt + 1)
                    if hg >= 2 and mtl == 0:
                        sc.wait_ge(pe_sc, 16 * 4 * (hg - 1))
                    ps = PA if mt % 2 == 0 else PB
                    sc.activation(Q_hg[hg % 2][:, 1024 * mtl:1024 * (mtl + 1)], ps[:, 0:1024],
                                  AF.Identity, bias=bqs[:, mt:mt + 1], scale=SCALE).then_inc(se5q, 1)
                for hl in range(4):
                    H = 4 * hg + hl
                    for kt in range(16):
                        if kt == 4 and H >= 1:
                            Hp = H - 1
                            sc.wait_ge(ve_fin, 2 * Hp + 2)
                            for tt in range(8):
                                sc.dma_start(out=ao_scr[tt, :, Hp, :],
                                             in_=ao_sb[Hp % 2][:, 128 * tt:128 * (tt + 1)]).then_inc(st_ao[Hp % 2], 16)
                        sc.wait_ge(pe_sc, 16 * H + kt + 1)
                        if H >= 1:
                            sc.wait_ge(pe_avz, 16 * (H - 1) + kt + 1)
                        slot = PA if kt % 2 == 0 else PB
                        q0 = q0_of(kt)
                        sc.activation(attT[:, ATO[kt]:ATO[kt] + ATW[kt]],
                                      slot[:, q0:1024], AF.Exp).then_inc(se_exp, 1)
            sc.wait_ge(ve_fin, 2 * NH)
            for tt in range(8):
                sc.dma_start(out=ao_scr[tt, :, NH - 1, :],
                             in_=ao_sb[(NH - 1) % 2][:, 128 * tt:128 * (tt + 1)]).then_inc(st_ao[(NH - 1) % 2], 16)
            for oc in range(4):
                for tt in range(8):
                    i = 8 * oc + tt
                    sc.wait_ge(ve6, i + 1)
                    sc.dma_start(out=x2_scr[128 * tt:128 * (tt + 1), 1024 * oc:1024 * (oc + 1)],
                                 in_=x2o[tt % 2]).then_inc(st_x2[i % 2], 16)
            for tt in range(8):
                sc.wait_ge(ve7, VE1_TT * tt + 2)
                sc.activation(sq, mv[:, 1:2], AF.Sqrt, bias=eps_t, scale=1.0).then_inc(se7, 1)
            for rp in range(4):
                for tt in range(8):
                    i = 8 * rp + tt
                    sc.wait_ge(ve8, i + 1)
                    sc.dma_start(out=out_own[128 * tt:128 * (tt + 1), 1024 * rp:1024 * (rp + 1)],
                                 in_=outb[tt % 2]).then_inc(st_out[i % 2], 16)
            sc.wait_ge(st_out[0], 16 * 16)
            sc.wait_ge(st_out[1], 16 * 16)

        @block.tensor
        def _(te):
            te.wait_ge(ve0, 1)
            for tt in range(8):
                te.wait_ge(ve1, VE1_TT * (tt + 1))
                for gq in range(4):
                    G = 4 * tt + gq
                    if G > 0:
                        te.wait_ge(ve1b, G)
                    last = None
                    for j in range(8):
                        d = 8 * gq + j
                        last = te.transpose(PD[:, 128 * j:128 * (j + 1)],
                                            xnf[:, 128 * d:128 * (d + 1)], ident)
                    last.then_inc(pe1, 1)
            prev_pd = None
            for mt in range(33):
                if mt < 32:
                    te.wait_ge(ld_wv[mt % 2], 16 * (mt // 2 + 1))
                    if mt == 0:
                        te.wait_ge(ve1b, 32)
                    if mt >= 3:
                        te.wait_ge(se2a, mt - 2)
                    ps = (PA, PB, PC)[mt % 3]
                    last = None
                    for it in range(32):
                        for (c0, c1) in ((0, 512), (512, 1024)):
                            last = te.matmul(ps[:, c0:c1], wcol[mt % 2][:, 128 * it:128 * (it + 1)],
                                             xn_dm[:, 1024 * it + c0:1024 * it + c1],
                                             start=(it == 0), stop=(it == 31))
                    last.then_inc(pe2a, 1)
                if mt >= 1:
                    mp = mt - 1
                    te.wait_ge(se2a, mp + 1)
                    if prev_pd is not None:
                        te.wait_ge(*prev_pd)
                    last = None
                    for j in range(8):
                        last = te.transpose(PD[:, 128 * j:128 * (j + 1)],
                                            xvf[mp % 2][:, 128 * j:128 * (j + 1)], ident)
                    last.then_inc(pe_vt, 1)
                    prev_pd = (se_vx, mp + 1)
                    if mp % 2 == 1 and mp // 2 < 16:
                        G = mp // 2
                        te.wait_ge(ve_dq, dq_chunk_done(G // 4, 2 * (G % 4) + 1))
                        te.wait_ge(*prev_pd)
                        last = None
                        for j in range(8):
                            last = te.transpose(PD[:, 128 * j:128 * (j + 1)],
                                                wl_f2[G % 4][:, 128 * j:128 * (j + 1)], ident)
                        last.then_inc(pe_wt, 1)
                        prev_pd = (ve_wt, G + 1)
            for mt in range(32):
                te.wait_ge(ld_wk[mt % 2], 16 * (mt // 2 + 1))
                if mt < 3:
                    te.wait_ge(se2a, (31, 32, 30)[mt])
                if mt >= 3:
                    te.wait_ge(se2b, mt - 2)
                ps = (PA, PB, PC)[mt % 3]
                last = None
                for it in range(32):
                    for (c0, c1) in ((0, 512), (512, 1024)):
                        last = te.matmul(ps[:, c0:c1], wcol[mt % 2][:, 128 * it:128 * (it + 1)],
                                         xn_dm[:, 1024 * it + c0:1024 * it + c1],
                                         start=(it == 0), stop=(it == 31))
                last.then_inc(pe2b, 1)
            for hg in range(NHG):
                for mtl in range(4):
                    mt = 4 * hg + mtl
                    te.wait_ge(ld_wq[mt % 2], 16 * (mt // 2 + 1))
                    if hg == 0 and mtl < 2:
                        te.wait_ge(se2b, 31 + mtl)
                        te.wait_ge(ve_wt, 16)
                    if hg >= 1 and mtl < 2:
                        te.wait_ge(se_exp, 16 * 4 * hg)
                    if mtl >= 2:
                        te.wait_ge(se5q, 4 * hg + mtl - 1)
                    ps = PA if mt % 2 == 0 else PB
                    last = None
                    for it in range(32):
                        for (c0, c1) in ((0, 512), (512, 1024)):
                            last = te.matmul(ps[:, c0:c1], wcol[mt % 2][:, 128 * it:128 * (it + 1)],
                                             xn_dm[:, 1024 * it + c0:1024 * it + c1],
                                             start=(it == 0), stop=(it == 31))
                    last.then_inc(pe2b, 1)
                def mk_avz(H):
                    def avz_pair(s_):
                        if s_ == 0:
                            if H >= 1:
                                te.wait_ge(ve_mid, H)
                            else:
                                te.wait_ge(ve_wt, 16)
                        for kt in (2 * s_, 2 * s_ + 1):
                            te.wait_ge(se_exp, 16 * H + 2 * s_ + 2)
                            lastm = None
                            for (c0, c1) in q_ranges(kt):
                                stop = (kt == 7) if c1 <= 512 else (kt == 15)
                                lastm = te.matmul(
                                    PC[:, c0:c1], v_sb[H % 2][:, 128 * kt:128 * (kt + 1)],
                                    attT[:, ATO[kt] + c0 - q0_of(kt):ATO[kt] + c1 - q0_of(kt)],
                                    start=(kt == 0), stop=stop)
                            for (c0, c1) in q_ranges(kt):
                                stop = (kt == 7) if c1 <= 512 else (kt == 15)
                                lastm = te.matmul(
                                    PD[0:1, c0:c1], onesb[:, 0:1],
                                    attT[:, ATO[kt] + c0 - q0_of(kt):ATO[kt] + c1 - q0_of(kt)],
                                    start=(kt == 0), stop=stop)
                            lastm.then_inc(pe_avz, 1)
                    return avz_pair

                def rz_bcast(Hp, h2):
                        te.wait_ge(ve_rz, 2 * Hp + h2 + 1)
                        te.matmul(PD[:, 512 * h2:512 * (h2 + 1)], onesr[0:1, :],
                                  rzrow[0:1, 512 * h2:512 * (h2 + 1)],
                                  start=True, stop=True).then_inc(pe_rzb, 1)

                for hl in range(4):
                    H = 4 * hg + hl
                    te.wait_ge(ld_kv[H % 2], 16 * 8 * (H // 2 + 1))
                    avz_pair = mk_avz(H)
                    for s_ in range(8):
                        for kt in (2 * s_, 2 * s_ + 1):
                            tgt = 16 * H + kt - 1
                            if tgt > 0:
                                te.wait_ge(se_exp, tgt)
                            if kt == 0:
                                te.wait_ge(se5q, 4 * (hg + 1))
                            slot = PA if kt % 2 == 0 else PB
                            s_of = kt // 2
                            j_of = kt % 2
                            last = None
                            for (c0, c1) in q_ranges(kt):
                                has_mask = (c0 == q0_of(kt))
                                last = te.matmul(slot[:, c0:c1],
                                                 k_sb[H % 2][:, 128 * kt:128 * (kt + 1)],
                                                 Q_hg[hg % 2][:, 1024 * hl + c0:1024 * hl + c1],
                                                 start=True, stop=not has_mask)
                                if has_mask:
                                    last = te.matmul(
                                        slot[:, c0:c0 + 128], identb,
                                        masks[:, 256 * s_of + 128 * j_of:256 * s_of + 128 * (j_of + 1)],
                                        start=False, stop=True)
                            last.then_inc(pe_sc, 1)
                        if s_ == 0 and H >= 1:
                            mk_avz(H - 1)(7)
                        if s_ == 1 and H >= 1:
                            rz_bcast(H - 1, 0)
                        if s_ == 2 and H >= 1:
                            rz_bcast(H - 1, 1)
                        if s_ >= 2:
                            avz_pair(s_ - 2)
                    avz_pair(6)
                    if H == NH - 1:
                        avz_pair(7)
                        rz_bcast(H, 0)
                        rz_bcast(H, 1)
            for c in range(4):
                if c == 0:
                    te.wait_ge(ve_fin, 2 * NH)
                te.wait_ge(ld_misc, 64 + 16 * (c + 1))
                if c > 0:
                    te.wait_ge(ve_bb, c)
                last = None
                for h2 in range(2):
                    last = te.matmul(PD[:, 512 * h2:512 * (h2 + 1)], onesr[0:1, :],
                                     rzrow[0:1, 512 * h2:512 * (h2 + 1)], start=True, stop=True)
                last.then_inc(pe_bb, 1)
            for oc in range(4):
                te.wait_ge(ld_wo, 16 * 32 * (oc + 1))
                if oc == 0:
                    te.wait_ge(ve_bb, 4)
                for tt in range(8):
                    i = 8 * oc + tt
                    te.wait_ge(ld_ao[i % 2], 16 * (i // 2 + 1))
                    if i < 2:
                        te.wait_ge(se_exp, 16 * NH)
                    if i >= 2:
                        te.wait_ge(ve6, i - 1)
                    ps = PA if tt % 2 == 0 else PB
                    last = None
                    for it in range(32):
                        for (c0, c1) in ((0, 512), (512, 1024)):
                            last = te.matmul(ps[:, c0:c1], aost[tt % 2][:, 128 * it:128 * (it + 1)],
                                             wbig[:, 1024 * it + c0:1024 * it + c1],
                                             start=(it == 0), stop=(it == 31))
                    last.then_inc(pe6, 1)
            for tt in range(8):
                te.wait_ge(ve7, VE1_TT * (tt + 1))
                for gq in range(4):
                    G = 4 * tt + gq
                    if G > 0:
                        te.wait_ge(ve7b, G)
                    else:
                        te.wait_ge(ve_bb, 4)
                    last = None
                    for j in range(8):
                        d = 8 * gq + j
                        last = te.transpose(PD[:, 128 * j:128 * (j + 1)],
                                            xnf[:, 128 * d:128 * (d + 1)], ident)
                    last.then_inc(pe7, 1)
            for c in range(4):
                if c == 0:
                    te.wait_ge(ve7b, 32)
                te.wait_ge(ld_misc, 128 + 16 * (c + 1))
                if c > 0:
                    te.wait_ge(ve_bb, 4 + c)
                last = None
                for h2 in range(2):
                    last = te.matmul(PD[:, 512 * h2:512 * (h2 + 1)], onesr[0:1, :],
                                     rzrow[0:1, 512 * h2:512 * (h2 + 1)], start=True, stop=True)
                last.then_inc(pe_bb, 1)
            wbr2t = wbig.rearrange("p (it rb f) -> p it rb f", it=32, rb=2)
            for rp in range(4):
                te.wait_ge(ld_wl, 16 * 64 * (rp + 1))
                if rp == 0:
                    te.wait_ge(ve_bb, 8)
                    te.wait_ge(ve6, 32)
                for tt in range(8):
                    i = 8 * rp + tt
                    ps = PA if tt % 2 == 0 else PB
                    if i >= 2:
                        te.wait_ge(ve8, i - 1)
                    last = None
                    for it in range(32):
                        for rb in range(2):
                            last = te.matmul(
                                ps[:, 512 * rb:512 * (rb + 1)],
                                xn_dm[:, 1024 * it + 128 * tt:1024 * it + 128 * (tt + 1)],
                                wbr2t[:, it, rb, :],
                                start=(it == 0), stop=(it == 31))
                    last.then_inc(pe8, 1)
    return nc


def kernel(L0_attn_x, L0_attn_Wq, L0_attn_bq, L0_attn_Wk, L0_attn_bk,
           L0_attn_Wv, L0_attn_bv, L0_attn_Wo, L0_attn_bo,
           L0_ffn_qweight, L0_ffn_lut4, L0_ffn_lut8, L0_ffn_bias):
    x = np.asarray(L0_attn_x, dtype=np.float32)
    Wq = np.asarray(L0_attn_Wq, dtype=np.float32)
    Wk = np.asarray(L0_attn_Wk, dtype=np.float32)
    Wv = np.asarray(L0_attn_Wv, dtype=np.float32)
    Wo = np.asarray(L0_attn_Wo, dtype=np.float32)
    bq = np.asarray(L0_attn_bq, dtype=np.float32)
    bk = np.asarray(L0_attn_bk, dtype=np.float32)
    bv = np.asarray(L0_attn_bv, dtype=np.float32)
    bo = np.asarray(L0_attn_bo, dtype=np.float32)
    qw = np.asarray(L0_ffn_qweight)
    lut4 = np.asarray(L0_ffn_lut4, dtype=np.float32)
    bfb = np.asarray(L0_ffn_bias, dtype=np.float32)

    wq4 = np.ascontiguousarray(Wq.reshape(32, 128, 32, 128).transpose(0, 3, 2, 1))
    wk4 = np.ascontiguousarray(Wk.reshape(32, 128, 32, 128).transpose(0, 3, 2, 1))
    wv4 = np.ascontiguousarray(Wv.reshape(32, 128, 32, 128).transpose(0, 3, 2, 1))
    wot = np.ascontiguousarray(Wo.T)
    bq_t = np.ascontiguousarray(bq.reshape(32, 128).T)
    bk_t = np.ascontiguousarray(bk.reshape(32, 128).T)
    bv_t = np.ascontiguousarray(bv.reshape(32, 128).T)
    bo_row = np.ascontiguousarray(bo.reshape(1, 4096))
    bf_row = np.ascontiguousarray(bfb.reshape(1, 4096))

    def build_mask(parity):
        QT = QT_EVEN if parity == 0 else QT_ODD
        m = np.zeros((128, 2048), np.float32)
        for s in range(8):
            qt = QT[s]
            for j in range(2):
                kt = 2 * s + j
                if kt < qt:
                    t = np.zeros((128, 128), np.float32)
                elif kt == qt:
                    t = np.where(np.arange(128)[:, None] <= np.arange(128)[None, :],
                                 0.0, -30000.0).astype(np.float32)
                else:
                    t = np.full((128, 128), -30000.0, np.float32)
                m[:, 256 * s + 128 * j:256 * s + 128 * (j + 1)] = t
        return m.astype(BF)

    mask_by_par = [build_mask(0), build_mask(1)]

    nc = build_nc()
    in_maps = []
    for core in range(8):
        b = core // 2
        parity = core % 2
        QT = QT_EVEN if parity == 0 else QT_ODD
        rows = np.concatenate([128 * qt + np.arange(128) for qt in QT])
        in_maps.append({
            "x_own": np.ascontiguousarray(x[b][rows]),
            "wq4": wq4, "wk4": wk4, "wv4": wv4, "wot": wot,
            "bq_t": bq_t, "bk_t": bk_t, "bv_t": bv_t,
            "bo_row": bo_row, "bf_row": bf_row,
            "qw_own": np.ascontiguousarray(qw[0:4, 512 * core:512 * (core + 1), :]),
            "lut_own": np.ascontiguousarray(lut4[512 * core:512 * (core + 1), :]),
            "masks_in": mask_by_par[parity],
        })
    res = run_bass_kernel_spmd(nc, in_maps, core_ids=list(range(8)))
    out = np.empty((4, 2048, 4096), dtype=np.float32)
    for core in range(8):
        b = core // 2
        parity = core % 2
        QT = QT_EVEN if parity == 0 else QT_ODD
        rows = np.concatenate([128 * qt + np.arange(128) for qt in QT])
        out[b][rows] = res.results[core]["out_own"]
    return out


# version 13
# speedup vs baseline: 1.0682x; 1.0066x over previous
import contextlib
import math
import numpy as np
import ml_dtypes
import concourse.bass as bass
import concourse.mybir as mybir
from concourse.bass_utils import run_bass_kernel_spmd
from concourse.alu_op_type import AluOpType

F32 = mybir.dt.float32
BF16 = mybir.dt.bfloat16
I32 = mybir.dt.int32
U8 = mybir.dt.uint8
AF = mybir.ActivationFunctionType
BF = ml_dtypes.bfloat16

QT_EVEN = [0, 3, 4, 7, 8, 11, 12, 15]
QT_ODD = [1, 2, 5, 6, 9, 10, 13, 14]
SCALE = 1.0 / math.sqrt(128.0)
EPS = 1e-5
NH = 32
NHG = 8
CH = 512
NCHUNK = 4096 // CH
VE_DQ_OT = 4 + 1 + NCHUNK * 4
VE1_TT = 5

def q0_of(kt):
    return 128 * (kt // 2)
ATW = [1024 - q0_of(kt) for kt in range(16)]
ATO = [0] * 16
for _kt in range(1, 16):
    ATO[_kt] = ATO[_kt - 1] + ATW[_kt - 1]
ATT_TOT = ATO[15] + ATW[15]

def q_ranges(kt):
    q0 = q0_of(kt)
    return [(q0, 512), (512, 1024)] if q0 < 512 else [(q0, 1024)]


def build_nc():
    nc = bass.Bass("TRN2", target_bir_lowering=False)
    x_own = nc.dram_tensor("x_own", [1024, 4096], F32, kind="ExternalInput").ap()
    wq4 = nc.dram_tensor("wq4", [32, 128, 32, 128], F32, kind="ExternalInput").ap()
    wk4 = nc.dram_tensor("wk4", [32, 128, 32, 128], F32, kind="ExternalInput").ap()
    wv4 = nc.dram_tensor("wv4", [32, 128, 32, 128], F32, kind="ExternalInput").ap()
    wot = nc.dram_tensor("wot", [4096, 4096], F32, kind="ExternalInput").ap()
    bq_t = nc.dram_tensor("bq_t", [128, 32], F32, kind="ExternalInput").ap()
    bk_t = nc.dram_tensor("bk_t", [128, 32], F32, kind="ExternalInput").ap()
    bv_t = nc.dram_tensor("bv_t", [128, 32], F32, kind="ExternalInput").ap()
    bo_row = nc.dram_tensor("bo_row", [1, 4096], F32, kind="ExternalInput").ap()
    bf_row = nc.dram_tensor("bf_row", [1, 4096], F32, kind="ExternalInput").ap()
    qw_own = nc.dram_tensor("qw_own", [4, 512, 128], I32, kind="ExternalInput").ap()
    lut_own = nc.dram_tensor("lut_own", [512, 16], F32, kind="ExternalInput").ap()
    masks_in = nc.dram_tensor("masks_in", [128, 2048], BF16, kind="ExternalInput").ap()
    out_own = nc.dram_tensor("out_own", [1024, 4096], F32, kind="ExternalOutput").ap()
    kag_inA = nc.dram_tensor("kag_inA", [2048, 1024], BF16).ap()
    kag_inB = nc.dram_tensor("kag_inB", [2048, 1024], BF16).ap()
    kag_outA = nc.dram_tensor("kag_outA", [4096, 1024], BF16).ap()
    kag_outB = nc.dram_tensor("kag_outB", [4096, 1024], BF16).ap()
    vag_in = nc.dram_tensor("vag_in", [1024, 4096], BF16).ap()
    vag_out = nc.dram_tensor("vag_out", [2048, 4096], BF16).ap()
    wlag_in = nc.dram_tensor("wlag_in", [4096, 512], BF16).ap()
    wlag_out = nc.dram_tensor("wlag_out", [32768, 512], BF16).ap()
    ao_scr = nc.dram_tensor("ao_scr", [8, 128, 32, 128], BF16).ap()
    x2_scr = nc.dram_tensor("x2_scr", [1024, 4096], F32).ap()

    es = contextlib.ExitStack()
    with es:
        AR = es.enter_context(nc.sbuf_tensor("AR", [128, 204 * 1024], U8))
        arf, arb, ari, aru = AR.bitcast(F32), AR.bitcast(BF16), AR.bitcast(I32), AR.bitcast(U8)

        def fsl(o, n):
            assert o % 64 == 0
            return arf.ap()[:, o // 4:o // 4 + n]

        def bsl(o, n):
            assert o % 64 == 0
            return arb.ap()[:, o // 2:o // 2 + n]

        def isl(o, n):
            assert o % 64 == 0
            return ari.ap()[:, o // 4:o // 4 + n]

        def usl(o, n):
            assert o % 64 == 0
            return aru.ap()[:, o:o + n]

        masks = bsl(0, 2048)
        biasB = bsl(4096, 4096)
        rzb = fsl(12288, 1024)
        rzrow = fsl(16384, 1024)
        ident = fsl(20480, 128)
        identb = bsl(23040, 128)
        iota_i = isl(20992, 128)
        onesr = fsl(21504, 128)
        onesb = bsl(22016, 32)
        bqs = fsl(22080, 32)
        bk_sb = fsl(22208, 32)
        bv_sb = fsl(22336, 32)
        bn6 = fsl(22464, 48)
        mv = fsl(22656, 2)
        sq = fsl(22720, 1)
        rstd = fsl(22784, 1)
        nmr = fsl(22848, 1)
        eps_t = fsl(22912, 1)
        S1 = 24 * 1024
        xn_dm = bsl(S1, 32768)
        S2 = 88 * 1024
        bits = [usl(S2 + 4096 * i, 4096) for i in range(4)]
        qw_i32 = isl(S2 + 16384, 512)
        qw_u8 = usl(S2 + 16384, 2048)
        lut_t = fsl(S2 + 18432, 16)
        dlev = fsl(S2 + 18496, 8)
        tbuf = fsl(S2 + 18560, 8 * CH)
        ubuf = fsl(S2 + 34944, 4 * CH)
        vbuf = fsl(S2 + 43136, 2 * CH)
        wl_f2 = [fsl(S2 + 47232 + 4096 * i, 1024) for i in range(4)]
        wlt_ev = [bsl(4096 + 2048 * i, 1024) for i in range(2)]
        attT = bsl(S2, ATT_TOT)
        Q_hg = [bsl(S2 + 18432 + 8192 * i, 4096) for i in range(2)]
        k_sb = [bsl(S2 + 34816 + 4096 * i, 2048) for i in range(2)]
        v_sb = [bsl(S2 + 43008 + 4096 * i, 2048) for i in range(2)]
        ao_sb = [bsl(S2 + 51200 + 2048 * i, 1024) for i in range(2)]
        wbig = bsl(S2, 32768)
        R3 = 152 * 1024
        vexp = [bsl(R3 + 24576 + 2048 * i, 1024) for i in range(2)]
        xvf = [fsl(R3 + 16384 + 4096 * i, 1024) for i in range(2)]
        x_io = [fsl(R3 + 16384 * i, 4096) for i in range(2)]
        xnf = fsl(R3 + 32768, 4096)
        wcol = [bsl(R3 + 8192 * i, 4096) for i in range(2)]
        kexp = [bsl(R3 + 16384 + 2048 * i, 1024) for i in range(2)]
        aost = [bsl(R3 + 8192 * i, 4096) for i in range(2)]
        xch = [fsl(R3 + 16384 + 4096 * i, 1024) for i in range(2)]
        x2o = [fsl(R3 + 24576 + 4096 * i, 1024) for i in range(2)]
        x2ch = [fsl(R3 + 4096 * i, 1024) for i in range(2)]
        outb = [fsl(R3 + 8192 + 4096 * i, 1024) for i in range(2)]

        PA = es.enter_context(nc.psum_tensor("PA", [128, 1024], F32))
        PB = es.enter_context(nc.psum_tensor("PB", [128, 1024], F32))
        PC = es.enter_context(nc.psum_tensor("PC", [128, 1024], F32))
        PD = es.enter_context(nc.psum_tensor("PD", [128, 1024], F32))

        def sem(name):
            return es.enter_context(nc.semaphore(name))
        ld_misc = sem("ld_misc"); ld_qw = sem("ld_qw")
        ld_x1 = [sem("ld_x1a"), sem("ld_x1b")]
        ld_wv = [sem("ld_wva"), sem("ld_wvb")]
        ld_wk = [sem("ld_wka"), sem("ld_wkb")]
        ld_wq = [sem("ld_wqa"), sem("ld_wqb")]
        ld_wo = sem("ld_wo"); ld_wl = sem("ld_wl")
        ld_kv = [sem("ld_kva"), sem("ld_kvb")]
        ld_ao = [sem("ld_aoa"), sem("ld_aob")]
        ld_x6 = [sem("ld_x6a"), sem("ld_x6b")]
        ld_x27 = [sem("ld_x27a"), sem("ld_x27b")]
        ld_x28 = [sem("ld_x28a"), sem("ld_x28b")]
        exp_v = [sem("exp_va"), sem("exp_vb")]
        exp_k = [sem("exp_ka"), sem("exp_kb"), sem("exp_kc"), sem("exp_kd")]
        exp_wl = [sem("exp_wla"), sem("exp_wlb")]
        cc_k1 = sem("cc_k1"); cc_k2 = sem("cc_k2"); cc_v = sem("cc_v"); cc_wl = sem("cc_wl")
        st_ao = [sem("st_aoa"), sem("st_aob")]
        st_x2 = [sem("st_x2a"), sem("st_x2b")]
        st_out = [sem("st_outa"), sem("st_outb")]
        ge = sem("ge"); ve0 = sem("ve0")
        ve1 = sem("ve1"); se1 = sem("se1"); pe1 = sem("pe1"); ve1b = sem("ve1b")
        pe_bb = sem("pe_bb"); ve_bb = sem("ve_bb")
        pe2a = sem("pe2a"); se2a = sem("se2a"); pe_vt = sem("pe_vt"); se_vx = sem("se_vx")
        pe2b = sem("pe2b"); se2b = sem("se2b")
        ve_dq = sem("ve_dq"); pe_wt = sem("pe_wt"); ve_wt = sem("ve_wt")
        pe_sc = sem("pe_sc"); pe_avz = sem("pe_avz"); pe_rzb = sem("pe_rzb")
        se_exp = sem("se_exp"); se5q = sem("se5q")
        ve_mask = sem("ve_mask"); ve_rz = sem("ve_rz"); ve_fin = sem("ve_fin")
        pe6 = sem("pe6"); ve6 = sem("ve6")
        ve7 = sem("ve7"); se7 = sem("se7"); pe7 = sem("pe7"); ve7b = sem("ve7b")
        pe8 = sem("pe8"); ve8 = sem("ve8"); ve_mid = sem("ve_mid")

        block = es.enter_context(nc.Block())

        def dq_chunk_done(ot, c_local):
            return ot * VE_DQ_OT + 5 + 4 * (c_local + 1)

        @block.sync
        def _(s):
            s.dma_start(out=masks, in_=masks_in).then_inc(ld_misc, 16)
            s.dma_start(out=bqs, in_=bq_t).then_inc(ld_misc, 16)
            s.dma_start(out=bk_sb, in_=bk_t).then_inc(ld_misc, 16)
            s.dma_start(out=bv_sb, in_=bv_t).then_inc(ld_misc, 16)
            for tt in range(8):
                if tt >= 2:
                    s.wait_ge(ve1, VE1_TT * (tt - 1))
                s.dma_start(out=x_io[tt % 2],
                            in_=x_own[128 * tt:128 * (tt + 1), :]).then_inc(ld_x1[tt % 2], 16)
            for mt in range(32):
                s.wait_ge(se_vx, mt + 1)
                for tt in range(8):
                    s.dma_start(out=vag_in[128 * tt:128 * (tt + 1), 128 * mt:128 * (mt + 1)],
                                in_=vexp[mt % 2][:, 128 * tt:128 * (tt + 1)]).then_inc(exp_v[mt % 2], 16)
                if mt % 2 == 1:
                    G = mt // 2
                    s.wait_ge(ve_wt, G + 1)
                    ot, gq = G // 4, G % 4
                    for j in range(8):
                        it = 8 * gq + j
                        s.dma_start(out=wlag_in[128 * it:128 * (it + 1), 128 * ot:128 * (ot + 1)],
                                    in_=wlt_ev[G % 2][:, 128 * j:128 * (j + 1)]).then_inc(exp_wl[G % 2], 16)
            for mt in range(32):
                s.wait_ge(se2b, mt + 1)
                half = mt // 16
                mloc = mt % 16
                kag = kag_inA if half == 0 else kag_inB
                s.dma_start(out=kag[128 * mloc:128 * (mloc + 1), :],
                            in_=kexp[mt % 2]).then_inc(exp_k[2 * half + mt % 2], 16)
            kr2A = kag_outA.rearrange("(r f) (j w t) -> r f j w t", r=2, j=4, w=2, t=128)
            kr2B = kag_outB.rearrange("(r f) (j w t) -> r f j w t", r=2, j=4, w=2, t=128)
            vr = vag_out.rearrange("(r j w p) f -> r p j w f", r=2, j=4, w=2, p=128)
            for H in range(NH):
                par = H % 2
                if H == 0:
                    s.wait_ge(cc_k1, 1)
                    s.wait_ge(cc_v, 1)
                if H == 16:
                    s.wait_ge(cc_k2, 1)
                if H < 2:
                    s.wait_ge(ve_dq, 4 * VE_DQ_OT)
                    s.wait_ge(ve_wt, 16)
                if H >= 2:
                    s.wait_ge(pe_sc, 16 * (H - 1))
                    s.wait_ge(pe_avz, 16 * (H - 1))
                for rp in range(2):
                    for w in range(2):
                        goff = (384 * w) if rp == 0 else (128 + 128 * w)
                        dk = k_sb[par].rearrange("p (j rest) -> p j rest", j=4)[:, :, goff:goff + 128]
                        kr2h = kr2A if H < 16 else kr2B
                        Hl = H % 16
                        s.dma_start(out=dk, in_=kr2h[rp, 128 * Hl:128 * (Hl + 1), :, w, :]).then_inc(ld_kv[par], 16)
                        dv = v_sb[par].rearrange("p (j rest) -> p j rest", j=4)[:, :, goff:goff + 128]
                        s.dma_start(out=dv, in_=vr[rp, :, :, w, 128 * H:128 * (H + 1)]).then_inc(ld_kv[par], 16)
            for c in range(4):
                if c == 0:
                    s.wait_ge(ve_fin, 3 * NH)
                if c > 0:
                    s.wait_ge(pe_bb, c)
                s.dma_start(out=rzrow[0:1, :],
                            in_=bo_row[0:1, 1024 * c:1024 * (c + 1)]).then_inc(ld_misc, 16)
            for oc in range(4):
                for tt in range(8):
                    i = 8 * oc + tt
                    if i == 0:
                        s.wait_ge(st_ao[0], 16 * 8 * NH // 2)
                        s.wait_ge(st_ao[1], 16 * 8 * NH // 2)
                        s.wait_ge(se5q, 4 * NHG)
                    if i >= 2:
                        s.wait_ge(pe6, i - 1)
                    s.dma_start(out=aost[tt % 2],
                                in_=ao_scr[tt].rearrange("p it t -> p (it t)")).then_inc(ld_ao[i % 2], 16)
                    if i >= 2:
                        s.wait_ge(ve6, i - 1)
                    s.dma_start(out=xch[tt % 2],
                                in_=x_own[128 * tt:128 * (tt + 1), 1024 * oc:1024 * (oc + 1)]).then_inc(ld_x6[i % 2], 16)
            for tt in range(8):
                s.wait_ge(st_x2[0], 16 * 16)
                s.wait_ge(st_x2[1], 16 * 16)
                if tt < 2:
                    s.wait_ge(pe6, 32)
                    s.wait_ge(ve6, 32)
                if tt >= 2:
                    s.wait_ge(ve7, VE1_TT * (tt - 1))
                s.dma_start(out=x_io[tt % 2],
                            in_=x2_scr[128 * tt:128 * (tt + 1), :]).then_inc(ld_x27[tt % 2], 16)
            for c in range(4):
                if c == 0:
                    s.wait_ge(ve7b, 32)
                s.wait_ge(pe_bb, 4 + c)
                s.dma_start(out=rzrow[0:1, :],
                            in_=bf_row[0:1, 1024 * c:1024 * (c + 1)]).then_inc(ld_misc, 16)
            for rp in range(4):
                for tt in range(8):
                    i = 8 * rp + tt
                    if i < 2:
                        s.wait_ge(ve7, VE1_TT * 8)
                    if i >= 2:
                        s.wait_ge(ve8, i - 1)
                    s.dma_start(out=x2ch[tt % 2],
                                in_=x2_scr[128 * tt:128 * (tt + 1), 1024 * rp:1024 * (rp + 1)]).then_inc(ld_x28[i % 2], 16)
            s.wait_ge(st_out[0], 16 * 16)
            s.wait_ge(st_out[1], 16 * 16)

        @block.gpsimd
        def _(g):
            g.iota(iota_i, pattern=[[1, 128]], base=0, channel_multiplier=-1).then_inc(ge, 1)
            qw_r = qw_own.rearrange("pl (t r) w -> t r pl w", t=4, r=128)
            g.dma_start(out=qw_i32, in_=qw_r[0]).then_inc(ld_qw, 16)
            g.dma_start(out=lut_t, in_=lut_own[0:128, :]).then_inc(ld_qw, 16)
            for mt in range(32):
                if mt < 2:
                    g.wait_ge(ve1, VE1_TT * 8)
                    g.wait_ge(pe1, 32)
                if mt >= 2:
                    g.wait_ge(pe2a, mt - 1)
                g.dma_start(out=wcol[mt % 2], in_=wv4[mt]).then_inc(ld_wv[mt % 2], 16)
                if mt in (7, 15, 23):
                    ot = (mt + 1) // 8
                    g.wait_ge(ve_dq, ot * VE_DQ_OT)
                    g.dma_start(out=qw_i32, in_=qw_r[ot]).then_inc(ld_qw, 16)
                    g.dma_start(out=lut_t, in_=lut_own[128 * ot:128 * (ot + 1), :]).then_inc(ld_qw, 16)
            g.wait_ge(exp_v[0], 16 * 8 * 16)
            g.wait_ge(exp_v[1], 16 * 8 * 16)
            g.collective_compute("AllGather", AluOpType.bypass,
                                 replica_groups=[[0, 1], [2, 3], [4, 5], [6, 7]],
                                 ins=[vag_in.opt()], outs=[vag_out.opt()]).then_inc(cc_v)
            for mt in range(32):
                if mt < 2:
                    g.wait_ge(pe2a, 31 + mt)
                if mt >= 2:
                    g.wait_ge(pe2b, mt - 1)
                g.dma_start(out=wcol[mt % 2], in_=wk4[mt]).then_inc(ld_wk[mt % 2], 16)
                if mt == 19:
                    g.wait_ge(exp_k[0], 16 * 8)
                    g.wait_ge(exp_k[1], 16 * 8)
                    g.collective_compute("AllGather", AluOpType.bypass,
                                         replica_groups=[[0, 1], [2, 3], [4, 5], [6, 7]],
                                         ins=[kag_inA.opt()], outs=[kag_outA.opt()]).then_inc(cc_k1)
            g.wait_ge(exp_k[2], 16 * 8)
            g.wait_ge(exp_k[3], 16 * 8)
            g.collective_compute("AllGather", AluOpType.bypass,
                                 replica_groups=[[0, 1], [2, 3], [4, 5], [6, 7]],
                                 ins=[kag_inB.opt()], outs=[kag_outB.opt()]).then_inc(cc_k2)
            for mt in range(32):
                if mt < 2:
                    g.wait_ge(pe2b, 31 + mt)
                else:
                    g.wait_ge(se5q, mt - 1)
                g.dma_start(out=wcol[mt % 2], in_=wq4[mt]).then_inc(ld_wq[mt % 2], 16)
                if mt == 8:
                    g.wait_ge(exp_wl[0], 16 * 64)
                    g.wait_ge(exp_wl[1], 16 * 64)
                    g.collective_compute("AllGather", AluOpType.bypass,
                                         replica_groups=[[0, 1, 2, 3, 4, 5, 6, 7]],
                                         ins=[wlag_in.opt()], outs=[wlag_out.opt()]).then_inc(cc_wl)
            wor = wot.rearrange("(it p) f -> it p f", it=32)
            wbr = wbig.rearrange("p (it f) -> p it f", it=32)
            for oc in range(4):
                if oc == 0:
                    g.wait_ge(pe_avz, 16 * NH)
                    g.wait_ge(st_ao[0], 16 * 8 * NH // 2)
                    g.wait_ge(st_ao[1], 16 * 8 * NH // 2)
                else:
                    g.wait_ge(pe6, 8 * oc)
                for it in range(32):
                    g.dma_start(out=wbr[:, it, :],
                                in_=wor[it][:, 1024 * oc:1024 * (oc + 1)]).then_inc(ld_wo, 16)
            wlr = wlag_out.rearrange("(r it p) o -> r it p o", r=8, it=32)
            wbr2 = wbig.rearrange("p (it rb f) -> p it rb f", it=32, rb=2)
            for rp in range(4):
                if rp == 0:
                    g.wait_ge(cc_wl, 1)
                    g.wait_ge(pe6, 32)
                else:
                    g.wait_ge(pe8, 8 * rp)
                for rb in range(2):
                    for it in range(32):
                        g.dma_start(out=wbr2[:, it, rb, :],
                                    in_=wlr[2 * rp + rb, it]).then_inc(ld_wl, 16)

        @block.vector
        def _(v):
            v.wait_ge(ge, 1)
            v.tensor_scalar(out=ident, in0=iota_i, scalar1=0.0, scalar2=None,
                            op0=AluOpType.is_equal)
            v.tensor_scalar(out=identb, in0=iota_i, scalar1=0.0, scalar2=None,
                            op0=AluOpType.is_equal)
            v.memset(onesb[:, 0:1], 1.0)
            v.memset(onesr[0:1, :], 1.0)
            v.memset(eps_t, EPS)
            v.wait_ge(ld_misc, 64)
            v.tensor_scalar(out=bqs, in0=bqs, scalar1=SCALE, scalar2=None,
                            op0=AluOpType.mult).then_inc(ve0, 1)

            def ln_tile(tt, ld_sem, vsem, ssem, psem, x_src):
                v.wait_ge(ld_sem[tt % 2], 16 * (tt // 2 + 1))
                last = None
                for i in range(8):
                    last = v.bn_stats(bn6[:, 6 * i:6 * i + 6], x_src[:, 512 * i:512 * (i + 1)])
                last.then_inc(vsem, 1)
                v.wait_ge(vsem, VE1_TT * tt + 1)
                v.bn_aggr(mv, bn6).then_inc(vsem, 1)
                v.wait_ge(ssem, tt + 1)
                v.reciprocal(rstd, sq).then_inc(vsem, 1)
                v.wait_ge(vsem, VE1_TT * tt + 3)
                v.tensor_scalar(out=nmr, in0=mv[:, 0:1], scalar1=rstd, scalar2=-1.0,
                                op0=AluOpType.mult, op1=AluOpType.mult).then_inc(vsem, 1)
                v.wait_ge(vsem, VE1_TT * tt + 4)
                if tt >= 1:
                    v.wait_ge(psem, 4 * tt)
                v.tensor_scalar(out=xnf, in0=x_src, scalar1=rstd, scalar2=nmr,
                                op0=AluOpType.mult, op1=AluOpType.add).then_inc(vsem, 1)

            xnr = xn_dm.rearrange("p (it t) -> p it t", it=32)
            pdr = PD.rearrange("p (j t) -> p j t", j=8)
            for tt in range(8):
                ln_tile(tt, ld_x1, ve1, se1, pe1, x_io[tt % 2])
                for gq in range(4):
                    G = 4 * tt + gq
                    v.wait_ge(pe1, G + 1)
                    v.tensor_copy(xnr[:, 8 * gq:8 * gq + 8, 128 * tt:128 * (tt + 1)],
                                  pdr).then_inc(ve1b, 1)
            vcnt = 0
            for ot in range(4):
                v.wait_ge(ld_qw, 32 * (ot + 1))
                for pl in range(4):
                    last = None
                    for sh in range(8):
                        last = v.tensor_scalar(
                            out=bits[pl].rearrange("p (b sh) -> p sh b", sh=8)[:, sh, :],
                            in0=qw_u8[:, 512 * pl:512 * (pl + 1)],
                            scalar1=sh, scalar2=1,
                            op0=AluOpType.logical_shift_right,
                            op1=AluOpType.bitwise_and)
                    last.then_inc(ve_dq, 1); vcnt += 1
                v.wait_ge(ve_dq, vcnt)
                v.tensor_tensor(out=dlev,
                                in0=lut_t.rearrange("p (c two) -> p two c", two=2)[:, 1, :],
                                in1=lut_t.rearrange("p (c two) -> p two c", two=2)[:, 0, :],
                                op=AluOpType.subtract).then_inc(ve_dq, 1)
                vcnt += 1
                v.wait_ge(ve_dq, vcnt)
                for chk in range(NCHUNK):
                    C = NCHUNK * ot + chk
                    gidx = C // 2
                    sl = slice(CH * chk, CH * (chk + 1))
                    last = None
                    for c8 in range(8):
                        last = v.tensor_scalar(
                            out=tbuf[:, CH * c8:CH * (c8 + 1)], in0=bits[0][:, sl],
                            scalar1=dlev[:, c8:c8 + 1], scalar2=lut_t[:, 2 * c8:2 * c8 + 1],
                            op0=AluOpType.mult, op1=AluOpType.add)
                    last.then_inc(ve_dq, 1); vcnt += 1
                    v.wait_ge(ve_dq, vcnt)
                    for c4 in range(4):
                        last = v.select(ubuf[:, CH * c4:CH * (c4 + 1)], bits[1][:, sl],
                                        tbuf[:, CH * (2 * c4 + 1):CH * (2 * c4 + 2)],
                                        tbuf[:, CH * 2 * c4:CH * (2 * c4 + 1)], add_drain=True)
                    last.then_inc(ve_dq, 1); vcnt += 1
                    v.wait_ge(ve_dq, vcnt)
                    for c2 in range(2):
                        last = v.select(vbuf[:, CH * c2:CH * (c2 + 1)], bits[2][:, sl],
                                        ubuf[:, CH * (2 * c2 + 1):CH * (2 * c2 + 2)],
                                        ubuf[:, CH * 2 * c2:CH * (2 * c2 + 1)], add_drain=True)
                    last.then_inc(ve_dq, 1); vcnt += 1
                    v.wait_ge(ve_dq, vcnt)
                    if gidx >= 4:
                        v.wait_ge(pe_wt, gidx - 3)
                    v.select(wl_f2[gidx % 4][:, 512 * (C % 2):512 * (C % 2) + 512],
                             bits[3][:, sl], vbuf[:, CH:2 * CH], vbuf[:, 0:CH],
                             add_drain=True).then_inc(ve_dq, 1)
                    vcnt += 1
                    v.wait_ge(ve_dq, vcnt)
                    if C % 2 == 1 and gidx >= 2:
                        G = gidx - 2
                        v.wait_ge(pe_wt, G + 1)
                        if G >= 2:
                            v.wait_ge(exp_wl[G % 2], 16 * 8 * (G // 2))
                        v.tensor_copy(wlt_ev[G % 2], PD[:, 0:1024]).then_inc(ve_wt, 1)
                assert vcnt == (ot + 1) * VE_DQ_OT
            for G in (14, 15):
                v.wait_ge(pe_wt, G + 1)
                v.wait_ge(exp_wl[G % 2], 16 * 8 * (G // 2))
                v.tensor_copy(wlt_ev[G % 2], PD[:, 0:1024]).then_inc(ve_wt, 1)
            for H in range(NH):
                v.wait_ge(pe_avz, 16 * H + 8)
                v.reciprocal(rzrow[0:1, 0:512], PD[0:1, 0:512]).then_inc(ve_rz, 1)
                v.wait_ge(pe_avz, 16 * H + 16)
                v.reciprocal(rzrow[0:1, 512:1024], PD[0:1, 512:1024]).then_inc(ve_rz, 1)
                v.wait_ge(pe_rzb, 2 * H + 1)
                v.tensor_copy(rzb[:, 0:512], PD[:, 0:512]).then_inc(ve_fin, 1)
                v.wait_ge(ve_fin, 3 * H + 1)
                if H >= 2:
                    v.wait_ge(st_ao[H % 2], 16 * 8 * (H // 2))
                v.tensor_tensor(out=ao_sb[H % 2][:, 0:512], in0=PC[:, 0:512],
                                in1=rzb[:, 0:512],
                                op=AluOpType.mult).then_inc(ve_mid, 1)
                v.wait_ge(pe_rzb, 2 * H + 2)
                v.tensor_copy(rzb[:, 512:1024], PD[:, 512:1024]).then_inc(ve_fin, 1)
                v.wait_ge(ve_fin, 3 * H + 2)
                v.tensor_tensor(out=ao_sb[H % 2][:, 512:1024], in0=PC[:, 512:1024],
                                in1=rzb[:, 512:1024],
                                op=AluOpType.mult).then_inc(ve_mid, 1)
                v.wait_ge(ve_mid, 2 * H + 2)
                v.tensor_scalar(out=ao_sb[H % 2], in0=ao_sb[H % 2],
                                scalar1=bv_sb[:, H:H + 1], scalar2=None,
                                op0=AluOpType.add).then_inc(ve_fin, 1)
            for c in range(4):
                v.wait_ge(pe_bb, c + 1)
                v.tensor_copy(biasB[:, 1024 * c:1024 * (c + 1)], PD[:, 0:1024]).then_inc(ve_bb, 1)
            for oc in range(4):
                for tt in range(8):
                    i = 8 * oc + tt
                    v.wait_ge(pe6, i + 1)
                    v.wait_ge(ld_x6[i % 2], 16 * (i // 2 + 1))
                    if i >= 2:
                        v.wait_ge(st_x2[i % 2], 16 * (i // 2))
                    ps = PA if tt % 2 == 0 else PB
                    v.tensor_tensor(out=x2o[tt % 2], in0=ps[:, 0:1024],
                                    in1=biasB[:, 1024 * oc:1024 * (oc + 1)],
                                    op=AluOpType.add).then_inc(ve_mid, 1)
                    v.wait_ge(ve_mid, 64 + i + 1)
                    v.tensor_tensor(out=x2o[tt % 2], in0=x2o[tt % 2], in1=xch[tt % 2],
                                    op=AluOpType.add).then_inc(ve6, 1)
            for tt in range(8):
                ln_tile(tt, ld_x27, ve7, se7, pe7, x_io[tt % 2])
                for gq in range(4):
                    G = 4 * tt + gq
                    v.wait_ge(pe7, G + 1)
                    v.tensor_copy(xnr[:, 8 * gq:8 * gq + 8, 128 * tt:128 * (tt + 1)],
                                  pdr).then_inc(ve7b, 1)
            for c in range(4):
                v.wait_ge(pe_bb, 4 + c + 1)
                v.tensor_copy(biasB[:, 1024 * c:1024 * (c + 1)], PD[:, 0:1024]).then_inc(ve_bb, 1)
            for rp in range(4):
                for tt in range(8):
                    i = 8 * rp + tt
                    v.wait_ge(pe8, i + 1)
                    v.wait_ge(ld_x28[i % 2], 16 * (i // 2 + 1))
                    if i >= 2:
                        v.wait_ge(st_out[i % 2], 16 * (i // 2))
                    ps = PA if tt % 2 == 0 else PB
                    v.tensor_tensor(out=outb[tt % 2], in0=ps[:, 0:1024],
                                    in1=biasB[:, 1024 * rp:1024 * (rp + 1)],
                                    op=AluOpType.add).then_inc(ve_mid, 1)
                    v.wait_ge(ve_mid, 96 + i + 1)
                    v.tensor_tensor(out=outb[tt % 2], in0=outb[tt % 2], in1=x2ch[tt % 2],
                                    op=AluOpType.add).then_inc(ve8, 1)

        @block.scalar
        def _(sc):
            sc.wait_ge(ve0, 1)
            for tt in range(8):
                sc.wait_ge(ve1, VE1_TT * tt + 2)
                sc.activation(sq, mv[:, 1:2], AF.Sqrt, bias=eps_t, scale=1.0).then_inc(se1, 1)
            for mt in range(32):
                sc.wait_ge(pe2a, mt + 1)
                if mt >= 2:
                    sc.wait_ge(pe_vt, mt - 1)
                ps = (PA, PB, PC)[mt % 3]
                sc.activation(xvf[mt % 2], ps[:, 0:1024], AF.Identity).then_inc(se2a, 1)
                sc.wait_ge(pe_vt, mt + 1)
                if mt >= 2:
                    sc.wait_ge(exp_v[mt % 2], 16 * 8 * (mt // 2))
                sc.activation(vexp[mt % 2], PD[:, 0:1024], AF.Identity).then_inc(se_vx, 1)
            for mt in range(32):
                sc.wait_ge(pe2b, mt + 1)
                if mt >= 2:
                    pm = mt - 2
                    sc.wait_ge(exp_k[2 * (pm // 16) + pm % 2], 16 * ((pm % 16) // 2 + 1))
                ps = (PA, PB, PC)[mt % 3]
                sc.activation(kexp[mt % 2], ps[:, 0:1024], AF.Identity,
                              bias=bk_sb[:, mt:mt + 1], scale=1.0).then_inc(se2b, 1)
            for hg in range(NHG):
                for mtl in range(4):
                    mt = 4 * hg + mtl
                    sc.wait_ge(pe2b, 32 + mt + 1)
                    if hg >= 2 and mtl == 0:
                        sc.wait_ge(pe_sc, 16 * 4 * (hg - 1))
                    ps = PA if mt % 2 == 0 else PB
                    sc.activation(Q_hg[hg % 2][:, 1024 * mtl:1024 * (mtl + 1)], ps[:, 0:1024],
                                  AF.Identity, bias=bqs[:, mt:mt + 1], scale=SCALE).then_inc(se5q, 1)
                for hl in range(4):
                    H = 4 * hg + hl
                    for kt in range(16):
                        if kt == 6 and H >= 1:
                            Hp = H - 1
                            sc.wait_ge(ve_fin, 3 * Hp + 3)
                            for tt in range(8):
                                sc.dma_start(out=ao_scr[tt, :, Hp, :],
                                             in_=ao_sb[Hp % 2][:, 128 * tt:128 * (tt + 1)]).then_inc(st_ao[Hp % 2], 16)
                        sc.wait_ge(pe_sc, 16 * H + kt + 1)
                        if H >= 1:
                            sc.wait_ge(pe_avz, 16 * (H - 1) + kt + 1)
                        slot = PA if kt % 2 == 0 else PB
                        q0 = q0_of(kt)
                        sc.activation(attT[:, ATO[kt]:ATO[kt] + ATW[kt]],
                                      slot[:, q0:1024], AF.Exp).then_inc(se_exp, 1)
            sc.wait_ge(ve_fin, 3 * NH)
            for tt in range(8):
                sc.dma_start(out=ao_scr[tt, :, NH - 1, :],
                             in_=ao_sb[(NH - 1) % 2][:, 128 * tt:128 * (tt + 1)]).then_inc(st_ao[(NH - 1) % 2], 16)
            for oc in range(4):
                for tt in range(8):
                    i = 8 * oc + tt
                    sc.wait_ge(ve6, i + 1)
                    sc.dma_start(out=x2_scr[128 * tt:128 * (tt + 1), 1024 * oc:1024 * (oc + 1)],
                                 in_=x2o[tt % 2]).then_inc(st_x2[i % 2], 16)
            for tt in range(8):
                sc.wait_ge(ve7, VE1_TT * tt + 2)
                sc.activation(sq, mv[:, 1:2], AF.Sqrt, bias=eps_t, scale=1.0).then_inc(se7, 1)
            for rp in range(4):
                for tt in range(8):
                    i = 8 * rp + tt
                    sc.wait_ge(ve8, i + 1)
                    sc.dma_start(out=out_own[128 * tt:128 * (tt + 1), 1024 * rp:1024 * (rp + 1)],
                                 in_=outb[tt % 2]).then_inc(st_out[i % 2], 16)
            sc.wait_ge(st_out[0], 16 * 16)
            sc.wait_ge(st_out[1], 16 * 16)

        @block.tensor
        def _(te):
            te.wait_ge(ve0, 1)
            for tt in range(8):
                te.wait_ge(ve1, VE1_TT * (tt + 1))
                for gq in range(4):
                    G = 4 * tt + gq
                    if G > 0:
                        te.wait_ge(ve1b, G)
                    last = None
                    for j in range(8):
                        d = 8 * gq + j
                        last = te.transpose(PD[:, 128 * j:128 * (j + 1)],
                                            xnf[:, 128 * d:128 * (d + 1)], ident)
                    last.then_inc(pe1, 1)
            prev_pd = None
            for mt in range(33):
                if mt < 32:
                    te.wait_ge(ld_wv[mt % 2], 16 * (mt // 2 + 1))
                    if mt == 0:
                        te.wait_ge(ve1b, 32)
                    if mt >= 3:
                        te.wait_ge(se2a, mt - 2)
                    ps = (PA, PB, PC)[mt % 3]
                    last = None
                    for it in range(32):
                        for (c0, c1) in ((0, 512), (512, 1024)):
                            last = te.matmul(ps[:, c0:c1], wcol[mt % 2][:, 128 * it:128 * (it + 1)],
                                             xn_dm[:, 1024 * it + c0:1024 * it + c1],
                                             start=(it == 0), stop=(it == 31))
                    last.then_inc(pe2a, 1)
                if mt >= 1:
                    mp = mt - 1
                    te.wait_ge(se2a, mp + 1)
                    if prev_pd is not None:
                        te.wait_ge(*prev_pd)
                    last = None
                    for j in range(8):
                        last = te.transpose(PD[:, 128 * j:128 * (j + 1)],
                                            xvf[mp % 2][:, 128 * j:128 * (j + 1)], ident)
                    last.then_inc(pe_vt, 1)
                    prev_pd = (se_vx, mp + 1)
                    if mp % 2 == 1 and mp // 2 < 16:
                        G = mp // 2
                        te.wait_ge(ve_dq, dq_chunk_done(G // 4, 2 * (G % 4) + 1))
                        te.wait_ge(*prev_pd)
                        last = None
                        for j in range(8):
                            last = te.transpose(PD[:, 128 * j:128 * (j + 1)],
                                                wl_f2[G % 4][:, 128 * j:128 * (j + 1)], ident)
                        last.then_inc(pe_wt, 1)
                        prev_pd = (ve_wt, G + 1)
            for mt in range(32):
                te.wait_ge(ld_wk[mt % 2], 16 * (mt // 2 + 1))
                if mt < 3:
                    te.wait_ge(se2a, (31, 32, 30)[mt])
                if mt >= 3:
                    te.wait_ge(se2b, mt - 2)
                ps = (PA, PB, PC)[mt % 3]
                last = None
                for it in range(32):
                    for (c0, c1) in ((0, 512), (512, 1024)):
                        last = te.matmul(ps[:, c0:c1], wcol[mt % 2][:, 128 * it:128 * (it + 1)],
                                         xn_dm[:, 1024 * it + c0:1024 * it + c1],
                                         start=(it == 0), stop=(it == 31))
                last.then_inc(pe2b, 1)
            for hg in range(NHG):
                for mtl in range(4):
                    mt = 4 * hg + mtl
                    te.wait_ge(ld_wq[mt % 2], 16 * (mt // 2 + 1))
                    if hg == 0 and mtl < 2:
                        te.wait_ge(se2b, 31 + mtl)
                        te.wait_ge(ve_wt, 16)
                    if hg >= 1 and mtl < 2:
                        te.wait_ge(se_exp, 16 * 4 * hg)
                    if mtl >= 2:
                        te.wait_ge(se5q, 4 * hg + mtl - 1)
                    ps = PA if mt % 2 == 0 else PB
                    last = None
                    for it in range(32):
                        for (c0, c1) in ((0, 512), (512, 1024)):
                            last = te.matmul(ps[:, c0:c1], wcol[mt % 2][:, 128 * it:128 * (it + 1)],
                                             xn_dm[:, 1024 * it + c0:1024 * it + c1],
                                             start=(it == 0), stop=(it == 31))
                    last.then_inc(pe2b, 1)
                def mk_avz(H):
                    def avz_pair(s_):
                        if s_ == 0:
                            if H >= 1:
                                te.wait_ge(ve_mid, 2 * H)
                            else:
                                te.wait_ge(ve_wt, 16)
                        for kt in (2 * s_, 2 * s_ + 1):
                            te.wait_ge(se_exp, 16 * H + 2 * s_ + 2)
                            lastm = None
                            for (c0, c1) in q_ranges(kt):
                                stop = (kt == 7) if c1 <= 512 else (kt == 15)
                                lastm = te.matmul(
                                    PC[:, c0:c1], v_sb[H % 2][:, 128 * kt:128 * (kt + 1)],
                                    attT[:, ATO[kt] + c0 - q0_of(kt):ATO[kt] + c1 - q0_of(kt)],
                                    start=(kt == 0), stop=stop)
                            for (c0, c1) in q_ranges(kt):
                                stop = (kt == 7) if c1 <= 512 else (kt == 15)
                                lastm = te.matmul(
                                    PD[0:1, c0:c1], onesb[:, 0:1],
                                    attT[:, ATO[kt] + c0 - q0_of(kt):ATO[kt] + c1 - q0_of(kt)],
                                    start=(kt == 0), stop=stop)
                            lastm.then_inc(pe_avz, 1)
                    return avz_pair

                def rz_bcast(Hp, h2):
                        te.wait_ge(ve_rz, 2 * Hp + h2 + 1)
                        te.matmul(PD[:, 512 * h2:512 * (h2 + 1)], onesr[0:1, :],
                                  rzrow[0:1, 512 * h2:512 * (h2 + 1)],
                                  start=True, stop=True).then_inc(pe_rzb, 1)

                for hl in range(4):
                    H = 4 * hg + hl
                    te.wait_ge(ld_kv[H % 2], 16 * 8 * (H // 2 + 1))
                    avz_pair = mk_avz(H)
                    for s_ in range(8):
                        for kt in (2 * s_, 2 * s_ + 1):
                            tgt = 16 * H + kt - 1
                            if tgt > 0:
                                te.wait_ge(se_exp, tgt)
                            if kt == 0:
                                te.wait_ge(se5q, 4 * (hg + 1))
                            slot = PA if kt % 2 == 0 else PB
                            s_of = kt // 2
                            j_of = kt % 2
                            last = None
                            for (c0, c1) in q_ranges(kt):
                                has_mask = (c0 == q0_of(kt))
                                last = te.matmul(slot[:, c0:c1],
                                                 k_sb[H % 2][:, 128 * kt:128 * (kt + 1)],
                                                 Q_hg[hg % 2][:, 1024 * hl + c0:1024 * hl + c1],
                                                 start=True, stop=not has_mask)
                                if has_mask:
                                    last = te.matmul(
                                        slot[:, c0:c0 + 128], identb,
                                        masks[:, 256 * s_of + 128 * j_of:256 * s_of + 128 * (j_of + 1)],
                                        start=False, stop=True)
                            last.then_inc(pe_sc, 1)
                        if s_ == 1 and H >= 1:
                            rz_bcast(H - 1, 0)
                        if s_ == 2 and H >= 1:
                            rz_bcast(H - 1, 1)
                        if s_ >= 2:
                            avz_pair(s_ - 2)
                    avz_pair(6)
                    avz_pair(7)
                    if H == NH - 1:
                        rz_bcast(H, 0)
                        rz_bcast(H, 1)
            for c in range(4):
                if c == 0:
                    te.wait_ge(ve_fin, 3 * NH)
                te.wait_ge(ld_misc, 64 + 16 * (c + 1))
                if c > 0:
                    te.wait_ge(ve_bb, c)
                last = None
                for h2 in range(2):
                    last = te.matmul(PD[:, 512 * h2:512 * (h2 + 1)], onesr[0:1, :],
                                     rzrow[0:1, 512 * h2:512 * (h2 + 1)], start=True, stop=True)
                last.then_inc(pe_bb, 1)
            for oc in range(4):
                te.wait_ge(ld_wo, 16 * 32 * (oc + 1))
                if oc == 0:
                    te.wait_ge(ve_bb, 4)
                for tt in range(8):
                    i = 8 * oc + tt
                    te.wait_ge(ld_ao[i % 2], 16 * (i // 2 + 1))
                    if i < 2:
                        te.wait_ge(se_exp, 16 * NH)
                    if i >= 2:
                        te.wait_ge(ve6, i - 1)
                    ps = PA if tt % 2 == 0 else PB
                    last = None
                    for it in range(32):
                        for (c0, c1) in ((0, 512), (512, 1024)):
                            last = te.matmul(ps[:, c0:c1], aost[tt % 2][:, 128 * it:128 * (it + 1)],
                                             wbig[:, 1024 * it + c0:1024 * it + c1],
                                             start=(it == 0), stop=(it == 31))
                    last.then_inc(pe6, 1)
            for tt in range(8):
                te.wait_ge(ve7, VE1_TT * (tt + 1))
                for gq in range(4):
                    G = 4 * tt + gq
                    if G > 0:
                        te.wait_ge(ve7b, G)
                    else:
                        te.wait_ge(ve_bb, 4)
                    last = None
                    for j in range(8):
                        d = 8 * gq + j
                        last = te.transpose(PD[:, 128 * j:128 * (j + 1)],
                                            xnf[:, 128 * d:128 * (d + 1)], ident)
                    last.then_inc(pe7, 1)
            for c in range(4):
                if c == 0:
                    te.wait_ge(ve7b, 32)
                te.wait_ge(ld_misc, 128 + 16 * (c + 1))
                if c > 0:
                    te.wait_ge(ve_bb, 4 + c)
                last = None
                for h2 in range(2):
                    last = te.matmul(PD[:, 512 * h2:512 * (h2 + 1)], onesr[0:1, :],
                                     rzrow[0:1, 512 * h2:512 * (h2 + 1)], start=True, stop=True)
                last.then_inc(pe_bb, 1)
            wbr2t = wbig.rearrange("p (it rb f) -> p it rb f", it=32, rb=2)
            for rp in range(4):
                te.wait_ge(ld_wl, 16 * 64 * (rp + 1))
                if rp == 0:
                    te.wait_ge(ve_bb, 8)
                    te.wait_ge(ve6, 32)
                for tt in range(8):
                    i = 8 * rp + tt
                    ps = PA if tt % 2 == 0 else PB
                    if i >= 2:
                        te.wait_ge(ve8, i - 1)
                    last = None
                    for it in range(32):
                        for rb in range(2):
                            last = te.matmul(
                                ps[:, 512 * rb:512 * (rb + 1)],
                                xn_dm[:, 1024 * it + 128 * tt:1024 * it + 128 * (tt + 1)],
                                wbr2t[:, it, rb, :],
                                start=(it == 0), stop=(it == 31))
                    last.then_inc(pe8, 1)
    return nc


def kernel(L0_attn_x, L0_attn_Wq, L0_attn_bq, L0_attn_Wk, L0_attn_bk,
           L0_attn_Wv, L0_attn_bv, L0_attn_Wo, L0_attn_bo,
           L0_ffn_qweight, L0_ffn_lut4, L0_ffn_lut8, L0_ffn_bias):
    x = np.asarray(L0_attn_x, dtype=np.float32)
    Wq = np.asarray(L0_attn_Wq, dtype=np.float32)
    Wk = np.asarray(L0_attn_Wk, dtype=np.float32)
    Wv = np.asarray(L0_attn_Wv, dtype=np.float32)
    Wo = np.asarray(L0_attn_Wo, dtype=np.float32)
    bq = np.asarray(L0_attn_bq, dtype=np.float32)
    bk = np.asarray(L0_attn_bk, dtype=np.float32)
    bv = np.asarray(L0_attn_bv, dtype=np.float32)
    bo = np.asarray(L0_attn_bo, dtype=np.float32)
    qw = np.asarray(L0_ffn_qweight)
    lut4 = np.asarray(L0_ffn_lut4, dtype=np.float32)
    bfb = np.asarray(L0_ffn_bias, dtype=np.float32)

    wq4 = np.ascontiguousarray(Wq.reshape(32, 128, 32, 128).transpose(0, 3, 2, 1))
    wk4 = np.ascontiguousarray(Wk.reshape(32, 128, 32, 128).transpose(0, 3, 2, 1))
    wv4 = np.ascontiguousarray(Wv.reshape(32, 128, 32, 128).transpose(0, 3, 2, 1))
    wot = np.ascontiguousarray(Wo.T)
    bq_t = np.ascontiguousarray(bq.reshape(32, 128).T)
    bk_t = np.ascontiguousarray(bk.reshape(32, 128).T)
    bv_t = np.ascontiguousarray(bv.reshape(32, 128).T)
    bo_row = np.ascontiguousarray(bo.reshape(1, 4096))
    bf_row = np.ascontiguousarray(bfb.reshape(1, 4096))

    def build_mask(parity):
        QT = QT_EVEN if parity == 0 else QT_ODD
        m = np.zeros((128, 2048), np.float32)
        for s in range(8):
            qt = QT[s]
            for j in range(2):
                kt = 2 * s + j
                if kt < qt:
                    t = np.zeros((128, 128), np.float32)
                elif kt == qt:
                    t = np.where(np.arange(128)[:, None] <= np.arange(128)[None, :],
                                 0.0, -30000.0).astype(np.float32)
                else:
                    t = np.full((128, 128), -30000.0, np.float32)
                m[:, 256 * s + 128 * j:256 * s + 128 * (j + 1)] = t
        return m.astype(BF)

    mask_by_par = [build_mask(0), build_mask(1)]

    nc = build_nc()
    in_maps = []
    for core in range(8):
        b = core // 2
        parity = core % 2
        QT = QT_EVEN if parity == 0 else QT_ODD
        rows = np.concatenate([128 * qt + np.arange(128) for qt in QT])
        in_maps.append({
            "x_own": np.ascontiguousarray(x[b][rows]),
            "wq4": wq4, "wk4": wk4, "wv4": wv4, "wot": wot,
            "bq_t": bq_t, "bk_t": bk_t, "bv_t": bv_t,
            "bo_row": bo_row, "bf_row": bf_row,
            "qw_own": np.ascontiguousarray(qw[0:4, 512 * core:512 * (core + 1), :]),
            "lut_own": np.ascontiguousarray(lut4[512 * core:512 * (core + 1), :]),
            "masks_in": mask_by_par[parity],
        })
    res = run_bass_kernel_spmd(nc, in_maps, core_ids=list(range(8)))
    out = np.empty((4, 2048, 4096), dtype=np.float32)
    for core in range(8):
        b = core // 2
        parity = core % 2
        QT = QT_EVEN if parity == 0 else QT_ODD
        rows = np.concatenate([128 * qt + np.arange(128) for qt in QT])
        out[b][rows] = res.results[core]["out_own"]
    return out


# version 14
# speedup vs baseline: 1.0922x; 1.0225x over previous
import contextlib
import math
import numpy as np
import ml_dtypes
import concourse.bass as bass
import concourse.mybir as mybir
from concourse.bass_utils import run_bass_kernel_spmd
from concourse.alu_op_type import AluOpType

F32 = mybir.dt.float32
BF16 = mybir.dt.bfloat16
I32 = mybir.dt.int32
U8 = mybir.dt.uint8
AF = mybir.ActivationFunctionType
BF = ml_dtypes.bfloat16

QT_EVEN = [0, 3, 4, 7, 8, 11, 12, 15]
QT_ODD = [1, 2, 5, 6, 9, 10, 13, 14]
SCALE = 1.0 / math.sqrt(128.0)
EPS = 1e-5
NH = 32
NHG = 8
CH = 512
NCHUNK = 4096 // CH
VE_DQ_OT = 4 + 1 + NCHUNK * 4
VE1_TT = 5

def q0_of(kt):
    return 128 * (kt // 2)
ATW = [1024 - q0_of(kt) for kt in range(16)]
ATO = [0] * 16
for _kt in range(1, 16):
    ATO[_kt] = ATO[_kt - 1] + ATW[_kt - 1]
ATT_TOT = ATO[15] + ATW[15]

def q_ranges(kt):
    q0 = q0_of(kt)
    return [(q0, 512), (512, 1024)] if q0 < 512 else [(q0, 1024)]


def build_nc():
    nc = bass.Bass("TRN2", target_bir_lowering=False)
    x_own = nc.dram_tensor("x_own", [1024, 4096], F32, kind="ExternalInput").ap()
    wq4 = nc.dram_tensor("wq4", [32, 128, 32, 128], F32, kind="ExternalInput").ap()
    wk4 = nc.dram_tensor("wk4", [32, 128, 32, 128], F32, kind="ExternalInput").ap()
    wv4 = nc.dram_tensor("wv4", [32, 128, 32, 128], F32, kind="ExternalInput").ap()
    wot = nc.dram_tensor("wot", [4096, 4096], F32, kind="ExternalInput").ap()
    bq_t = nc.dram_tensor("bq_t", [128, 32], F32, kind="ExternalInput").ap()
    bk_t = nc.dram_tensor("bk_t", [128, 32], F32, kind="ExternalInput").ap()
    bv_t = nc.dram_tensor("bv_t", [128, 32], F32, kind="ExternalInput").ap()
    bo_row = nc.dram_tensor("bo_row", [1, 4096], F32, kind="ExternalInput").ap()
    bf_row = nc.dram_tensor("bf_row", [1, 4096], F32, kind="ExternalInput").ap()
    qw_own = nc.dram_tensor("qw_own", [4, 512, 128], I32, kind="ExternalInput").ap()
    lut_own = nc.dram_tensor("lut_own", [512, 16], F32, kind="ExternalInput").ap()
    masks_in = nc.dram_tensor("masks_in", [128, 2048], BF16, kind="ExternalInput").ap()
    out_own = nc.dram_tensor("out_own", [1024, 4096], F32, kind="ExternalOutput").ap()
    kag_inA = nc.dram_tensor("kag_inA", [2048, 1024], BF16).ap()
    kag_inB = nc.dram_tensor("kag_inB", [2048, 1024], BF16).ap()
    kag_outA = nc.dram_tensor("kag_outA", [4096, 1024], BF16).ap()
    kag_outB = nc.dram_tensor("kag_outB", [4096, 1024], BF16).ap()
    vag_in = nc.dram_tensor("vag_in", [1024, 4096], BF16).ap()
    vag_out = nc.dram_tensor("vag_out", [2048, 4096], BF16).ap()
    wlag_in = nc.dram_tensor("wlag_in", [4096, 512], BF16).ap()
    wlag_out = nc.dram_tensor("wlag_out", [32768, 512], BF16).ap()
    ao_scr = nc.dram_tensor("ao_scr", [8, 128, 32, 128], BF16).ap()
    x2_scr = nc.dram_tensor("x2_scr", [1024, 4096], F32).ap()

    es = contextlib.ExitStack()
    with es:
        AR = es.enter_context(nc.sbuf_tensor("AR", [128, 204 * 1024], U8))
        arf, arb, ari, aru = AR.bitcast(F32), AR.bitcast(BF16), AR.bitcast(I32), AR.bitcast(U8)

        def fsl(o, n):
            assert o % 64 == 0
            return arf.ap()[:, o // 4:o // 4 + n]

        def bsl(o, n):
            assert o % 64 == 0
            return arb.ap()[:, o // 2:o // 2 + n]

        def isl(o, n):
            assert o % 64 == 0
            return ari.ap()[:, o // 4:o // 4 + n]

        def usl(o, n):
            assert o % 64 == 0
            return aru.ap()[:, o:o + n]

        masks = bsl(0, 2048)
        biasB = bsl(4096, 4096)
        rzb = fsl(12288, 1024)
        rzrow = fsl(16384, 1024)
        ident = fsl(20480, 128)
        identb = bsl(23040, 128)
        iota_i = isl(20992, 128)
        onesr = fsl(21504, 128)
        onesb = bsl(22016, 32)
        bqs = fsl(22080, 32)
        bk_sb = fsl(22208, 32)
        bv_sb = fsl(22336, 32)
        bn6 = fsl(22464, 48)
        mv = fsl(22656, 2)
        sq = fsl(22720, 1)
        rstd = fsl(22784, 1)
        nmr = fsl(22848, 1)
        eps_t = fsl(22912, 1)
        S1 = 24 * 1024
        xn_dm = bsl(S1, 32768)
        S2 = 88 * 1024
        bits = [usl(S2 + 4096 * i, 4096) for i in range(4)]
        qw_i32 = isl(S2 + 16384, 512)
        qw_u8 = usl(S2 + 16384, 2048)
        lut_t = fsl(S2 + 18432, 16)
        dlev = fsl(S2 + 18496, 8)
        tbuf = fsl(S2 + 18560, 8 * CH)
        ubuf = fsl(S2 + 34944, 4 * CH)
        vbuf = fsl(S2 + 43136, 2 * CH)
        wl_f2 = [fsl(S2 + 47232 + 4096 * i, 1024) for i in range(4)]
        wlt_ev = [bsl(4096 + 2048 * i, 1024) for i in range(2)]
        attT = bsl(S2, ATT_TOT)
        Q_hg = [bsl(S2 + 18432 + 8192 * i, 4096) for i in range(2)]
        k_sb = [bsl(S2 + 34816 + 4096 * i, 2048) for i in range(2)]
        v_sb = [bsl(S2 + 43008 + 4096 * i, 2048) for i in range(2)]
        ao_sb = [bsl(S2 + 51200 + 2048 * i, 1024) for i in range(2)]
        wbig = bsl(S2, 32768)
        R3 = 152 * 1024
        vexp = [bsl(R3 + 24576 + 2048 * i, 1024) for i in range(2)]
        xvf = [fsl(R3 + 16384 + 4096 * i, 1024) for i in range(2)]
        x_io = [fsl(R3 + 16384 * i, 4096) for i in range(2)]
        xnf = fsl(R3 + 32768, 4096)
        wcol = [bsl(R3 + 8192 * i, 4096) for i in range(2)]
        kexp = [bsl(R3 + 16384 + 2048 * i, 1024) for i in range(2)]
        aost = [bsl(R3 + 8192 * i, 4096) for i in range(2)]
        xch = [fsl(R3 + 16384 + 4096 * i, 1024) for i in range(2)]
        x2o = [fsl(R3 + 24576 + 4096 * i, 1024) for i in range(2)]
        x2ch = [fsl(R3 + 4096 * i, 1024) for i in range(2)]
        outb = [fsl(R3 + 8192 + 4096 * i, 1024) for i in range(2)]
        wunit3 = bsl(R3 + 16384, 16384)

        PA = es.enter_context(nc.psum_tensor("PA", [128, 1024], F32))
        PB = es.enter_context(nc.psum_tensor("PB", [128, 1024], F32))
        PC = es.enter_context(nc.psum_tensor("PC", [128, 1024], F32))
        PD = es.enter_context(nc.psum_tensor("PD", [128, 1024], F32))

        def sem(name):
            return es.enter_context(nc.semaphore(name))
        ld_misc = sem("ld_misc"); ld_qw = sem("ld_qw")
        ld_x1 = [sem("ld_x1a"), sem("ld_x1b")]
        ld_wv = [sem("ld_wva"), sem("ld_wvb")]
        ld_wk = [sem("ld_wka"), sem("ld_wkb")]
        ld_wq = [sem("ld_wqa"), sem("ld_wqb")]
        ld_wo = sem("ld_wo"); ld_wl = [sem("ld_wla"), sem("ld_wlb"), sem("ld_wlc")]
        ld_kv = [sem("ld_kva"), sem("ld_kvb")]
        ld_ao = [sem("ld_aoa"), sem("ld_aob")]
        ld_x6 = [sem("ld_x6a"), sem("ld_x6b")]
        ld_x27 = [sem("ld_x27a"), sem("ld_x27b")]
        ld_x28 = [sem("ld_x28a"), sem("ld_x28b")]
        exp_v = [sem("exp_va"), sem("exp_vb")]
        exp_k = [sem("exp_ka"), sem("exp_kb"), sem("exp_kc"), sem("exp_kd")]
        exp_wl = [sem("exp_wla"), sem("exp_wlb")]
        cc_k1 = sem("cc_k1"); cc_k2 = sem("cc_k2"); cc_v = sem("cc_v"); cc_wl = sem("cc_wl")
        st_ao = [sem("st_aoa"), sem("st_aob")]
        st_x2 = [sem("st_x2a"), sem("st_x2b")]
        st_out = [sem("st_outa"), sem("st_outb")]
        ge = sem("ge"); ve0 = sem("ve0")
        ve1 = sem("ve1"); se1 = sem("se1"); pe1 = sem("pe1"); ve1b = sem("ve1b")
        pe_bb = sem("pe_bb"); ve_bb = sem("ve_bb")
        pe2a = sem("pe2a"); se2a = sem("se2a"); pe_vt = sem("pe_vt"); se_vx = sem("se_vx")
        pe2b = sem("pe2b"); se2b = sem("se2b")
        ve_dq = sem("ve_dq"); pe_wt = sem("pe_wt"); ve_wt = sem("ve_wt")
        pe_sc = sem("pe_sc"); pe_avz = sem("pe_avz"); pe_rzb = sem("pe_rzb")
        se_exp = sem("se_exp"); se5q = sem("se5q")
        ve_mask = sem("ve_mask"); ve_rz = sem("ve_rz"); ve_fin = sem("ve_fin")
        pe6 = sem("pe6"); ve6 = sem("ve6")
        ve7 = sem("ve7"); se7 = sem("se7"); pe7 = sem("pe7"); ve7b = sem("ve7b")
        pe8 = sem("pe8"); ve8 = sem("ve8"); ve_mid = sem("ve_mid")

        block = es.enter_context(nc.Block())

        def dq_chunk_done(ot, c_local):
            return ot * VE_DQ_OT + 5 + 4 * (c_local + 1)

        @block.sync
        def _(s):
            s.dma_start(out=masks, in_=masks_in).then_inc(ld_misc, 16)
            s.dma_start(out=bqs, in_=bq_t).then_inc(ld_misc, 16)
            s.dma_start(out=bk_sb, in_=bk_t).then_inc(ld_misc, 16)
            s.dma_start(out=bv_sb, in_=bv_t).then_inc(ld_misc, 16)
            for tt in range(8):
                if tt >= 2:
                    s.wait_ge(ve1, VE1_TT * (tt - 1))
                s.dma_start(out=x_io[tt % 2],
                            in_=x_own[128 * tt:128 * (tt + 1), :]).then_inc(ld_x1[tt % 2], 16)
            for mt in range(32):
                s.wait_ge(se_vx, mt + 1)
                for tt in range(8):
                    s.dma_start(out=vag_in[128 * tt:128 * (tt + 1), 128 * mt:128 * (mt + 1)],
                                in_=vexp[mt % 2][:, 128 * tt:128 * (tt + 1)]).then_inc(exp_v[mt % 2], 16)
                if mt % 2 == 1:
                    G = mt // 2
                    s.wait_ge(ve_wt, G + 1)
                    ot, gq = G // 4, G % 4
                    for j in range(8):
                        it = 8 * gq + j
                        s.dma_start(out=wlag_in[128 * it:128 * (it + 1), 128 * ot:128 * (ot + 1)],
                                    in_=wlt_ev[G % 2][:, 128 * j:128 * (j + 1)]).then_inc(exp_wl[G % 2], 16)
            for mt in range(32):
                s.wait_ge(se2b, mt + 1)
                half = mt // 16
                mloc = mt % 16
                kag = kag_inA if half == 0 else kag_inB
                s.dma_start(out=kag[128 * mloc:128 * (mloc + 1), :],
                            in_=kexp[mt % 2]).then_inc(exp_k[2 * half + mt % 2], 16)
            kr2A = kag_outA.rearrange("(r f) (j w t) -> r f j w t", r=2, j=4, w=2, t=128)
            kr2B = kag_outB.rearrange("(r f) (j w t) -> r f j w t", r=2, j=4, w=2, t=128)
            vr = vag_out.rearrange("(r j w p) f -> r p j w f", r=2, j=4, w=2, p=128)
            for H in range(NH):
                par = H % 2
                if H == 0:
                    s.wait_ge(cc_k1, 1)
                    s.wait_ge(cc_v, 1)
                if H == 16:
                    s.wait_ge(cc_k2, 1)
                if H < 2:
                    s.wait_ge(ve_dq, 4 * VE_DQ_OT)
                    s.wait_ge(ve_wt, 16)
                if H >= 2:
                    s.wait_ge(pe_sc, 16 * (H - 1))
                    s.wait_ge(pe_avz, 16 * (H - 1))
                for rp in range(2):
                    for w in range(2):
                        goff = (384 * w) if rp == 0 else (128 + 128 * w)
                        dk = k_sb[par].rearrange("p (j rest) -> p j rest", j=4)[:, :, goff:goff + 128]
                        kr2h = kr2A if H < 16 else kr2B
                        Hl = H % 16
                        s.dma_start(out=dk, in_=kr2h[rp, 128 * Hl:128 * (Hl + 1), :, w, :]).then_inc(ld_kv[par], 16)
                        dv = v_sb[par].rearrange("p (j rest) -> p j rest", j=4)[:, :, goff:goff + 128]
                        s.dma_start(out=dv, in_=vr[rp, :, :, w, 128 * H:128 * (H + 1)]).then_inc(ld_kv[par], 16)
            for c in range(4):
                if c == 0:
                    s.wait_ge(ve_fin, 3 * NH)
                if c > 0:
                    s.wait_ge(pe_bb, c)
                s.dma_start(out=rzrow[0:1, :],
                            in_=bo_row[0:1, 1024 * c:1024 * (c + 1)]).then_inc(ld_misc, 16)
            for oc in range(4):
                for tt in range(8):
                    i = 8 * oc + tt
                    if i == 0:
                        s.wait_ge(st_ao[0], 16 * 8 * NH // 2)
                        s.wait_ge(st_ao[1], 16 * 8 * NH // 2)
                        s.wait_ge(se5q, 4 * NHG)
                    if i >= 2:
                        s.wait_ge(pe6, i - 1)
                    s.dma_start(out=aost[tt % 2],
                                in_=ao_scr[tt].rearrange("p it t -> p (it t)")).then_inc(ld_ao[i % 2], 16)
                    if i >= 2:
                        s.wait_ge(ve6, i - 1)
                    s.dma_start(out=xch[tt % 2],
                                in_=x_own[128 * tt:128 * (tt + 1), 1024 * oc:1024 * (oc + 1)]).then_inc(ld_x6[i % 2], 16)
            for tt in range(8):
                s.wait_ge(st_x2[0], 16 * 16)
                s.wait_ge(st_x2[1], 16 * 16)
                if tt < 2:
                    s.wait_ge(pe6, 32)
                    s.wait_ge(ve6, 32)
                if tt >= 2:
                    s.wait_ge(ve7, VE1_TT * (tt - 1))
                s.dma_start(out=x_io[tt % 2],
                            in_=x2_scr[128 * tt:128 * (tt + 1), :]).then_inc(ld_x27[tt % 2], 16)
            for c in range(4):
                if c == 0:
                    s.wait_ge(ve7b, 32)
                s.wait_ge(pe_bb, 4 + c)
                s.dma_start(out=rzrow[0:1, :],
                            in_=bf_row[0:1, 1024 * c:1024 * (c + 1)]).then_inc(ld_misc, 16)
            for rp in range(4):
                for tt in range(8):
                    i = 8 * rp + tt
                    if i < 2:
                        s.wait_ge(ve7, VE1_TT * 8)
                    if i >= 2:
                        s.wait_ge(ve8, i - 1)
                    s.dma_start(out=x2ch[tt % 2],
                                in_=x2_scr[128 * tt:128 * (tt + 1), 1024 * rp:1024 * (rp + 1)]).then_inc(ld_x28[i % 2], 16)
            s.wait_ge(st_out[0], 16 * 16)
            s.wait_ge(st_out[1], 16 * 16)

        @block.gpsimd
        def _(g):
            g.iota(iota_i, pattern=[[1, 128]], base=0, channel_multiplier=-1).then_inc(ge, 1)
            qw_r = qw_own.rearrange("pl (t r) w -> t r pl w", t=4, r=128)
            g.dma_start(out=qw_i32, in_=qw_r[0]).then_inc(ld_qw, 16)
            g.dma_start(out=lut_t, in_=lut_own[0:128, :]).then_inc(ld_qw, 16)
            for mt in range(32):
                if mt < 2:
                    g.wait_ge(ve1, VE1_TT * 8)
                    g.wait_ge(pe1, 32)
                if mt >= 2:
                    g.wait_ge(pe2a, mt - 1)
                g.dma_start(out=wcol[mt % 2], in_=wv4[mt]).then_inc(ld_wv[mt % 2], 16)
                if mt in (7, 15, 23):
                    ot = (mt + 1) // 8
                    g.wait_ge(ve_dq, ot * VE_DQ_OT)
                    g.dma_start(out=qw_i32, in_=qw_r[ot]).then_inc(ld_qw, 16)
                    g.dma_start(out=lut_t, in_=lut_own[128 * ot:128 * (ot + 1), :]).then_inc(ld_qw, 16)
            g.wait_ge(exp_v[0], 16 * 8 * 16)
            g.wait_ge(exp_v[1], 16 * 8 * 16)
            g.collective_compute("AllGather", AluOpType.bypass,
                                 replica_groups=[[0, 1], [2, 3], [4, 5], [6, 7]],
                                 ins=[vag_in.opt()], outs=[vag_out.opt()]).then_inc(cc_v)
            for mt in range(32):
                if mt < 2:
                    g.wait_ge(pe2a, 31 + mt)
                if mt >= 2:
                    g.wait_ge(pe2b, mt - 1)
                g.dma_start(out=wcol[mt % 2], in_=wk4[mt]).then_inc(ld_wk[mt % 2], 16)
                if mt == 19:
                    g.wait_ge(exp_k[0], 16 * 8)
                    g.wait_ge(exp_k[1], 16 * 8)
                    g.collective_compute("AllGather", AluOpType.bypass,
                                         replica_groups=[[0, 1], [2, 3], [4, 5], [6, 7]],
                                         ins=[kag_inA.opt()], outs=[kag_outA.opt()]).then_inc(cc_k1)
            g.wait_ge(exp_k[2], 16 * 8)
            g.wait_ge(exp_k[3], 16 * 8)
            g.collective_compute("AllGather", AluOpType.bypass,
                                 replica_groups=[[0, 1], [2, 3], [4, 5], [6, 7]],
                                 ins=[kag_inB.opt()], outs=[kag_outB.opt()]).then_inc(cc_k2)
            for mt in range(32):
                if mt < 2:
                    g.wait_ge(pe2b, 31 + mt)
                else:
                    g.wait_ge(se5q, mt - 1)
                g.dma_start(out=wcol[mt % 2], in_=wq4[mt]).then_inc(ld_wq[mt % 2], 16)
                if mt == 8:
                    g.wait_ge(exp_wl[0], 16 * 64)
                    g.wait_ge(exp_wl[1], 16 * 64)
                    g.collective_compute("AllGather", AluOpType.bypass,
                                         replica_groups=[[0, 1, 2, 3, 4, 5, 6, 7]],
                                         ins=[wlag_in.opt()], outs=[wlag_out.opt()]).then_inc(cc_wl)
            wor = wot.rearrange("(it p) f -> it p f", it=32)
            wbr = wbig.rearrange("p (it f) -> p it f", it=32)
            for oc in range(4):
                if oc == 0:
                    g.wait_ge(pe_avz, 16 * NH)
                    g.wait_ge(st_ao[0], 16 * 8 * NH // 2)
                    g.wait_ge(st_ao[1], 16 * 8 * NH // 2)
                else:
                    g.wait_ge(pe6, 8 * oc)
                for it in range(32):
                    g.dma_start(out=wbr[:, it, :],
                                in_=wor[it][:, 1024 * oc:1024 * (oc + 1)]).then_inc(ld_wo, 16)
            wlr = wlag_out.rearrange("(r it p) o -> r it p o", r=8, it=32)
            wug = [wbig[:, 16384 * i:16384 * (i + 1)].rearrange(
                       "p (it f) -> p it f", it=32) for i in range(2)]
            wug.append(wunit3.rearrange("p (it f) -> p it f", it=32))
            for u in range(8):
                if u == 2:
                    g.wait_ge(ve7, 40)
                    g.wait_ge(pe7, 32)
                    g.wait_ge(st_x2[0], 16 * 16)
                    g.wait_ge(st_x2[1], 16 * 16)
                elif u == 0:
                    g.wait_ge(cc_wl, 1)
                    g.wait_ge(pe6, 32)
                elif u >= 3:
                    g.wait_ge(pe8, 8 * ((u - 3) // 2) + 8)
                for it in range(32):
                    g.dma_start(out=wug[u % 3][:, it, :],
                                in_=wlr[u, it]).then_inc(ld_wl[u % 3], 16)

        @block.vector
        def _(v):
            v.wait_ge(ge, 1)
            v.tensor_scalar(out=ident, in0=iota_i, scalar1=0.0, scalar2=None,
                            op0=AluOpType.is_equal)
            v.tensor_scalar(out=identb, in0=iota_i, scalar1=0.0, scalar2=None,
                            op0=AluOpType.is_equal)
            v.memset(onesb[:, 0:1], 1.0)
            v.memset(onesr[0:1, :], 1.0)
            v.memset(eps_t, EPS)
            v.wait_ge(ld_misc, 64)
            v.tensor_scalar(out=bqs, in0=bqs, scalar1=SCALE, scalar2=None,
                            op0=AluOpType.mult).then_inc(ve0, 1)

            def ln_tile(tt, ld_sem, vsem, ssem, psem, x_src):
                v.wait_ge(ld_sem[tt % 2], 16 * (tt // 2 + 1))
                last = None
                for i in range(8):
                    last = v.bn_stats(bn6[:, 6 * i:6 * i + 6], x_src[:, 512 * i:512 * (i + 1)])
                last.then_inc(vsem, 1)
                v.wait_ge(vsem, VE1_TT * tt + 1)
                v.bn_aggr(mv, bn6).then_inc(vsem, 1)
                v.wait_ge(ssem, tt + 1)
                v.reciprocal(rstd, sq).then_inc(vsem, 1)
                v.wait_ge(vsem, VE1_TT * tt + 3)
                v.tensor_scalar(out=nmr, in0=mv[:, 0:1], scalar1=rstd, scalar2=-1.0,
                                op0=AluOpType.mult, op1=AluOpType.mult).then_inc(vsem, 1)
                v.wait_ge(vsem, VE1_TT * tt + 4)
                if tt >= 1:
                    v.wait_ge(psem, 4 * tt)
                v.tensor_scalar(out=xnf, in0=x_src, scalar1=rstd, scalar2=nmr,
                                op0=AluOpType.mult, op1=AluOpType.add).then_inc(vsem, 1)

            xnr = xn_dm.rearrange("p (it t) -> p it t", it=32)
            pdr = PD.rearrange("p (j t) -> p j t", j=8)
            for tt in range(8):
                ln_tile(tt, ld_x1, ve1, se1, pe1, x_io[tt % 2])
                for gq in range(4):
                    G = 4 * tt + gq
                    v.wait_ge(pe1, G + 1)
                    v.tensor_copy(xnr[:, 8 * gq:8 * gq + 8, 128 * tt:128 * (tt + 1)],
                                  pdr).then_inc(ve1b, 1)
            vcnt = 0
            for ot in range(4):
                v.wait_ge(ld_qw, 32 * (ot + 1))
                for pl in range(4):
                    last = None
                    for sh in range(8):
                        last = v.tensor_scalar(
                            out=bits[pl].rearrange("p (b sh) -> p sh b", sh=8)[:, sh, :],
                            in0=qw_u8[:, 512 * pl:512 * (pl + 1)],
                            scalar1=sh, scalar2=1,
                            op0=AluOpType.logical_shift_right,
                            op1=AluOpType.bitwise_and)
                    last.then_inc(ve_dq, 1); vcnt += 1
                v.wait_ge(ve_dq, vcnt)
                v.tensor_tensor(out=dlev,
                                in0=lut_t.rearrange("p (c two) -> p two c", two=2)[:, 1, :],
                                in1=lut_t.rearrange("p (c two) -> p two c", two=2)[:, 0, :],
                                op=AluOpType.subtract).then_inc(ve_dq, 1)
                vcnt += 1
                v.wait_ge(ve_dq, vcnt)
                for chk in range(NCHUNK):
                    C = NCHUNK * ot + chk
                    gidx = C // 2
                    sl = slice(CH * chk, CH * (chk + 1))
                    last = None
                    for c8 in range(8):
                        last = v.tensor_scalar(
                            out=tbuf[:, CH * c8:CH * (c8 + 1)], in0=bits[0][:, sl],
                            scalar1=dlev[:, c8:c8 + 1], scalar2=lut_t[:, 2 * c8:2 * c8 + 1],
                            op0=AluOpType.mult, op1=AluOpType.add)
                    last.then_inc(ve_dq, 1); vcnt += 1
                    v.wait_ge(ve_dq, vcnt)
                    for c4 in range(4):
                        last = v.select(ubuf[:, CH * c4:CH * (c4 + 1)], bits[1][:, sl],
                                        tbuf[:, CH * (2 * c4 + 1):CH * (2 * c4 + 2)],
                                        tbuf[:, CH * 2 * c4:CH * (2 * c4 + 1)], add_drain=True)
                    last.then_inc(ve_dq, 1); vcnt += 1
                    v.wait_ge(ve_dq, vcnt)
                    for c2 in range(2):
                        last = v.select(vbuf[:, CH * c2:CH * (c2 + 1)], bits[2][:, sl],
                                        ubuf[:, CH * (2 * c2 + 1):CH * (2 * c2 + 2)],
                                        ubuf[:, CH * 2 * c2:CH * (2 * c2 + 1)], add_drain=True)
                    last.then_inc(ve_dq, 1); vcnt += 1
                    v.wait_ge(ve_dq, vcnt)
                    if gidx >= 4:
                        v.wait_ge(pe_wt, gidx - 3)
                    v.select(wl_f2[gidx % 4][:, 512 * (C % 2):512 * (C % 2) + 512],
                             bits[3][:, sl], vbuf[:, CH:2 * CH], vbuf[:, 0:CH],
                             add_drain=True).then_inc(ve_dq, 1)
                    vcnt += 1
                    v.wait_ge(ve_dq, vcnt)
                    if C % 2 == 1 and gidx >= 2:
                        G = gidx - 2
                        v.wait_ge(pe_wt, G + 1)
                        if G >= 2:
                            v.wait_ge(exp_wl[G % 2], 16 * 8 * (G // 2))
                        v.tensor_copy(wlt_ev[G % 2], PD[:, 0:1024]).then_inc(ve_wt, 1)
                assert vcnt == (ot + 1) * VE_DQ_OT
            for G in (14, 15):
                v.wait_ge(pe_wt, G + 1)
                v.wait_ge(exp_wl[G % 2], 16 * 8 * (G // 2))
                v.tensor_copy(wlt_ev[G % 2], PD[:, 0:1024]).then_inc(ve_wt, 1)
            for H in range(NH):
                v.wait_ge(pe_avz, 16 * H + 8)
                v.reciprocal(rzrow[0:1, 0:512], PD[0:1, 0:512]).then_inc(ve_rz, 1)
                v.wait_ge(pe_avz, 16 * H + 16)
                v.reciprocal(rzrow[0:1, 512:1024], PD[0:1, 512:1024]).then_inc(ve_rz, 1)
                v.wait_ge(pe_rzb, 2 * H + 1)
                v.tensor_copy(rzb[:, 0:512], PD[:, 0:512]).then_inc(ve_fin, 1)
                v.wait_ge(ve_fin, 3 * H + 1)
                if H >= 2:
                    v.wait_ge(st_ao[H % 2], 16 * 8 * (H // 2))
                v.tensor_tensor(out=ao_sb[H % 2][:, 0:512], in0=PC[:, 0:512],
                                in1=rzb[:, 0:512],
                                op=AluOpType.mult).then_inc(ve_mid, 1)
                v.wait_ge(pe_rzb, 2 * H + 2)
                v.tensor_copy(rzb[:, 512:1024], PD[:, 512:1024]).then_inc(ve_fin, 1)
                v.wait_ge(ve_fin, 3 * H + 2)
                v.tensor_tensor(out=ao_sb[H % 2][:, 512:1024], in0=PC[:, 512:1024],
                                in1=rzb[:, 512:1024],
                                op=AluOpType.mult).then_inc(ve_mid, 1)
                v.wait_ge(ve_mid, 2 * H + 2)
                v.tensor_scalar(out=ao_sb[H % 2], in0=ao_sb[H % 2],
                                scalar1=bv_sb[:, H:H + 1], scalar2=None,
                                op0=AluOpType.add).then_inc(ve_fin, 1)
            for c in range(4):
                v.wait_ge(pe_bb, c + 1)
                v.tensor_copy(biasB[:, 1024 * c:1024 * (c + 1)], PD[:, 0:1024]).then_inc(ve_bb, 1)
            for oc in range(4):
                for tt in range(8):
                    i = 8 * oc + tt
                    v.wait_ge(pe6, i + 1)
                    v.wait_ge(ld_x6[i % 2], 16 * (i // 2 + 1))
                    if i >= 2:
                        v.wait_ge(st_x2[i % 2], 16 * (i // 2))
                    ps = PA if tt % 2 == 0 else PB
                    v.tensor_tensor(out=x2o[tt % 2], in0=ps[:, 0:1024],
                                    in1=biasB[:, 1024 * oc:1024 * (oc + 1)],
                                    op=AluOpType.add).then_inc(ve_mid, 1)
                    v.wait_ge(ve_mid, 64 + i + 1)
                    v.tensor_tensor(out=x2o[tt % 2], in0=x2o[tt % 2], in1=xch[tt % 2],
                                    op=AluOpType.add).then_inc(ve6, 1)
            for tt in range(8):
                ln_tile(tt, ld_x27, ve7, se7, pe7, x_io[tt % 2])
                for gq in range(4):
                    G = 4 * tt + gq
                    v.wait_ge(pe7, G + 1)
                    v.tensor_copy(xnr[:, 8 * gq:8 * gq + 8, 128 * tt:128 * (tt + 1)],
                                  pdr).then_inc(ve7b, 1)
            for c in range(4):
                v.wait_ge(pe_bb, 4 + c + 1)
                v.tensor_copy(biasB[:, 1024 * c:1024 * (c + 1)], PD[:, 0:1024]).then_inc(ve_bb, 1)
            for rp in range(4):
                for tt in range(8):
                    i = 8 * rp + tt
                    v.wait_ge(pe8, i + 1)
                    v.wait_ge(ld_x28[i % 2], 16 * (i // 2 + 1))
                    if i >= 2:
                        v.wait_ge(st_out[i % 2], 16 * (i // 2))
                    ps = PA if tt % 2 == 0 else PB
                    v.tensor_tensor(out=outb[tt % 2], in0=ps[:, 0:1024],
                                    in1=biasB[:, 1024 * rp:1024 * (rp + 1)],
                                    op=AluOpType.add).then_inc(ve_mid, 1)
                    v.wait_ge(ve_mid, 96 + i + 1)
                    v.tensor_tensor(out=outb[tt % 2], in0=outb[tt % 2], in1=x2ch[tt % 2],
                                    op=AluOpType.add).then_inc(ve8, 1)

        @block.scalar
        def _(sc):
            sc.wait_ge(ve0, 1)
            for tt in range(8):
                sc.wait_ge(ve1, VE1_TT * tt + 2)
                sc.activation(sq, mv[:, 1:2], AF.Sqrt, bias=eps_t, scale=1.0).then_inc(se1, 1)
            for mt in range(32):
                sc.wait_ge(pe2a, mt + 1)
                if mt >= 2:
                    sc.wait_ge(pe_vt, mt - 1)
                ps = (PA, PB, PC)[mt % 3]
                sc.activation(xvf[mt % 2], ps[:, 0:1024], AF.Identity).then_inc(se2a, 1)
                sc.wait_ge(pe_vt, mt + 1)
                if mt >= 2:
                    sc.wait_ge(exp_v[mt % 2], 16 * 8 * (mt // 2))
                sc.activation(vexp[mt % 2], PD[:, 0:1024], AF.Identity).then_inc(se_vx, 1)
            for mt in range(32):
                sc.wait_ge(pe2b, mt + 1)
                if mt >= 2:
                    pm = mt - 2
                    sc.wait_ge(exp_k[2 * (pm // 16) + pm % 2], 16 * ((pm % 16) // 2 + 1))
                ps = (PA, PB, PC)[mt % 3]
                sc.activation(kexp[mt % 2], ps[:, 0:1024], AF.Identity,
                              bias=bk_sb[:, mt:mt + 1], scale=1.0).then_inc(se2b, 1)
            for hg in range(NHG):
                for mtl in range(4):
                    mt = 4 * hg + mtl
                    sc.wait_ge(pe2b, 32 + mt + 1)
                    if hg >= 2 and mtl == 0:
                        sc.wait_ge(pe_sc, 16 * 4 * (hg - 1))
                    ps = PA if mt % 2 == 0 else PB
                    sc.activation(Q_hg[hg % 2][:, 1024 * mtl:1024 * (mtl + 1)], ps[:, 0:1024],
                                  AF.Identity, bias=bqs[:, mt:mt + 1], scale=SCALE).then_inc(se5q, 1)
                for hl in range(4):
                    H = 4 * hg + hl
                    for kt in range(16):
                        if kt == 6 and H >= 1:
                            Hp = H - 1
                            sc.wait_ge(ve_fin, 3 * Hp + 3)
                            for tt in range(8):
                                sc.dma_start(out=ao_scr[tt, :, Hp, :],
                                             in_=ao_sb[Hp % 2][:, 128 * tt:128 * (tt + 1)]).then_inc(st_ao[Hp % 2], 16)
                        sc.wait_ge(pe_sc, 16 * H + kt + 1)
                        if H >= 1:
                            sc.wait_ge(pe_avz, 16 * (H - 1) + kt + 1)
                        slot = PA if kt % 2 == 0 else PB
                        q0 = q0_of(kt)
                        sc.activation(attT[:, ATO[kt]:ATO[kt] + ATW[kt]],
                                      slot[:, q0:1024], AF.Exp).then_inc(se_exp, 1)
            sc.wait_ge(ve_fin, 3 * NH)
            for tt in range(8):
                sc.dma_start(out=ao_scr[tt, :, NH - 1, :],
                             in_=ao_sb[(NH - 1) % 2][:, 128 * tt:128 * (tt + 1)]).then_inc(st_ao[(NH - 1) % 2], 16)
            for oc in range(4):
                for tt in range(8):
                    i = 8 * oc + tt
                    sc.wait_ge(ve6, i + 1)
                    sc.dma_start(out=x2_scr[128 * tt:128 * (tt + 1), 1024 * oc:1024 * (oc + 1)],
                                 in_=x2o[tt % 2]).then_inc(st_x2[i % 2], 16)
            for tt in range(8):
                sc.wait_ge(ve7, VE1_TT * tt + 2)
                sc.activation(sq, mv[:, 1:2], AF.Sqrt, bias=eps_t, scale=1.0).then_inc(se7, 1)
            for rp in range(4):
                for tt in range(8):
                    i = 8 * rp + tt
                    sc.wait_ge(ve8, i + 1)
                    sc.dma_start(out=out_own[128 * tt:128 * (tt + 1), 1024 * rp:1024 * (rp + 1)],
                                 in_=outb[tt % 2]).then_inc(st_out[i % 2], 16)
            sc.wait_ge(st_out[0], 16 * 16)
            sc.wait_ge(st_out[1], 16 * 16)

        @block.tensor
        def _(te):
            te.wait_ge(ve0, 1)
            for tt in range(8):
                te.wait_ge(ve1, VE1_TT * (tt + 1))
                for gq in range(4):
                    G = 4 * tt + gq
                    if G > 0:
                        te.wait_ge(ve1b, G)
                    last = None
                    for j in range(8):
                        d = 8 * gq + j
                        last = te.transpose(PD[:, 128 * j:128 * (j + 1)],
                                            xnf[:, 128 * d:128 * (d + 1)], ident)
                    last.then_inc(pe1, 1)
            prev_pd = None
            for mt in range(33):
                if mt < 32:
                    te.wait_ge(ld_wv[mt % 2], 16 * (mt // 2 + 1))
                    if mt == 0:
                        te.wait_ge(ve1b, 32)
                    if mt >= 3:
                        te.wait_ge(se2a, mt - 2)
                    ps = (PA, PB, PC)[mt % 3]
                    last = None
                    for it in range(32):
                        for (c0, c1) in ((0, 512), (512, 1024)):
                            last = te.matmul(ps[:, c0:c1], wcol[mt % 2][:, 128 * it:128 * (it + 1)],
                                             xn_dm[:, 1024 * it + c0:1024 * it + c1],
                                             start=(it == 0), stop=(it == 31))
                    last.then_inc(pe2a, 1)
                if mt >= 1:
                    mp = mt - 1
                    te.wait_ge(se2a, mp + 1)
                    if prev_pd is not None:
                        te.wait_ge(*prev_pd)
                    last = None
                    for j in range(8):
                        last = te.transpose(PD[:, 128 * j:128 * (j + 1)],
                                            xvf[mp % 2][:, 128 * j:128 * (j + 1)], ident)
                    last.then_inc(pe_vt, 1)
                    prev_pd = (se_vx, mp + 1)
                    if mp % 2 == 1 and mp // 2 < 16:
                        G = mp // 2
                        te.wait_ge(ve_dq, dq_chunk_done(G // 4, 2 * (G % 4) + 1))
                        te.wait_ge(*prev_pd)
                        last = None
                        for j in range(8):
                            last = te.transpose(PD[:, 128 * j:128 * (j + 1)],
                                                wl_f2[G % 4][:, 128 * j:128 * (j + 1)], ident)
                        last.then_inc(pe_wt, 1)
                        prev_pd = (ve_wt, G + 1)
            for mt in range(32):
                te.wait_ge(ld_wk[mt % 2], 16 * (mt // 2 + 1))
                if mt < 3:
                    te.wait_ge(se2a, (31, 32, 30)[mt])
                if mt >= 3:
                    te.wait_ge(se2b, mt - 2)
                ps = (PA, PB, PC)[mt % 3]
                last = None
                for it in range(32):
                    for (c0, c1) in ((0, 512), (512, 1024)):
                        last = te.matmul(ps[:, c0:c1], wcol[mt % 2][:, 128 * it:128 * (it + 1)],
                                         xn_dm[:, 1024 * it + c0:1024 * it + c1],
                                         start=(it == 0), stop=(it == 31))
                last.then_inc(pe2b, 1)
            for hg in range(NHG):
                for mtl in range(4):
                    mt = 4 * hg + mtl
                    te.wait_ge(ld_wq[mt % 2], 16 * (mt // 2 + 1))
                    if hg == 0 and mtl < 2:
                        te.wait_ge(se2b, 31 + mtl)
                        te.wait_ge(ve_wt, 16)
                    if hg >= 1 and mtl < 2:
                        te.wait_ge(se_exp, 16 * 4 * hg)
                    if mtl >= 2:
                        te.wait_ge(se5q, 4 * hg + mtl - 1)
                    ps = PA if mt % 2 == 0 else PB
                    last = None
                    for it in range(32):
                        for (c0, c1) in ((0, 512), (512, 1024)):
                            last = te.matmul(ps[:, c0:c1], wcol[mt % 2][:, 128 * it:128 * (it + 1)],
                                             xn_dm[:, 1024 * it + c0:1024 * it + c1],
                                             start=(it == 0), stop=(it == 31))
                    last.then_inc(pe2b, 1)
                def mk_avz(H):
                    def avz_pair(s_):
                        if s_ == 0:
                            if H >= 1:
                                te.wait_ge(ve_mid, 2 * H)
                            else:
                                te.wait_ge(ve_wt, 16)
                        for kt in (2 * s_, 2 * s_ + 1):
                            te.wait_ge(se_exp, 16 * H + 2 * s_ + 2)
                            lastm = None
                            for (c0, c1) in q_ranges(kt):
                                stop = (kt == 7) if c1 <= 512 else (kt == 15)
                                lastm = te.matmul(
                                    PC[:, c0:c1], v_sb[H % 2][:, 128 * kt:128 * (kt + 1)],
                                    attT[:, ATO[kt] + c0 - q0_of(kt):ATO[kt] + c1 - q0_of(kt)],
                                    start=(kt == 0), stop=stop)
                            for (c0, c1) in q_ranges(kt):
                                stop = (kt == 7) if c1 <= 512 else (kt == 15)
                                lastm = te.matmul(
                                    PD[0:1, c0:c1], onesb[:, 0:1],
                                    attT[:, ATO[kt] + c0 - q0_of(kt):ATO[kt] + c1 - q0_of(kt)],
                                    start=(kt == 0), stop=stop)
                            lastm.then_inc(pe_avz, 1)
                    return avz_pair

                def rz_bcast(Hp, h2):
                        te.wait_ge(ve_rz, 2 * Hp + h2 + 1)
                        te.matmul(PD[:, 512 * h2:512 * (h2 + 1)], onesr[0:1, :],
                                  rzrow[0:1, 512 * h2:512 * (h2 + 1)],
                                  start=True, stop=True).then_inc(pe_rzb, 1)

                for hl in range(4):
                    H = 4 * hg + hl
                    te.wait_ge(ld_kv[H % 2], 16 * 8 * (H // 2 + 1))
                    avz_pair = mk_avz(H)
                    for s_ in range(8):
                        for kt in (2 * s_, 2 * s_ + 1):
                            tgt = 16 * H + kt - 1
                            if tgt > 0:
                                te.wait_ge(se_exp, tgt)
                            if kt == 0:
                                te.wait_ge(se5q, 4 * (hg + 1))
                            slot = PA if kt % 2 == 0 else PB
                            s_of = kt // 2
                            j_of = kt % 2
                            last = None
                            for (c0, c1) in q_ranges(kt):
                                has_mask = (c0 == q0_of(kt))
                                last = te.matmul(slot[:, c0:c1],
                                                 k_sb[H % 2][:, 128 * kt:128 * (kt + 1)],
                                                 Q_hg[hg % 2][:, 1024 * hl + c0:1024 * hl + c1],
                                                 start=True, stop=not has_mask)
                                if has_mask:
                                    last = te.matmul(
                                        slot[:, c0:c0 + 128], identb,
                                        masks[:, 256 * s_of + 128 * j_of:256 * s_of + 128 * (j_of + 1)],
                                        start=False, stop=True)
                            last.then_inc(pe_sc, 1)
                        if s_ == 1 and H >= 1:
                            rz_bcast(H - 1, 0)
                        if s_ == 2 and H >= 1:
                            rz_bcast(H - 1, 1)
                        if s_ >= 2:
                            avz_pair(s_ - 2)
                    avz_pair(6)
                    avz_pair(7)
                    if H == NH - 1:
                        rz_bcast(H, 0)
                        rz_bcast(H, 1)
            for c in range(4):
                if c == 0:
                    te.wait_ge(ve_fin, 3 * NH)
                te.wait_ge(ld_misc, 64 + 16 * (c + 1))
                if c > 0:
                    te.wait_ge(ve_bb, c)
                last = None
                for h2 in range(2):
                    last = te.matmul(PD[:, 512 * h2:512 * (h2 + 1)], onesr[0:1, :],
                                     rzrow[0:1, 512 * h2:512 * (h2 + 1)], start=True, stop=True)
                last.then_inc(pe_bb, 1)
            for oc in range(4):
                te.wait_ge(ld_wo, 16 * 32 * (oc + 1))
                if oc == 0:
                    te.wait_ge(ve_bb, 4)
                for tt in range(8):
                    i = 8 * oc + tt
                    te.wait_ge(ld_ao[i % 2], 16 * (i // 2 + 1))
                    if i < 2:
                        te.wait_ge(se_exp, 16 * NH)
                    if i >= 2:
                        te.wait_ge(ve6, i - 1)
                    ps = PA if tt % 2 == 0 else PB
                    last = None
                    for it in range(32):
                        for (c0, c1) in ((0, 512), (512, 1024)):
                            last = te.matmul(ps[:, c0:c1], aost[tt % 2][:, 128 * it:128 * (it + 1)],
                                             wbig[:, 1024 * it + c0:1024 * it + c1],
                                             start=(it == 0), stop=(it == 31))
                    last.then_inc(pe6, 1)
            for tt in range(8):
                te.wait_ge(ve7, VE1_TT * (tt + 1))
                for gq in range(4):
                    G = 4 * tt + gq
                    if G > 0:
                        te.wait_ge(ve7b, G)
                    else:
                        te.wait_ge(ve_bb, 4)
                    last = None
                    for j in range(8):
                        d = 8 * gq + j
                        last = te.transpose(PD[:, 128 * j:128 * (j + 1)],
                                            xnf[:, 128 * d:128 * (d + 1)], ident)
                    last.then_inc(pe7, 1)
            for c in range(4):
                if c == 0:
                    te.wait_ge(ve7b, 32)
                te.wait_ge(ld_misc, 128 + 16 * (c + 1))
                if c > 0:
                    te.wait_ge(ve_bb, 4 + c)
                last = None
                for h2 in range(2):
                    last = te.matmul(PD[:, 512 * h2:512 * (h2 + 1)], onesr[0:1, :],
                                     rzrow[0:1, 512 * h2:512 * (h2 + 1)], start=True, stop=True)
                last.then_inc(pe_bb, 1)
            wut = [wbig[:, 16384 * i:16384 * (i + 1)].rearrange(
                       "p (it f) -> p it f", it=32) for i in range(2)]
            wut.append(wunit3.rearrange("p (it f) -> p it f", it=32))
            for rp in range(4):
                if rp == 0:
                    te.wait_ge(ve_bb, 8)
                    te.wait_ge(ve6, 32)
                for tt in range(8):
                    i = 8 * rp + tt
                    ps = PA if tt % 2 == 0 else PB
                    if i >= 2:
                        te.wait_ge(ve8, i - 1)
                    last = None
                    for rb in range(2):
                        u = 2 * rp + rb
                        if tt == 0:
                            te.wait_ge(ld_wl[u % 3], 16 * 32 * (u // 3 + 1))
                        for it in range(32):
                            last = te.matmul(
                                ps[:, 512 * rb:512 * (rb + 1)],
                                xn_dm[:, 1024 * it + 128 * tt:1024 * it + 128 * (tt + 1)],
                                wut[u % 3][:, it, :],
                                start=(it == 0), stop=(it == 31))
                    last.then_inc(pe8, 1)
    return nc


def kernel(L0_attn_x, L0_attn_Wq, L0_attn_bq, L0_attn_Wk, L0_attn_bk,
           L0_attn_Wv, L0_attn_bv, L0_attn_Wo, L0_attn_bo,
           L0_ffn_qweight, L0_ffn_lut4, L0_ffn_lut8, L0_ffn_bias):
    x = np.asarray(L0_attn_x, dtype=np.float32)
    Wq = np.asarray(L0_attn_Wq, dtype=np.float32)
    Wk = np.asarray(L0_attn_Wk, dtype=np.float32)
    Wv = np.asarray(L0_attn_Wv, dtype=np.float32)
    Wo = np.asarray(L0_attn_Wo, dtype=np.float32)
    bq = np.asarray(L0_attn_bq, dtype=np.float32)
    bk = np.asarray(L0_attn_bk, dtype=np.float32)
    bv = np.asarray(L0_attn_bv, dtype=np.float32)
    bo = np.asarray(L0_attn_bo, dtype=np.float32)
    qw = np.asarray(L0_ffn_qweight)
    lut4 = np.asarray(L0_ffn_lut4, dtype=np.float32)
    bfb = np.asarray(L0_ffn_bias, dtype=np.float32)

    wq4 = np.ascontiguousarray(Wq.reshape(32, 128, 32, 128).transpose(0, 3, 2, 1))
    wk4 = np.ascontiguousarray(Wk.reshape(32, 128, 32, 128).transpose(0, 3, 2, 1))
    wv4 = np.ascontiguousarray(Wv.reshape(32, 128, 32, 128).transpose(0, 3, 2, 1))
    wot = np.ascontiguousarray(Wo.T)
    bq_t = np.ascontiguousarray(bq.reshape(32, 128).T)
    bk_t = np.ascontiguousarray(bk.reshape(32, 128).T)
    bv_t = np.ascontiguousarray(bv.reshape(32, 128).T)
    bo_row = np.ascontiguousarray(bo.reshape(1, 4096))
    bf_row = np.ascontiguousarray(bfb.reshape(1, 4096))

    def build_mask(parity):
        QT = QT_EVEN if parity == 0 else QT_ODD
        m = np.zeros((128, 2048), np.float32)
        for s in range(8):
            qt = QT[s]
            for j in range(2):
                kt = 2 * s + j
                if kt < qt:
                    t = np.zeros((128, 128), np.float32)
                elif kt == qt:
                    t = np.where(np.arange(128)[:, None] <= np.arange(128)[None, :],
                                 0.0, -30000.0).astype(np.float32)
                else:
                    t = np.full((128, 128), -30000.0, np.float32)
                m[:, 256 * s + 128 * j:256 * s + 128 * (j + 1)] = t
        return m.astype(BF)

    mask_by_par = [build_mask(0), build_mask(1)]

    nc = build_nc()
    in_maps = []
    for core in range(8):
        b = core // 2
        parity = core % 2
        QT = QT_EVEN if parity == 0 else QT_ODD
        rows = np.concatenate([128 * qt + np.arange(128) for qt in QT])
        in_maps.append({
            "x_own": np.ascontiguousarray(x[b][rows]),
            "wq4": wq4, "wk4": wk4, "wv4": wv4, "wot": wot,
            "bq_t": bq_t, "bk_t": bk_t, "bv_t": bv_t,
            "bo_row": bo_row, "bf_row": bf_row,
            "qw_own": np.ascontiguousarray(qw[0:4, 512 * core:512 * (core + 1), :]),
            "lut_own": np.ascontiguousarray(lut4[512 * core:512 * (core + 1), :]),
            "masks_in": mask_by_par[parity],
        })
    res = run_bass_kernel_spmd(nc, in_maps, core_ids=list(range(8)))
    out = np.empty((4, 2048, 4096), dtype=np.float32)
    for core in range(8):
        b = core // 2
        parity = core % 2
        QT = QT_EVEN if parity == 0 else QT_ODD
        rows = np.concatenate([128 * qt + np.arange(128) for qt in QT])
        out[b][rows] = res.results[core]["out_own"]
    return out


# version 15 (final)
# speedup vs baseline: 1.1350x; 1.0391x over previous
import contextlib
import math
import numpy as np
import ml_dtypes
import concourse.bass as bass
import concourse.mybir as mybir
from concourse.bass_utils import run_bass_kernel_spmd
from concourse.alu_op_type import AluOpType

F32 = mybir.dt.float32
BF16 = mybir.dt.bfloat16
I32 = mybir.dt.int32
U8 = mybir.dt.uint8
AF = mybir.ActivationFunctionType
BF = ml_dtypes.bfloat16

QT_EVEN = [0, 3, 4, 7, 8, 11, 12, 15]
QT_ODD = [1, 2, 5, 6, 9, 10, 13, 14]
SCALE = 1.0 / math.sqrt(128.0)
EPS = 1e-5
NH = 32
NHG = 8
CH = 512
NCHUNK = 4096 // CH
VE_DQ_OT = 4 + 1 + NCHUNK * 4
VE1_TT = 5

def q0_of(kt):
    return 128 * (kt // 2)
ATW = [1024 - q0_of(kt) for kt in range(16)]
ATO = [0] * 16
for _kt in range(1, 16):
    ATO[_kt] = ATO[_kt - 1] + ATW[_kt - 1]
ATT_TOT = ATO[15] + ATW[15]

def q_ranges(kt):
    q0 = q0_of(kt)
    return [(q0, 512), (512, 1024)] if q0 < 512 else [(q0, 1024)]


def build_nc():
    nc = bass.Bass("TRN2", target_bir_lowering=False)
    x_own = nc.dram_tensor("x_own", [1024, 4096], F32, kind="ExternalInput").ap()
    wq4 = nc.dram_tensor("wq4", [32, 128, 32, 128], F32, kind="ExternalInput").ap()
    wk4 = nc.dram_tensor("wk4", [32, 128, 32, 128], F32, kind="ExternalInput").ap()
    wv4 = nc.dram_tensor("wv4", [32, 128, 32, 128], F32, kind="ExternalInput").ap()
    wot = nc.dram_tensor("wot", [4096, 4096], F32, kind="ExternalInput").ap()
    bq_t = nc.dram_tensor("bq_t", [128, 32], F32, kind="ExternalInput").ap()
    bk_t = nc.dram_tensor("bk_t", [128, 32], F32, kind="ExternalInput").ap()
    bv_t = nc.dram_tensor("bv_t", [128, 32], F32, kind="ExternalInput").ap()
    bo_row = nc.dram_tensor("bo_row", [1, 4096], F32, kind="ExternalInput").ap()
    bf_row = nc.dram_tensor("bf_row", [1, 4096], F32, kind="ExternalInput").ap()
    qw_own = nc.dram_tensor("qw_own", [4, 512, 128], I32, kind="ExternalInput").ap()
    lut_own = nc.dram_tensor("lut_own", [512, 16], F32, kind="ExternalInput").ap()
    masks_in = nc.dram_tensor("masks_in", [128, 2048], BF16, kind="ExternalInput").ap()
    out_own = nc.dram_tensor("out_own", [1024, 4096], F32, kind="ExternalOutput").ap()
    kag_inA = nc.dram_tensor("kag_inA", [2048, 1024], BF16).ap()
    kag_inB = nc.dram_tensor("kag_inB", [2048, 1024], BF16).ap()
    kag_outA = nc.dram_tensor("kag_outA", [4096, 1024], BF16).ap()
    kag_outB = nc.dram_tensor("kag_outB", [4096, 1024], BF16).ap()
    vag_in = nc.dram_tensor("vag_in", [1024, 4096], BF16).ap()
    vag_out = nc.dram_tensor("vag_out", [2048, 4096], BF16).ap()
    wlag_in = nc.dram_tensor("wlag_in", [4096, 512], BF16).ap()
    wlag_out = nc.dram_tensor("wlag_out", [32768, 512], BF16).ap()
    ao_scr = nc.dram_tensor("ao_scr", [8, 128, 32, 128], BF16).ap()
    x2_scr = nc.dram_tensor("x2_scr", [1024, 4096], F32).ap()

    es = contextlib.ExitStack()
    with es:
        AR = es.enter_context(nc.sbuf_tensor("AR", [128, 204 * 1024], U8))
        arf, arb, ari, aru = AR.bitcast(F32), AR.bitcast(BF16), AR.bitcast(I32), AR.bitcast(U8)

        def fsl(o, n):
            assert o % 64 == 0
            return arf.ap()[:, o // 4:o // 4 + n]

        def bsl(o, n):
            assert o % 64 == 0
            return arb.ap()[:, o // 2:o // 2 + n]

        def isl(o, n):
            assert o % 64 == 0
            return ari.ap()[:, o // 4:o // 4 + n]

        def usl(o, n):
            assert o % 64 == 0
            return aru.ap()[:, o:o + n]

        masks = bsl(0, 2048)
        biasB = bsl(4096, 4096)
        rzb = fsl(12288, 1024)
        rzrow = fsl(16384, 1024)
        ident = fsl(20480, 128)
        identb = bsl(23040, 128)
        iota_i = isl(20992, 128)
        onesr = fsl(21504, 128)
        onesb = bsl(22016, 32)
        bqs = fsl(22080, 32)
        bk_sb = fsl(22208, 32)
        bv_sb = fsl(22336, 32)
        bn6 = fsl(22464, 48)
        mv = fsl(22656, 2)
        sq = fsl(22720, 1)
        rstd = fsl(22784, 1)
        nmr = fsl(22848, 1)
        eps_t = fsl(22912, 1)
        S1 = 24 * 1024
        xn_dm = bsl(S1, 32768)
        S2 = 88 * 1024
        bits = [usl(S2 + 4096 * i, 4096) for i in range(4)]
        qw_i32 = isl(S2 + 16384, 512)
        qw_u8 = usl(S2 + 16384, 2048)
        lut_t = fsl(S2 + 18432, 16)
        dlev = fsl(S2 + 18496, 8)
        tbuf = fsl(S2 + 18560, 8 * CH)
        ubuf = fsl(S2 + 34944, 4 * CH)
        vbuf = fsl(S2 + 43136, 2 * CH)
        wl_f2 = [fsl(S2 + 47232 + 4096 * i, 1024) for i in range(4)]
        wlt_ev = [bsl(4096 + 2048 * i, 1024) for i in range(2)]
        attT = bsl(S2, ATT_TOT)
        Q_hg = [bsl(S2 + 18432 + 8192 * i, 4096) for i in range(2)]
        k_sb = [bsl(S2 + 34816 + 4096 * i, 2048) for i in range(2)]
        v_sb = [bsl(S2 + 43008 + 4096 * i, 2048) for i in range(2)]
        ao_sb = [bsl(S2 + 51200 + 2048 * i, 1024) for i in range(2)]
        wbig = bsl(S2, 32768)
        R3 = 152 * 1024
        vexp = [bsl(R3 + 24576 + 2048 * i, 1024) for i in range(2)]
        xvf = [fsl(R3 + 16384 + 4096 * i, 1024) for i in range(2)]
        x_io = [fsl(R3 + 16384 * i, 4096) for i in range(2)]
        xnf = fsl(R3 + 32768, 4096)
        wcol = [bsl(R3 + 8192 * i, 4096) for i in range(2)]
        kexp = [bsl(R3 + 16384 + 2048 * i, 1024) for i in range(2)]
        aost = [bsl(R3 + 8192 * i, 4096) for i in range(2)]
        xch = [fsl(R3 + 16384 + 4096 * i, 1024) for i in range(2)]
        x2o = [fsl(R3 + 24576 + 4096 * i, 1024) for i in range(2)]
        x2ch = [fsl(R3 + 4096 * i, 1024) for i in range(2)]
        outb = [fsl(R3 + 8192 + 4096 * i, 1024) for i in range(2)]
        wunit3 = bsl(R3 + 16384, 16384)

        PA = es.enter_context(nc.psum_tensor("PA", [128, 1024], F32))
        PB = es.enter_context(nc.psum_tensor("PB", [128, 1024], F32))
        PC = es.enter_context(nc.psum_tensor("PC", [128, 1024], F32))
        PD = es.enter_context(nc.psum_tensor("PD", [128, 1024], F32))

        def sem(name):
            return es.enter_context(nc.semaphore(name))
        ld_misc = sem("ld_misc"); ld_qw = sem("ld_qw")
        ld_x1 = [sem("ld_x1a"), sem("ld_x1b")]
        ld_wv = [sem("ld_wva"), sem("ld_wvb")]
        ld_wk = [sem("ld_wka"), sem("ld_wkb")]
        ld_wq = [sem("ld_wqa"), sem("ld_wqb")]
        ld_wo = [sem("ld_woa"), sem("ld_wob"), sem("ld_woc"), sem("ld_wod")]
        ld_wl = [sem("ld_wla"), sem("ld_wlb"), sem("ld_wlc")]
        ld_kv = [sem("ld_kva"), sem("ld_kvb")]
        ld_ao = [sem("ld_aoa"), sem("ld_aob")]
        ld_x6 = [sem("ld_x6a"), sem("ld_x6b")]
        ld_x27 = [sem("ld_x27a"), sem("ld_x27b")]
        ld_x28 = [sem("ld_x28a"), sem("ld_x28b")]
        exp_v = [sem("exp_va"), sem("exp_vb")]
        exp_k = [sem("exp_ka"), sem("exp_kb"), sem("exp_kc"), sem("exp_kd")]
        exp_wl = [sem("exp_wla"), sem("exp_wlb")]
        cc_k1 = sem("cc_k1"); cc_k2 = sem("cc_k2"); cc_v = sem("cc_v"); cc_wl = sem("cc_wl")
        st_ao = [sem("st_aoa"), sem("st_aob")]
        st_x2 = [sem("st_x2a"), sem("st_x2b")]
        st_out = [sem("st_outa"), sem("st_outb")]
        ge = sem("ge"); ve0 = sem("ve0")
        ve1 = sem("ve1"); se1 = sem("se1"); pe1 = sem("pe1"); ve1b = sem("ve1b")
        pe_bb = sem("pe_bb"); ve_bb = sem("ve_bb")
        pe2a = sem("pe2a"); se2a = sem("se2a"); pe_vt = sem("pe_vt"); se_vx = sem("se_vx")
        pe2b = sem("pe2b"); se2b = sem("se2b")
        ve_dq = sem("ve_dq"); pe_wt = sem("pe_wt"); ve_wt = sem("ve_wt")
        pe_sc = sem("pe_sc"); pe_avz = sem("pe_avz"); pe_rzb = sem("pe_rzb")
        se_exp = sem("se_exp"); se5q = sem("se5q")
        ve_mask = sem("ve_mask"); ve_rz = sem("ve_rz"); ve_fin = sem("ve_fin")
        pe6 = sem("pe6"); ve6 = sem("ve6")
        ve7 = sem("ve7"); se7 = sem("se7"); pe7 = sem("pe7"); ve7b = sem("ve7b")
        pe8 = sem("pe8"); ve8 = sem("ve8"); ve_mid = sem("ve_mid")

        block = es.enter_context(nc.Block())

        def dq_chunk_done(ot, c_local):
            return ot * VE_DQ_OT + 5 + 4 * (c_local + 1)

        @block.sync
        def _(s):
            s.dma_start(out=masks, in_=masks_in).then_inc(ld_misc, 16)
            s.dma_start(out=bqs, in_=bq_t).then_inc(ld_misc, 16)
            s.dma_start(out=bk_sb, in_=bk_t).then_inc(ld_misc, 16)
            s.dma_start(out=bv_sb, in_=bv_t).then_inc(ld_misc, 16)
            for tt in range(8):
                if tt >= 2:
                    s.wait_ge(ve1, VE1_TT * (tt - 1))
                s.dma_start(out=x_io[tt % 2],
                            in_=x_own[128 * tt:128 * (tt + 1), :]).then_inc(ld_x1[tt % 2], 16)
            for mt in range(32):
                s.wait_ge(se_vx, mt + 1)
                for tt in range(8):
                    s.dma_start(out=vag_in[128 * tt:128 * (tt + 1), 128 * mt:128 * (mt + 1)],
                                in_=vexp[mt % 2][:, 128 * tt:128 * (tt + 1)]).then_inc(exp_v[mt % 2], 16)
                if mt % 2 == 1:
                    G = mt // 2
                    s.wait_ge(ve_wt, G + 1)
                    ot, gq = G // 4, G % 4
                    for j in range(8):
                        it = 8 * gq + j
                        s.dma_start(out=wlag_in[128 * it:128 * (it + 1), 128 * ot:128 * (ot + 1)],
                                    in_=wlt_ev[G % 2][:, 128 * j:128 * (j + 1)]).then_inc(exp_wl[G % 2], 16)
            for mt in range(32):
                s.wait_ge(se2b, mt + 1)
                half = mt // 16
                mloc = mt % 16
                kag = kag_inA if half == 0 else kag_inB
                s.dma_start(out=kag[128 * mloc:128 * (mloc + 1), :],
                            in_=kexp[mt % 2]).then_inc(exp_k[2 * half + mt % 2], 16)
            kr2A = kag_outA.rearrange("(r f) (j w t) -> r f j w t", r=2, j=4, w=2, t=128)
            kr2B = kag_outB.rearrange("(r f) (j w t) -> r f j w t", r=2, j=4, w=2, t=128)
            vr = vag_out.rearrange("(r j w p) f -> r p j w f", r=2, j=4, w=2, p=128)
            for H in range(NH):
                par = H % 2
                if H == 0:
                    s.wait_ge(cc_k1, 1)
                    s.wait_ge(cc_v, 1)
                if H == 16:
                    s.wait_ge(cc_k2, 1)
                if H < 2:
                    s.wait_ge(ve_dq, 4 * VE_DQ_OT)
                    s.wait_ge(ve_wt, 16)
                if H >= 2:
                    s.wait_ge(pe_sc, 16 * (H - 1))
                    s.wait_ge(pe_avz, 16 * (H - 1))
                for rp in range(2):
                    for w in range(2):
                        goff = (384 * w) if rp == 0 else (128 + 128 * w)
                        dk = k_sb[par].rearrange("p (j rest) -> p j rest", j=4)[:, :, goff:goff + 128]
                        kr2h = kr2A if H < 16 else kr2B
                        Hl = H % 16
                        s.dma_start(out=dk, in_=kr2h[rp, 128 * Hl:128 * (Hl + 1), :, w, :]).then_inc(ld_kv[par], 16)
                        dv = v_sb[par].rearrange("p (j rest) -> p j rest", j=4)[:, :, goff:goff + 128]
                        s.dma_start(out=dv, in_=vr[rp, :, :, w, 128 * H:128 * (H + 1)]).then_inc(ld_kv[par], 16)
            for c in range(4):
                if c == 0:
                    s.wait_ge(ve_fin, 3 * NH)
                if c > 0:
                    s.wait_ge(pe_bb, c)
                s.dma_start(out=rzrow[0:1, :],
                            in_=bo_row[0:1, 1024 * c:1024 * (c + 1)]).then_inc(ld_misc, 16)
            for oc in range(4):
                for tt in range(8):
                    i = 8 * oc + tt
                    if i == 0:
                        s.wait_ge(st_ao[0], 16 * 8 * NH // 2)
                        s.wait_ge(st_ao[1], 16 * 8 * NH // 2)
                        s.wait_ge(se5q, 4 * NHG)
                    if i >= 2:
                        s.wait_ge(pe6, i - 1)
                    s.dma_start(out=aost[tt % 2],
                                in_=ao_scr[tt].rearrange("p it t -> p (it t)")).then_inc(ld_ao[i % 2], 16)
                    if i >= 2:
                        s.wait_ge(ve6, i - 1)
                    s.dma_start(out=xch[tt % 2],
                                in_=x_own[128 * tt:128 * (tt + 1), 1024 * oc:1024 * (oc + 1)]).then_inc(ld_x6[i % 2], 16)
            for tt in range(8):
                s.wait_ge(st_x2[0], 16 * 16)
                s.wait_ge(st_x2[1], 16 * 16)
                if tt < 2:
                    s.wait_ge(pe6, 32)
                    s.wait_ge(ve6, 32)
                if tt >= 2:
                    s.wait_ge(ve7, VE1_TT * (tt - 1))
                s.dma_start(out=x_io[tt % 2],
                            in_=x2_scr[128 * tt:128 * (tt + 1), :]).then_inc(ld_x27[tt % 2], 16)
            for c in range(4):
                if c == 0:
                    s.wait_ge(ve7b, 32)
                s.wait_ge(pe_bb, 4 + c)
                s.dma_start(out=rzrow[0:1, :],
                            in_=bf_row[0:1, 1024 * c:1024 * (c + 1)]).then_inc(ld_misc, 16)
            for rp in range(4):
                for tt in range(8):
                    i = 8 * rp + tt
                    if i < 2:
                        s.wait_ge(ve7, VE1_TT * 8)
                    if i >= 2:
                        s.wait_ge(ve8, i - 1)
                    s.dma_start(out=x2ch[tt % 2],
                                in_=x2_scr[128 * tt:128 * (tt + 1), 1024 * rp:1024 * (rp + 1)]).then_inc(ld_x28[i % 2], 16)
            s.wait_ge(st_out[0], 16 * 16)
            s.wait_ge(st_out[1], 16 * 16)

        @block.gpsimd
        def _(g):
            g.iota(iota_i, pattern=[[1, 128]], base=0, channel_multiplier=-1).then_inc(ge, 1)
            qw_r = qw_own.rearrange("pl (t r) w -> t r pl w", t=4, r=128)
            g.dma_start(out=qw_i32, in_=qw_r[0]).then_inc(ld_qw, 16)
            g.dma_start(out=lut_t, in_=lut_own[0:128, :]).then_inc(ld_qw, 16)
            for mt in range(32):
                if mt < 2:
                    g.wait_ge(ve1, VE1_TT * 8)
                    g.wait_ge(pe1, 32)
                if mt >= 2:
                    g.wait_ge(pe2a, mt - 1)
                g.dma_start(out=wcol[mt % 2], in_=wv4[mt]).then_inc(ld_wv[mt % 2], 16)
                if mt in (7, 15, 23):
                    ot = (mt + 1) // 8
                    g.wait_ge(ve_dq, ot * VE_DQ_OT)
                    g.dma_start(out=qw_i32, in_=qw_r[ot]).then_inc(ld_qw, 16)
                    g.dma_start(out=lut_t, in_=lut_own[128 * ot:128 * (ot + 1), :]).then_inc(ld_qw, 16)
            g.wait_ge(exp_v[0], 16 * 8 * 16)
            g.wait_ge(exp_v[1], 16 * 8 * 16)
            g.collective_compute("AllGather", AluOpType.bypass,
                                 replica_groups=[[0, 1], [2, 3], [4, 5], [6, 7]],
                                 ins=[vag_in.opt()], outs=[vag_out.opt()]).then_inc(cc_v)
            for mt in range(32):
                if mt < 2:
                    g.wait_ge(pe2a, 31 + mt)
                if mt >= 2:
                    g.wait_ge(pe2b, mt - 1)
                g.dma_start(out=wcol[mt % 2], in_=wk4[mt]).then_inc(ld_wk[mt % 2], 16)
                if mt == 19:
                    g.wait_ge(exp_k[0], 16 * 8)
                    g.wait_ge(exp_k[1], 16 * 8)
                    g.collective_compute("AllGather", AluOpType.bypass,
                                         replica_groups=[[0, 1], [2, 3], [4, 5], [6, 7]],
                                         ins=[kag_inA.opt()], outs=[kag_outA.opt()]).then_inc(cc_k1)
            g.wait_ge(exp_k[2], 16 * 8)
            g.wait_ge(exp_k[3], 16 * 8)
            g.collective_compute("AllGather", AluOpType.bypass,
                                 replica_groups=[[0, 1], [2, 3], [4, 5], [6, 7]],
                                 ins=[kag_inB.opt()], outs=[kag_outB.opt()]).then_inc(cc_k2)
            for mt in range(32):
                if mt < 2:
                    g.wait_ge(pe2b, 31 + mt)
                else:
                    g.wait_ge(se5q, mt - 1)
                g.dma_start(out=wcol[mt % 2], in_=wq4[mt]).then_inc(ld_wq[mt % 2], 16)
                if mt == 8:
                    g.wait_ge(exp_wl[0], 16 * 64)
                    g.wait_ge(exp_wl[1], 16 * 64)
                    g.collective_compute("AllGather", AluOpType.bypass,
                                         replica_groups=[[0, 1, 2, 3, 4, 5, 6, 7]],
                                         ins=[wlag_in.opt()], outs=[wlag_out.opt()]).then_inc(cc_wl)
            wor = wot.rearrange("(it p) f -> it p f", it=32)
            wo_bufs = [wbig[:, 0:16384], wbig[:, 16384:32768],
                       xn_dm[:, 0:16384], xn_dm[:, 16384:32768]]
            wo_bufs = [b.rearrange("p (it f) -> p it f", it=32) for b in wo_bufs]
            for u in range(8):
                oc_, h_ = u // 2, u % 2
                if u == 0:
                    g.wait_ge(pe_avz, 16 * NH)
                    g.wait_ge(st_ao[0], 16 * 8 * NH // 2)
                    g.wait_ge(st_ao[1], 16 * 8 * NH // 2)
                elif u == 2:
                    g.wait_ge(pe2b, 64)
                elif u >= 4:
                    g.wait_ge(pe6, 8 * ((u - 4) // 2) + 8)
                for it in range(32):
                    g.dma_start(out=wo_bufs[u % 4][:, it, :],
                                in_=wor[it][:, 1024 * oc_ + 512 * h_:
                                            1024 * oc_ + 512 * h_ + 512]).then_inc(ld_wo[u % 4], 16)
            wlr = wlag_out.rearrange("(r it p) o -> r it p o", r=8, it=32)
            wug = [wbig[:, 16384 * i:16384 * (i + 1)].rearrange(
                       "p (it f) -> p it f", it=32) for i in range(2)]
            wug.append(wunit3.rearrange("p (it f) -> p it f", it=32))
            for u in range(8):
                if u == 2:
                    g.wait_ge(ve7, 40)
                    g.wait_ge(pe7, 32)
                    g.wait_ge(st_x2[0], 16 * 16)
                    g.wait_ge(st_x2[1], 16 * 16)
                elif u == 0:
                    g.wait_ge(cc_wl, 1)
                    g.wait_ge(pe6, 32)
                elif u >= 3:
                    g.wait_ge(pe8, 8 * ((u - 3) // 2) + 8)
                for it in range(32):
                    g.dma_start(out=wug[u % 3][:, it, :],
                                in_=wlr[u, it]).then_inc(ld_wl[u % 3], 16)

        @block.vector
        def _(v):
            v.wait_ge(ge, 1)
            v.tensor_scalar(out=ident, in0=iota_i, scalar1=0.0, scalar2=None,
                            op0=AluOpType.is_equal)
            v.tensor_scalar(out=identb, in0=iota_i, scalar1=0.0, scalar2=None,
                            op0=AluOpType.is_equal)
            v.memset(onesb[:, 0:1], 1.0)
            v.memset(onesr[0:1, :], 1.0)
            v.memset(eps_t, EPS)
            v.wait_ge(ld_misc, 64)
            v.tensor_scalar(out=bqs, in0=bqs, scalar1=SCALE, scalar2=None,
                            op0=AluOpType.mult).then_inc(ve0, 1)

            def ln_tile(tt, ld_sem, vsem, ssem, psem, x_src):
                v.wait_ge(ld_sem[tt % 2], 16 * (tt // 2 + 1))
                last = None
                for i in range(8):
                    last = v.bn_stats(bn6[:, 6 * i:6 * i + 6], x_src[:, 512 * i:512 * (i + 1)])
                last.then_inc(vsem, 1)
                v.wait_ge(vsem, VE1_TT * tt + 1)
                v.bn_aggr(mv, bn6).then_inc(vsem, 1)
                v.wait_ge(ssem, tt + 1)
                v.reciprocal(rstd, sq).then_inc(vsem, 1)
                v.wait_ge(vsem, VE1_TT * tt + 3)
                v.tensor_scalar(out=nmr, in0=mv[:, 0:1], scalar1=rstd, scalar2=-1.0,
                                op0=AluOpType.mult, op1=AluOpType.mult).then_inc(vsem, 1)
                v.wait_ge(vsem, VE1_TT * tt + 4)
                if tt >= 1:
                    v.wait_ge(psem, 4 * tt)
                v.tensor_scalar(out=xnf, in0=x_src, scalar1=rstd, scalar2=nmr,
                                op0=AluOpType.mult, op1=AluOpType.add).then_inc(vsem, 1)

            xnr = xn_dm.rearrange("p (it t) -> p it t", it=32)
            pdr = PD.rearrange("p (j t) -> p j t", j=8)
            for tt in range(8):
                ln_tile(tt, ld_x1, ve1, se1, pe1, x_io[tt % 2])
                for gq in range(4):
                    G = 4 * tt + gq
                    v.wait_ge(pe1, G + 1)
                    v.tensor_copy(xnr[:, 8 * gq:8 * gq + 8, 128 * tt:128 * (tt + 1)],
                                  pdr).then_inc(ve1b, 1)
            vcnt = 0
            for ot in range(4):
                v.wait_ge(ld_qw, 32 * (ot + 1))
                for pl in range(4):
                    last = None
                    for sh in range(8):
                        last = v.tensor_scalar(
                            out=bits[pl].rearrange("p (b sh) -> p sh b", sh=8)[:, sh, :],
                            in0=qw_u8[:, 512 * pl:512 * (pl + 1)],
                            scalar1=sh, scalar2=1,
                            op0=AluOpType.logical_shift_right,
                            op1=AluOpType.bitwise_and)
                    last.then_inc(ve_dq, 1); vcnt += 1
                v.wait_ge(ve_dq, vcnt)
                v.tensor_tensor(out=dlev,
                                in0=lut_t.rearrange("p (c two) -> p two c", two=2)[:, 1, :],
                                in1=lut_t.rearrange("p (c two) -> p two c", two=2)[:, 0, :],
                                op=AluOpType.subtract).then_inc(ve_dq, 1)
                vcnt += 1
                v.wait_ge(ve_dq, vcnt)
                for chk in range(NCHUNK):
                    C = NCHUNK * ot + chk
                    gidx = C // 2
                    sl = slice(CH * chk, CH * (chk + 1))
                    last = None
                    for c8 in range(8):
                        last = v.tensor_scalar(
                            out=tbuf[:, CH * c8:CH * (c8 + 1)], in0=bits[0][:, sl],
                            scalar1=dlev[:, c8:c8 + 1], scalar2=lut_t[:, 2 * c8:2 * c8 + 1],
                            op0=AluOpType.mult, op1=AluOpType.add)
                    last.then_inc(ve_dq, 1); vcnt += 1
                    v.wait_ge(ve_dq, vcnt)
                    for c4 in range(4):
                        last = v.select(ubuf[:, CH * c4:CH * (c4 + 1)], bits[1][:, sl],
                                        tbuf[:, CH * (2 * c4 + 1):CH * (2 * c4 + 2)],
                                        tbuf[:, CH * 2 * c4:CH * (2 * c4 + 1)], add_drain=True)
                    last.then_inc(ve_dq, 1); vcnt += 1
                    v.wait_ge(ve_dq, vcnt)
                    for c2 in range(2):
                        last = v.select(vbuf[:, CH * c2:CH * (c2 + 1)], bits[2][:, sl],
                                        ubuf[:, CH * (2 * c2 + 1):CH * (2 * c2 + 2)],
                                        ubuf[:, CH * 2 * c2:CH * (2 * c2 + 1)], add_drain=True)
                    last.then_inc(ve_dq, 1); vcnt += 1
                    v.wait_ge(ve_dq, vcnt)
                    if gidx >= 4:
                        v.wait_ge(pe_wt, gidx - 3)
                    v.select(wl_f2[gidx % 4][:, 512 * (C % 2):512 * (C % 2) + 512],
                             bits[3][:, sl], vbuf[:, CH:2 * CH], vbuf[:, 0:CH],
                             add_drain=True).then_inc(ve_dq, 1)
                    vcnt += 1
                    v.wait_ge(ve_dq, vcnt)
                    if C % 2 == 1 and gidx >= 2:
                        G = gidx - 2
                        v.wait_ge(pe_wt, G + 1)
                        if G >= 2:
                            v.wait_ge(exp_wl[G % 2], 16 * 8 * (G // 2))
                        v.tensor_copy(wlt_ev[G % 2], PD[:, 0:1024]).then_inc(ve_wt, 1)
                assert vcnt == (ot + 1) * VE_DQ_OT
            for G in (14, 15):
                v.wait_ge(pe_wt, G + 1)
                v.wait_ge(exp_wl[G % 2], 16 * 8 * (G // 2))
                v.tensor_copy(wlt_ev[G % 2], PD[:, 0:1024]).then_inc(ve_wt, 1)
            for H in range(NH):
                v.wait_ge(pe_avz, 16 * H + 8)
                v.reciprocal(rzrow[0:1, 0:512], PD[0:1, 0:512]).then_inc(ve_rz, 1)
                v.wait_ge(pe_avz, 16 * H + 16)
                v.reciprocal(rzrow[0:1, 512:1024], PD[0:1, 512:1024]).then_inc(ve_rz, 1)
                v.wait_ge(pe_rzb, 2 * H + 1)
                v.tensor_copy(rzb[:, 0:512], PD[:, 0:512]).then_inc(ve_fin, 1)
                v.wait_ge(ve_fin, 3 * H + 1)
                if H >= 2:
                    v.wait_ge(st_ao[H % 2], 16 * 8 * (H // 2))
                v.tensor_tensor(out=ao_sb[H % 2][:, 0:512], in0=PC[:, 0:512],
                                in1=rzb[:, 0:512],
                                op=AluOpType.mult).then_inc(ve_mid, 1)
                v.wait_ge(pe_rzb, 2 * H + 2)
                v.tensor_copy(rzb[:, 512:1024], PD[:, 512:1024]).then_inc(ve_fin, 1)
                v.wait_ge(ve_fin, 3 * H + 2)
                v.tensor_tensor(out=ao_sb[H % 2][:, 512:1024], in0=PC[:, 512:1024],
                                in1=rzb[:, 512:1024],
                                op=AluOpType.mult).then_inc(ve_mid, 1)
                v.wait_ge(ve_mid, 2 * H + 2)
                v.tensor_scalar(out=ao_sb[H % 2], in0=ao_sb[H % 2],
                                scalar1=bv_sb[:, H:H + 1], scalar2=None,
                                op0=AluOpType.add).then_inc(ve_fin, 1)
            for c in range(4):
                v.wait_ge(pe_bb, c + 1)
                v.tensor_copy(biasB[:, 1024 * c:1024 * (c + 1)], PD[:, 0:1024]).then_inc(ve_bb, 1)
            for oc in range(4):
                for tt in range(8):
                    i = 8 * oc + tt
                    v.wait_ge(pe6, i + 1)
                    v.wait_ge(ld_x6[i % 2], 16 * (i // 2 + 1))
                    if i >= 2:
                        v.wait_ge(st_x2[i % 2], 16 * (i // 2))
                    ps = PA if tt % 2 == 0 else PB
                    v.tensor_tensor(out=x2o[tt % 2], in0=ps[:, 0:1024],
                                    in1=biasB[:, 1024 * oc:1024 * (oc + 1)],
                                    op=AluOpType.add).then_inc(ve_mid, 1)
                    v.wait_ge(ve_mid, 64 + i + 1)
                    v.tensor_tensor(out=x2o[tt % 2], in0=x2o[tt % 2], in1=xch[tt % 2],
                                    op=AluOpType.add).then_inc(ve6, 1)
            for tt in range(8):
                ln_tile(tt, ld_x27, ve7, se7, pe7, x_io[tt % 2])
                for gq in range(4):
                    G = 4 * tt + gq
                    v.wait_ge(pe7, G + 1)
                    v.tensor_copy(xnr[:, 8 * gq:8 * gq + 8, 128 * tt:128 * (tt + 1)],
                                  pdr).then_inc(ve7b, 1)
            for c in range(4):
                v.wait_ge(pe_bb, 4 + c + 1)
                v.tensor_copy(biasB[:, 1024 * c:1024 * (c + 1)], PD[:, 0:1024]).then_inc(ve_bb, 1)
            for rp in range(4):
                for tt in range(8):
                    i = 8 * rp + tt
                    v.wait_ge(pe8, i + 1)
                    v.wait_ge(ld_x28[i % 2], 16 * (i // 2 + 1))
                    if i >= 2:
                        v.wait_ge(st_out[i % 2], 16 * (i // 2))
                    ps = PA if tt % 2 == 0 else PB
                    v.tensor_tensor(out=outb[tt % 2], in0=ps[:, 0:1024],
                                    in1=biasB[:, 1024 * rp:1024 * (rp + 1)],
                                    op=AluOpType.add).then_inc(ve_mid, 1)
                    v.wait_ge(ve_mid, 96 + i + 1)
                    v.tensor_tensor(out=outb[tt % 2], in0=outb[tt % 2], in1=x2ch[tt % 2],
                                    op=AluOpType.add).then_inc(ve8, 1)

        @block.scalar
        def _(sc):
            sc.wait_ge(ve0, 1)
            for tt in range(8):
                sc.wait_ge(ve1, VE1_TT * tt + 2)
                sc.activation(sq, mv[:, 1:2], AF.Sqrt, bias=eps_t, scale=1.0).then_inc(se1, 1)
            for mt in range(32):
                sc.wait_ge(pe2a, mt + 1)
                if mt >= 2:
                    sc.wait_ge(pe_vt, mt - 1)
                ps = (PA, PB, PC)[mt % 3]
                sc.activation(xvf[mt % 2], ps[:, 0:1024], AF.Identity).then_inc(se2a, 1)
                sc.wait_ge(pe_vt, mt + 1)
                if mt >= 2:
                    sc.wait_ge(exp_v[mt % 2], 16 * 8 * (mt // 2))
                sc.activation(vexp[mt % 2], PD[:, 0:1024], AF.Identity).then_inc(se_vx, 1)
            for mt in range(32):
                sc.wait_ge(pe2b, mt + 1)
                if mt >= 2:
                    pm = mt - 2
                    sc.wait_ge(exp_k[2 * (pm // 16) + pm % 2], 16 * ((pm % 16) // 2 + 1))
                ps = (PA, PB, PC)[mt % 3]
                sc.activation(kexp[mt % 2], ps[:, 0:1024], AF.Identity,
                              bias=bk_sb[:, mt:mt + 1], scale=1.0).then_inc(se2b, 1)
            for hg in range(NHG):
                for mtl in range(4):
                    mt = 4 * hg + mtl
                    sc.wait_ge(pe2b, 32 + mt + 1)
                    if hg >= 2 and mtl == 0:
                        sc.wait_ge(pe_sc, 16 * 4 * (hg - 1))
                    ps = PA if mt % 2 == 0 else PB
                    sc.activation(Q_hg[hg % 2][:, 1024 * mtl:1024 * (mtl + 1)], ps[:, 0:1024],
                                  AF.Identity, bias=bqs[:, mt:mt + 1], scale=SCALE).then_inc(se5q, 1)
                for hl in range(4):
                    H = 4 * hg + hl
                    for kt in range(16):
                        if kt == 6 and H >= 1:
                            Hp = H - 1
                            sc.wait_ge(ve_fin, 3 * Hp + 3)
                            for tt in range(8):
                                sc.dma_start(out=ao_scr[tt, :, Hp, :],
                                             in_=ao_sb[Hp % 2][:, 128 * tt:128 * (tt + 1)]).then_inc(st_ao[Hp % 2], 16)
                        sc.wait_ge(pe_sc, 16 * H + kt + 1)
                        if H >= 1:
                            sc.wait_ge(pe_avz, 16 * (H - 1) + kt + 1)
                        slot = PA if kt % 2 == 0 else PB
                        q0 = q0_of(kt)
                        sc.activation(attT[:, ATO[kt]:ATO[kt] + ATW[kt]],
                                      slot[:, q0:1024], AF.Exp).then_inc(se_exp, 1)
            sc.wait_ge(ve_fin, 3 * NH)
            for tt in range(8):
                sc.dma_start(out=ao_scr[tt, :, NH - 1, :],
                             in_=ao_sb[(NH - 1) % 2][:, 128 * tt:128 * (tt + 1)]).then_inc(st_ao[(NH - 1) % 2], 16)
            for oc in range(4):
                for tt in range(8):
                    i = 8 * oc + tt
                    sc.wait_ge(ve6, i + 1)
                    sc.dma_start(out=x2_scr[128 * tt:128 * (tt + 1), 1024 * oc:1024 * (oc + 1)],
                                 in_=x2o[tt % 2]).then_inc(st_x2[i % 2], 16)
            for tt in range(8):
                sc.wait_ge(ve7, VE1_TT * tt + 2)
                sc.activation(sq, mv[:, 1:2], AF.Sqrt, bias=eps_t, scale=1.0).then_inc(se7, 1)
            for rp in range(4):
                for tt in range(8):
                    i = 8 * rp + tt
                    sc.wait_ge(ve8, i + 1)
                    sc.dma_start(out=out_own[128 * tt:128 * (tt + 1), 1024 * rp:1024 * (rp + 1)],
                                 in_=outb[tt % 2]).then_inc(st_out[i % 2], 16)
            sc.wait_ge(st_out[0], 16 * 16)
            sc.wait_ge(st_out[1], 16 * 16)

        @block.tensor
        def _(te):
            te.wait_ge(ve0, 1)
            for tt in range(8):
                te.wait_ge(ve1, VE1_TT * (tt + 1))
                for gq in range(4):
                    G = 4 * tt + gq
                    if G > 0:
                        te.wait_ge(ve1b, G)
                    last = None
                    for j in range(8):
                        d = 8 * gq + j
                        last = te.transpose(PD[:, 128 * j:128 * (j + 1)],
                                            xnf[:, 128 * d:128 * (d + 1)], ident)
                    last.then_inc(pe1, 1)
            prev_pd = None
            for mt in range(33):
                if mt < 32:
                    te.wait_ge(ld_wv[mt % 2], 16 * (mt // 2 + 1))
                    if mt == 0:
                        te.wait_ge(ve1b, 32)
                    if mt >= 3:
                        te.wait_ge(se2a, mt - 2)
                    ps = (PA, PB, PC)[mt % 3]
                    last = None
                    for it in range(32):
                        for (c0, c1) in ((0, 512), (512, 1024)):
                            last = te.matmul(ps[:, c0:c1], wcol[mt % 2][:, 128 * it:128 * (it + 1)],
                                             xn_dm[:, 1024 * it + c0:1024 * it + c1],
                                             start=(it == 0), stop=(it == 31))
                    last.then_inc(pe2a, 1)
                if mt >= 1:
                    mp = mt - 1
                    te.wait_ge(se2a, mp + 1)
                    if prev_pd is not None:
                        te.wait_ge(*prev_pd)
                    last = None
                    for j in range(8):
                        last = te.transpose(PD[:, 128 * j:128 * (j + 1)],
                                            xvf[mp % 2][:, 128 * j:128 * (j + 1)], ident)
                    last.then_inc(pe_vt, 1)
                    prev_pd = (se_vx, mp + 1)
                    if mp % 2 == 1 and mp // 2 < 16:
                        G = mp // 2
                        te.wait_ge(ve_dq, dq_chunk_done(G // 4, 2 * (G % 4) + 1))
                        te.wait_ge(*prev_pd)
                        last = None
                        for j in range(8):
                            last = te.transpose(PD[:, 128 * j:128 * (j + 1)],
                                                wl_f2[G % 4][:, 128 * j:128 * (j + 1)], ident)
                        last.then_inc(pe_wt, 1)
                        prev_pd = (ve_wt, G + 1)
            for mt in range(32):
                te.wait_ge(ld_wk[mt % 2], 16 * (mt // 2 + 1))
                if mt < 3:
                    te.wait_ge(se2a, (31, 32, 30)[mt])
                if mt >= 3:
                    te.wait_ge(se2b, mt - 2)
                ps = (PA, PB, PC)[mt % 3]
                last = None
                for it in range(32):
                    for (c0, c1) in ((0, 512), (512, 1024)):
                        last = te.matmul(ps[:, c0:c1], wcol[mt % 2][:, 128 * it:128 * (it + 1)],
                                         xn_dm[:, 1024 * it + c0:1024 * it + c1],
                                         start=(it == 0), stop=(it == 31))
                last.then_inc(pe2b, 1)
            for hg in range(NHG):
                for mtl in range(4):
                    mt = 4 * hg + mtl
                    te.wait_ge(ld_wq[mt % 2], 16 * (mt // 2 + 1))
                    if hg == 0 and mtl < 2:
                        te.wait_ge(se2b, 31 + mtl)
                        te.wait_ge(ve_wt, 16)
                    if hg >= 1 and mtl < 2:
                        te.wait_ge(se_exp, 16 * 4 * hg)
                    if mtl >= 2:
                        te.wait_ge(se5q, 4 * hg + mtl - 1)
                    ps = PA if mt % 2 == 0 else PB
                    last = None
                    for it in range(32):
                        for (c0, c1) in ((0, 512), (512, 1024)):
                            last = te.matmul(ps[:, c0:c1], wcol[mt % 2][:, 128 * it:128 * (it + 1)],
                                             xn_dm[:, 1024 * it + c0:1024 * it + c1],
                                             start=(it == 0), stop=(it == 31))
                    last.then_inc(pe2b, 1)
                def mk_avz(H):
                    def avz_pair(s_):
                        if s_ == 0:
                            if H >= 1:
                                te.wait_ge(ve_mid, 2 * H)
                            else:
                                te.wait_ge(ve_wt, 16)
                        for kt in (2 * s_, 2 * s_ + 1):
                            te.wait_ge(se_exp, 16 * H + 2 * s_ + 2)
                            lastm = None
                            for (c0, c1) in q_ranges(kt):
                                stop = (kt == 7) if c1 <= 512 else (kt == 15)
                                lastm = te.matmul(
                                    PC[:, c0:c1], v_sb[H % 2][:, 128 * kt:128 * (kt + 1)],
                                    attT[:, ATO[kt] + c0 - q0_of(kt):ATO[kt] + c1 - q0_of(kt)],
                                    start=(kt == 0), stop=stop)
                            for (c0, c1) in q_ranges(kt):
                                stop = (kt == 7) if c1 <= 512 else (kt == 15)
                                lastm = te.matmul(
                                    PD[0:1, c0:c1], onesb[:, 0:1],
                                    attT[:, ATO[kt] + c0 - q0_of(kt):ATO[kt] + c1 - q0_of(kt)],
                                    start=(kt == 0), stop=stop)
                            lastm.then_inc(pe_avz, 1)
                    return avz_pair

                def rz_bcast(Hp, h2):
                        te.wait_ge(ve_rz, 2 * Hp + h2 + 1)
                        te.matmul(PD[:, 512 * h2:512 * (h2 + 1)], onesr[0:1, :],
                                  rzrow[0:1, 512 * h2:512 * (h2 + 1)],
                                  start=True, stop=True).then_inc(pe_rzb, 1)

                for hl in range(4):
                    H = 4 * hg + hl
                    te.wait_ge(ld_kv[H % 2], 16 * 8 * (H // 2 + 1))
                    avz_pair = mk_avz(H)
                    for s_ in range(8):
                        for kt in (2 * s_, 2 * s_ + 1):
                            tgt = 16 * H + kt - 1
                            if tgt > 0:
                                te.wait_ge(se_exp, tgt)
                            if kt == 0:
                                te.wait_ge(se5q, 4 * (hg + 1))
                            slot = PA if kt % 2 == 0 else PB
                            s_of = kt // 2
                            j_of = kt % 2
                            last = None
                            for (c0, c1) in q_ranges(kt):
                                has_mask = (c0 == q0_of(kt))
                                last = te.matmul(slot[:, c0:c1],
                                                 k_sb[H % 2][:, 128 * kt:128 * (kt + 1)],
                                                 Q_hg[hg % 2][:, 1024 * hl + c0:1024 * hl + c1],
                                                 start=True, stop=not has_mask)
                                if has_mask:
                                    last = te.matmul(
                                        slot[:, c0:c0 + 128], identb,
                                        masks[:, 256 * s_of + 128 * j_of:256 * s_of + 128 * (j_of + 1)],
                                        start=False, stop=True)
                            last.then_inc(pe_sc, 1)
                        if s_ == 1 and H >= 1:
                            rz_bcast(H - 1, 0)
                        if s_ == 2 and H >= 1:
                            rz_bcast(H - 1, 1)
                        if s_ >= 2:
                            avz_pair(s_ - 2)
                    avz_pair(6)
                    avz_pair(7)
                    if H == NH - 1:
                        rz_bcast(H, 0)
                        rz_bcast(H, 1)
            for c in range(4):
                if c == 0:
                    te.wait_ge(ve_fin, 3 * NH)
                te.wait_ge(ld_misc, 64 + 16 * (c + 1))
                if c > 0:
                    te.wait_ge(ve_bb, c)
                last = None
                for h2 in range(2):
                    last = te.matmul(PD[:, 512 * h2:512 * (h2 + 1)], onesr[0:1, :],
                                     rzrow[0:1, 512 * h2:512 * (h2 + 1)], start=True, stop=True)
                last.then_inc(pe_bb, 1)
            wo_bt = [wbig[:, 0:16384], wbig[:, 16384:32768],
                     xn_dm[:, 0:16384], xn_dm[:, 16384:32768]]
            wo_bt = [b.rearrange("p (it f) -> p it f", it=32) for b in wo_bt]
            for oc in range(4):
                if oc == 0:
                    te.wait_ge(ve_bb, 4)
                for tt in range(8):
                    i = 8 * oc + tt
                    te.wait_ge(ld_ao[i % 2], 16 * (i // 2 + 1))
                    if i < 2:
                        te.wait_ge(se_exp, 16 * NH)
                    if i >= 2:
                        te.wait_ge(ve6, i - 1)
                    ps = PA if tt % 2 == 0 else PB
                    last = None
                    for h in range(2):
                        u = 2 * oc + h
                        if tt == 0:
                            te.wait_ge(ld_wo[u % 4], 16 * 32 * (u // 4 + 1))
                        for it in range(32):
                            last = te.matmul(ps[:, 512 * h:512 * (h + 1)],
                                             aost[tt % 2][:, 128 * it:128 * (it + 1)],
                                             wo_bt[u % 4][:, it, :],
                                             start=(it == 0), stop=(it == 31))
                    last.then_inc(pe6, 1)
            for tt in range(8):
                te.wait_ge(ve7, VE1_TT * (tt + 1))
                for gq in range(4):
                    G = 4 * tt + gq
                    if G > 0:
                        te.wait_ge(ve7b, G)
                    else:
                        te.wait_ge(ve_bb, 4)
                    last = None
                    for j in range(8):
                        d = 8 * gq + j
                        last = te.transpose(PD[:, 128 * j:128 * (j + 1)],
                                            xnf[:, 128 * d:128 * (d + 1)], ident)
                    last.then_inc(pe7, 1)
            for c in range(4):
                if c == 0:
                    te.wait_ge(ve7b, 32)
                te.wait_ge(ld_misc, 128 + 16 * (c + 1))
                if c > 0:
                    te.wait_ge(ve_bb, 4 + c)
                last = None
                for h2 in range(2):
                    last = te.matmul(PD[:, 512 * h2:512 * (h2 + 1)], onesr[0:1, :],
                                     rzrow[0:1, 512 * h2:512 * (h2 + 1)], start=True, stop=True)
                last.then_inc(pe_bb, 1)
            wut = [wbig[:, 16384 * i:16384 * (i + 1)].rearrange(
                       "p (it f) -> p it f", it=32) for i in range(2)]
            wut.append(wunit3.rearrange("p (it f) -> p it f", it=32))
            for rp in range(4):
                if rp == 0:
                    te.wait_ge(ve_bb, 8)
                    te.wait_ge(ve6, 32)
                for tt in range(8):
                    i = 8 * rp + tt
                    ps = PA if tt % 2 == 0 else PB
                    if i >= 2:
                        te.wait_ge(ve8, i - 1)
                    last = None
                    for rb in range(2):
                        u = 2 * rp + rb
                        if tt == 0:
                            te.wait_ge(ld_wl[u % 3], 16 * 32 * (u // 3 + 1))
                        for it in range(32):
                            last = te.matmul(
                                ps[:, 512 * rb:512 * (rb + 1)],
                                xn_dm[:, 1024 * it + 128 * tt:1024 * it + 128 * (tt + 1)],
                                wut[u % 3][:, it, :],
                                start=(it == 0), stop=(it == 31))
                    last.then_inc(pe8, 1)
    return nc


def kernel(L0_attn_x, L0_attn_Wq, L0_attn_bq, L0_attn_Wk, L0_attn_bk,
           L0_attn_Wv, L0_attn_bv, L0_attn_Wo, L0_attn_bo,
           L0_ffn_qweight, L0_ffn_lut4, L0_ffn_lut8, L0_ffn_bias):
    x = np.asarray(L0_attn_x, dtype=np.float32)
    Wq = np.asarray(L0_attn_Wq, dtype=np.float32)
    Wk = np.asarray(L0_attn_Wk, dtype=np.float32)
    Wv = np.asarray(L0_attn_Wv, dtype=np.float32)
    Wo = np.asarray(L0_attn_Wo, dtype=np.float32)
    bq = np.asarray(L0_attn_bq, dtype=np.float32)
    bk = np.asarray(L0_attn_bk, dtype=np.float32)
    bv = np.asarray(L0_attn_bv, dtype=np.float32)
    bo = np.asarray(L0_attn_bo, dtype=np.float32)
    qw = np.asarray(L0_ffn_qweight)
    lut4 = np.asarray(L0_ffn_lut4, dtype=np.float32)
    bfb = np.asarray(L0_ffn_bias, dtype=np.float32)

    wq4 = np.ascontiguousarray(Wq.reshape(32, 128, 32, 128).transpose(0, 3, 2, 1))
    wk4 = np.ascontiguousarray(Wk.reshape(32, 128, 32, 128).transpose(0, 3, 2, 1))
    wv4 = np.ascontiguousarray(Wv.reshape(32, 128, 32, 128).transpose(0, 3, 2, 1))
    wot = np.ascontiguousarray(Wo.T)
    bq_t = np.ascontiguousarray(bq.reshape(32, 128).T)
    bk_t = np.ascontiguousarray(bk.reshape(32, 128).T)
    bv_t = np.ascontiguousarray(bv.reshape(32, 128).T)
    bo_row = np.ascontiguousarray(bo.reshape(1, 4096))
    bf_row = np.ascontiguousarray(bfb.reshape(1, 4096))

    def build_mask(parity):
        QT = QT_EVEN if parity == 0 else QT_ODD
        m = np.zeros((128, 2048), np.float32)
        for s in range(8):
            qt = QT[s]
            for j in range(2):
                kt = 2 * s + j
                if kt < qt:
                    t = np.zeros((128, 128), np.float32)
                elif kt == qt:
                    t = np.where(np.arange(128)[:, None] <= np.arange(128)[None, :],
                                 0.0, -30000.0).astype(np.float32)
                else:
                    t = np.full((128, 128), -30000.0, np.float32)
                m[:, 256 * s + 128 * j:256 * s + 128 * (j + 1)] = t
        return m.astype(BF)

    mask_by_par = [build_mask(0), build_mask(1)]

    nc = build_nc()
    in_maps = []
    for core in range(8):
        b = core // 2
        parity = core % 2
        QT = QT_EVEN if parity == 0 else QT_ODD
        rows = np.concatenate([128 * qt + np.arange(128) for qt in QT])
        in_maps.append({
            "x_own": np.ascontiguousarray(x[b][rows]),
            "wq4": wq4, "wk4": wk4, "wv4": wv4, "wot": wot,
            "bq_t": bq_t, "bk_t": bk_t, "bv_t": bv_t,
            "bo_row": bo_row, "bf_row": bf_row,
            "qw_own": np.ascontiguousarray(qw[0:4, 512 * core:512 * (core + 1), :]),
            "lut_own": np.ascontiguousarray(lut4[512 * core:512 * (core + 1), :]),
            "masks_in": mask_by_par[parity],
        })
    res = run_bass_kernel_spmd(nc, in_maps, core_ids=list(range(8)))
    out = np.empty((4, 2048, 4096), dtype=np.float32)
    for core in range(8):
        b = core // 2
        parity = core % 2
        QT = QT_EVEN if parity == 0 else QT_ODD
        rows = np.concatenate([128 * qt + np.arange(128) for qt in QT])
        out[b][rows] = res.results[core]["out_own"]
    return out


# version 16
# speedup vs baseline: 1.1481x; 1.0116x over previous
import contextlib
import math
import numpy as np
import ml_dtypes
import concourse.bass as bass
import concourse.mybir as mybir
from concourse.bass_utils import run_bass_kernel_spmd
from concourse.alu_op_type import AluOpType

F32 = mybir.dt.float32
BF16 = mybir.dt.bfloat16
I32 = mybir.dt.int32
U8 = mybir.dt.uint8
AF = mybir.ActivationFunctionType
BF = ml_dtypes.bfloat16

QT_EVEN = [0, 3, 4, 7, 8, 11, 12, 15]
QT_ODD = [1, 2, 5, 6, 9, 10, 13, 14]
SCALE = 1.0 / math.sqrt(128.0)
EPS = 1e-5
NH = 32
NHG = 8
CH = 512
NCHUNK = 4096 // CH
VE_DQ_OT = 4 + 1 + NCHUNK * 4
VE1_TT = 5

def q0_of(kt):
    return 128 * (kt // 2)
ATW = [1024 - q0_of(kt) for kt in range(16)]
ATO = [0] * 16
for _kt in range(1, 16):
    ATO[_kt] = ATO[_kt - 1] + ATW[_kt - 1]
ATT_TOT = ATO[15] + ATW[15]

def q_ranges(kt):
    q0 = q0_of(kt)
    return [(q0, 512), (512, 1024)] if q0 < 512 else [(q0, 1024)]


def build_nc():
    nc = bass.Bass("TRN2", target_bir_lowering=False)
    x_own = nc.dram_tensor("x_own", [1024, 4096], F32, kind="ExternalInput").ap()
    wq4 = nc.dram_tensor("wq4", [32, 128, 32, 128], F32, kind="ExternalInput").ap()
    wk4 = nc.dram_tensor("wk4", [32, 128, 32, 128], F32, kind="ExternalInput").ap()
    wv4 = nc.dram_tensor("wv4", [32, 128, 32, 128], F32, kind="ExternalInput").ap()
    wot = nc.dram_tensor("wot", [4096, 4096], F32, kind="ExternalInput").ap()
    bq_t = nc.dram_tensor("bq_t", [128, 32], F32, kind="ExternalInput").ap()
    bk_t = nc.dram_tensor("bk_t", [128, 32], F32, kind="ExternalInput").ap()
    bv_t = nc.dram_tensor("bv_t", [128, 32], F32, kind="ExternalInput").ap()
    bo_row = nc.dram_tensor("bo_row", [1, 4096], F32, kind="ExternalInput").ap()
    bf_row = nc.dram_tensor("bf_row", [1, 4096], F32, kind="ExternalInput").ap()
    qw_own = nc.dram_tensor("qw_own", [4, 512, 128], I32, kind="ExternalInput").ap()
    lut_own = nc.dram_tensor("lut_own", [512, 16], F32, kind="ExternalInput").ap()
    masks_in = nc.dram_tensor("masks_in", [128, 2048], BF16, kind="ExternalInput").ap()
    out_own = nc.dram_tensor("out_own", [1024, 4096], F32, kind="ExternalOutput").ap()
    kag_inA = nc.dram_tensor("kag_inA", [2048, 1024], BF16).ap()
    kag_inB = nc.dram_tensor("kag_inB", [2048, 1024], BF16).ap()
    kag_outA = nc.dram_tensor("kag_outA", [4096, 1024], BF16).ap()
    kag_outB = nc.dram_tensor("kag_outB", [4096, 1024], BF16).ap()
    vag_in = nc.dram_tensor("vag_in", [1024, 4096], BF16).ap()
    vag_out = nc.dram_tensor("vag_out", [2048, 4096], BF16).ap()
    wlag_in = nc.dram_tensor("wlag_in", [4096, 512], BF16).ap()
    wlag_out = nc.dram_tensor("wlag_out", [32768, 512], BF16).ap()
    ao_scr = nc.dram_tensor("ao_scr", [8, 128, 32, 128], BF16).ap()
    x2_scr = nc.dram_tensor("x2_scr", [1024, 4096], F32).ap()

    es = contextlib.ExitStack()
    with es:
        AR = es.enter_context(nc.sbuf_tensor("AR", [128, 204 * 1024], U8))
        arf, arb, ari, aru = AR.bitcast(F32), AR.bitcast(BF16), AR.bitcast(I32), AR.bitcast(U8)

        def fsl(o, n):
            assert o % 64 == 0
            return arf.ap()[:, o // 4:o // 4 + n]

        def bsl(o, n):
            assert o % 64 == 0
            return arb.ap()[:, o // 2:o // 2 + n]

        def isl(o, n):
            assert o % 64 == 0
            return ari.ap()[:, o // 4:o // 4 + n]

        def usl(o, n):
            assert o % 64 == 0
            return aru.ap()[:, o:o + n]

        masks = bsl(0, 2048)
        biasB = bsl(4096, 4096)
        rzb = fsl(12288, 1024)
        rzrow = fsl(16384, 1024)
        ident = fsl(20480, 128)
        identb = bsl(23040, 128)
        iota_i = isl(20992, 128)
        onesr = fsl(21504, 128)
        onesb = bsl(22016, 32)
        bqs = fsl(22080, 32)
        bk_sb = fsl(22208, 32)
        bv_sb = fsl(22336, 32)
        bn6 = fsl(22464, 48)
        mv = fsl(22656, 2)
        sq = fsl(22720, 1)
        rstd = fsl(22784, 1)
        nmr = fsl(22848, 1)
        eps_t = fsl(22912, 1)
        S1 = 24 * 1024
        xn_dm = bsl(S1, 32768)
        S2 = 88 * 1024
        bits = [usl(S2 + 4096 * i, 4096) for i in range(4)]
        qw_i32 = isl(S2 + 16384, 512)
        qw_u8 = usl(S2 + 16384, 2048)
        lut_t = fsl(S2 + 18432, 16)
        dlev = fsl(S2 + 18496, 8)
        tbuf = fsl(S2 + 18560, 8 * CH)
        ubuf = fsl(S2 + 34944, 4 * CH)
        vbuf = fsl(S2 + 43136, 2 * CH)
        wl_f2 = [fsl(S2 + 47232 + 4096 * i, 1024) for i in range(4)]
        wlt_ev = [bsl(4096 + 2048 * i, 1024) for i in range(2)]
        attT = bsl(S2, ATT_TOT)
        Q_hg = [bsl(S2 + 18432 + 8192 * i, 4096) for i in range(2)]
        k_sb = [bsl(S2 + 34816 + 4096 * i, 2048) for i in range(2)]
        v_sb = [bsl(S2 + 43008 + 4096 * i, 2048) for i in range(2)]
        ao_sb = [bsl(S2 + 51200 + 2048 * i, 1024) for i in range(2)]
        wbig = bsl(S2, 32768)
        R3 = 152 * 1024
        vexp = [bsl(R3 + 24576 + 2048 * i, 1024) for i in range(2)]
        xvf = [fsl(R3 + 16384 + 4096 * i, 1024) for i in range(2)]
        x_io = [fsl(R3 + 16384 * i, 4096) for i in range(2)]
        xnf = fsl(R3 + 32768, 4096)
        wcol = [bsl(R3 + 8192 * i, 4096) for i in range(2)]
        kexp = [bsl(R3 + 16384 + 2048 * i, 1024) for i in range(2)]
        aost = [bsl(R3 + 8192 * i, 4096) for i in range(2)]
        xch = [fsl(R3 + 16384 + 4096 * i, 1024) for i in range(2)]
        x2o = [fsl(R3 + 24576 + 4096 * i, 1024) for i in range(2)]
        x2ch = [fsl(R3 + 4096 * i, 1024) for i in range(2)]
        outb = [fsl(R3 + 8192 + 4096 * i, 1024) for i in range(2)]
        wunit3 = bsl(R3 + 16384, 16384)

        PA = es.enter_context(nc.psum_tensor("PA", [128, 1024], F32))
        PB = es.enter_context(nc.psum_tensor("PB", [128, 1024], F32))
        PC = es.enter_context(nc.psum_tensor("PC", [128, 1024], F32))
        PD = es.enter_context(nc.psum_tensor("PD", [128, 1024], F32))

        def sem(name):
            return es.enter_context(nc.semaphore(name))
        ld_misc = sem("ld_misc"); ld_qw = sem("ld_qw")
        ld_x1 = [sem("ld_x1a"), sem("ld_x1b")]
        ld_wv = [sem("ld_wva"), sem("ld_wvb")]
        ld_wk = [sem("ld_wka"), sem("ld_wkb")]
        ld_wq = [sem("ld_wqa"), sem("ld_wqb")]
        ld_wo = [sem("ld_woa"), sem("ld_wob"), sem("ld_woc"), sem("ld_wod")]
        ld_wl = [sem("ld_wla"), sem("ld_wlb"), sem("ld_wlc")]
        ld_kv = [sem("ld_kva"), sem("ld_kvb")]
        ld_ao = [sem("ld_aoa"), sem("ld_aob")]
        ld_x6 = [sem("ld_x6a"), sem("ld_x6b")]
        ld_x27 = [sem("ld_x27a"), sem("ld_x27b")]
        ld_x28 = [sem("ld_x28a"), sem("ld_x28b")]
        exp_v = [sem("exp_va"), sem("exp_vb")]
        exp_k = [sem("exp_ka"), sem("exp_kb"), sem("exp_kc"), sem("exp_kd")]
        exp_wl = [sem("exp_wla"), sem("exp_wlb")]
        cc_k1 = sem("cc_k1"); cc_k2 = sem("cc_k2"); cc_v = sem("cc_v"); cc_wl = sem("cc_wl")
        st_ao = [sem("st_aoa"), sem("st_aob")]
        st_x2 = [sem("st_x2a"), sem("st_x2b")]
        st_out = [sem("st_outa"), sem("st_outb")]
        ge = sem("ge"); ve0 = sem("ve0")
        ve1 = sem("ve1"); se1 = sem("se1"); pe1 = sem("pe1"); ve1b = sem("ve1b")
        pe_bb = sem("pe_bb"); ve_bb = sem("ve_bb")
        pe2a = sem("pe2a"); se2a = sem("se2a"); pe_vt = sem("pe_vt"); se_vx = sem("se_vx")
        pe2b = sem("pe2b"); se2b = sem("se2b")
        ve_dq = sem("ve_dq"); pe_wt = sem("pe_wt"); ve_wt = sem("ve_wt")
        pe_sc = sem("pe_sc"); pe_avz = sem("pe_avz"); pe_rzb = sem("pe_rzb")
        se_exp = sem("se_exp"); se5q = sem("se5q")
        ve_mask = sem("ve_mask"); ve_rz = sem("ve_rz"); ve_fin = sem("ve_fin")
        pe6 = sem("pe6"); ve6 = sem("ve6")
        ve7 = sem("ve7"); se7 = sem("se7"); pe7 = sem("pe7"); ve7b = sem("ve7b")
        pe8 = sem("pe8"); ve8 = sem("ve8"); ve_mid = sem("ve_mid")

        block = es.enter_context(nc.Block())

        def dq_chunk_done(ot, c_local):
            return ot * VE_DQ_OT + 5 + 4 * (c_local + 1)

        @block.sync
        def _(s):
            s.dma_start(out=masks, in_=masks_in).then_inc(ld_misc, 16)
            s.dma_start(out=bqs, in_=bq_t).then_inc(ld_misc, 16)
            s.dma_start(out=bk_sb, in_=bk_t).then_inc(ld_misc, 16)
            s.dma_start(out=bv_sb, in_=bv_t).then_inc(ld_misc, 16)
            for tt in range(8):
                if tt >= 2:
                    s.wait_ge(ve1, VE1_TT * (tt - 1))
                s.dma_start(out=x_io[tt % 2],
                            in_=x_own[128 * tt:128 * (tt + 1), :]).then_inc(ld_x1[tt % 2], 16)
            for mt in range(32):
                s.wait_ge(se_vx, mt + 1)
                for tt in range(8):
                    s.dma_start(out=vag_in[128 * tt:128 * (tt + 1), 128 * mt:128 * (mt + 1)],
                                in_=vexp[mt % 2][:, 128 * tt:128 * (tt + 1)]).then_inc(exp_v[mt % 2], 16)
                if mt % 2 == 1:
                    G = mt // 2
                    s.wait_ge(ve_wt, G + 1)
                    ot, gq = G // 4, G % 4
                    for j in range(8):
                        it = 8 * gq + j
                        s.dma_start(out=wlag_in[128 * it:128 * (it + 1), 128 * ot:128 * (ot + 1)],
                                    in_=wlt_ev[G % 2][:, 128 * j:128 * (j + 1)]).then_inc(exp_wl[G % 2], 16)
            for mt in range(32):
                s.wait_ge(se2b, mt + 1)
                half = mt // 16
                mloc = mt % 16
                kag = kag_inA if half == 0 else kag_inB
                s.dma_start(out=kag[128 * mloc:128 * (mloc + 1), :],
                            in_=kexp[mt % 2]).then_inc(exp_k[2 * half + mt % 2], 16)
            kr2A = kag_outA.rearrange("(r f) (j w t) -> r f j w t", r=2, j=4, w=2, t=128)
            kr2B = kag_outB.rearrange("(r f) (j w t) -> r f j w t", r=2, j=4, w=2, t=128)
            vr = vag_out.rearrange("(r j w p) f -> r p j w f", r=2, j=4, w=2, p=128)
            for H in range(NH):
                par = H % 2
                if H == 0:
                    s.wait_ge(cc_k1, 1)
                    s.wait_ge(cc_v, 1)
                if H == 16:
                    s.wait_ge(cc_k2, 1)
                if H < 2:
                    s.wait_ge(ve_dq, 4 * VE_DQ_OT)
                    s.wait_ge(ve_wt, 16)
                if H >= 2:
                    s.wait_ge(pe_sc, 16 * (H - 1))
                    s.wait_ge(pe_avz, 16 * (H - 1))
                for rp in range(2):
                    for w in range(2):
                        goff = (384 * w) if rp == 0 else (128 + 128 * w)
                        dk = k_sb[par].rearrange("p (j rest) -> p j rest", j=4)[:, :, goff:goff + 128]
                        kr2h = kr2A if H < 16 else kr2B
                        Hl = H % 16
                        s.dma_start(out=dk, in_=kr2h[rp, 128 * Hl:128 * (Hl + 1), :, w, :]).then_inc(ld_kv[par], 16)
                        dv = v_sb[par].rearrange("p (j rest) -> p j rest", j=4)[:, :, goff:goff + 128]
                        s.dma_start(out=dv, in_=vr[rp, :, :, w, 128 * H:128 * (H + 1)]).then_inc(ld_kv[par], 16)
            for c in range(4):
                if c == 0:
                    s.wait_ge(ve_fin, 3 * NH)
                if c > 0:
                    s.wait_ge(pe_bb, c)
                s.dma_start(out=rzrow[0:1, :],
                            in_=bo_row[0:1, 1024 * c:1024 * (c + 1)]).then_inc(ld_misc, 16)
            for oc in range(4):
                for tt in range(8):
                    i = 8 * oc + tt
                    if i == 0:
                        s.wait_ge(st_ao[0], 16 * 8 * NH // 2)
                        s.wait_ge(st_ao[1], 16 * 8 * NH // 2)
                        s.wait_ge(se5q, 4 * NHG)
                    if i >= 2:
                        s.wait_ge(pe6, i - 1)
                    s.dma_start(out=aost[tt % 2],
                                in_=ao_scr[tt].rearrange("p it t -> p (it t)")).then_inc(ld_ao[i % 2], 16)
                    if i >= 2:
                        s.wait_ge(ve6, i - 1)
                    s.dma_start(out=xch[tt % 2],
                                in_=x_own[128 * tt:128 * (tt + 1), 1024 * oc:1024 * (oc + 1)]).then_inc(ld_x6[i % 2], 16)
            for tt in range(8):
                s.wait_ge(st_x2[0], 16 * 16)
                s.wait_ge(st_x2[1], 16 * 16)
                if tt < 2:
                    s.wait_ge(pe6, 32)
                    s.wait_ge(ve6, 32)
                if tt >= 2:
                    s.wait_ge(ve7, VE1_TT * (tt - 1))
                s.dma_start(out=x_io[tt % 2],
                            in_=x2_scr[128 * tt:128 * (tt + 1), :]).then_inc(ld_x27[tt % 2], 16)
            for c in range(4):
                if c == 0:
                    s.wait_ge(ve7b, 32)
                s.wait_ge(pe_bb, 4 + c)
                s.dma_start(out=rzrow[0:1, :],
                            in_=bf_row[0:1, 1024 * c:1024 * (c + 1)]).then_inc(ld_misc, 16)
            for rp in range(4):
                for tt in range(8):
                    i = 8 * rp + tt
                    if i < 2:
                        s.wait_ge(ve7, VE1_TT * 8)
                    if i >= 2:
                        s.wait_ge(ve8, i - 1)
                    s.dma_start(out=x2ch[tt % 2],
                                in_=x2_scr[128 * tt:128 * (tt + 1), 1024 * rp:1024 * (rp + 1)]).then_inc(ld_x28[i % 2], 16)
            s.wait_ge(st_out[0], 16 * 16)
            s.wait_ge(st_out[1], 16 * 16)

        @block.gpsimd
        def _(g):
            g.iota(iota_i, pattern=[[1, 128]], base=0, channel_multiplier=-1).then_inc(ge, 1)
            qw_r = qw_own.rearrange("pl (t r) w -> t r pl w", t=4, r=128)
            g.dma_start(out=qw_i32, in_=qw_r[0]).then_inc(ld_qw, 16)
            g.dma_start(out=lut_t, in_=lut_own[0:128, :]).then_inc(ld_qw, 16)
            for mt in range(32):
                if mt < 2:
                    g.wait_ge(ve1, VE1_TT * 8)
                    g.wait_ge(pe1, 32)
                if mt >= 2:
                    g.wait_ge(pe2a, mt - 1)
                g.dma_start(out=wcol[mt % 2], in_=wv4[mt]).then_inc(ld_wv[mt % 2], 16)
                if mt in (7, 15, 23):
                    ot = (mt + 1) // 8
                    g.wait_ge(ve_dq, ot * VE_DQ_OT)
                    g.dma_start(out=qw_i32, in_=qw_r[ot]).then_inc(ld_qw, 16)
                    g.dma_start(out=lut_t, in_=lut_own[128 * ot:128 * (ot + 1), :]).then_inc(ld_qw, 16)
            g.wait_ge(exp_v[0], 16 * 8 * 16)
            g.wait_ge(exp_v[1], 16 * 8 * 16)
            g.collective_compute("AllGather", AluOpType.bypass,
                                 replica_groups=[[0, 1], [2, 3], [4, 5], [6, 7]],
                                 ins=[vag_in.opt()], outs=[vag_out.opt()]).then_inc(cc_v)
            for mt in range(32):
                if mt < 2:
                    g.wait_ge(pe2a, 31 + mt)
                if mt >= 2:
                    g.wait_ge(pe2b, mt - 1)
                g.dma_start(out=wcol[mt % 2], in_=wk4[mt]).then_inc(ld_wk[mt % 2], 16)
                if mt == 19:
                    g.wait_ge(exp_k[0], 16 * 8)
                    g.wait_ge(exp_k[1], 16 * 8)
                    g.collective_compute("AllGather", AluOpType.bypass,
                                         replica_groups=[[0, 1], [2, 3], [4, 5], [6, 7]],
                                         ins=[kag_inA.opt()], outs=[kag_outA.opt()]).then_inc(cc_k1)
            g.wait_ge(exp_k[2], 16 * 8)
            g.wait_ge(exp_k[3], 16 * 8)
            g.collective_compute("AllGather", AluOpType.bypass,
                                 replica_groups=[[0, 1], [2, 3], [4, 5], [6, 7]],
                                 ins=[kag_inB.opt()], outs=[kag_outB.opt()]).then_inc(cc_k2)
            for mt in range(32):
                if mt < 2:
                    g.wait_ge(pe2b, 31 + mt)
                else:
                    g.wait_ge(se5q, mt - 1)
                g.dma_start(out=wcol[mt % 2], in_=wq4[mt]).then_inc(ld_wq[mt % 2], 16)
                if mt == 16:
                    g.wait_ge(exp_wl[0], 16 * 64)
                    g.wait_ge(exp_wl[1], 16 * 64)
                    g.collective_compute("AllGather", AluOpType.bypass,
                                         replica_groups=[[0, 1, 2, 3, 4, 5, 6, 7]],
                                         ins=[wlag_in.opt()], outs=[wlag_out.opt()]).then_inc(cc_wl)
            wor = wot.rearrange("(it p) f -> it p f", it=32)
            wo_bufs = [wbig[:, 0:16384], wbig[:, 16384:32768],
                       xn_dm[:, 0:16384], xn_dm[:, 16384:32768]]
            wo_bufs = [b.rearrange("p (it f) -> p it f", it=32) for b in wo_bufs]
            for u in range(8):
                oc_, h_ = u // 2, u % 2
                if u == 0:
                    g.wait_ge(pe_avz, 16 * NH)
                    g.wait_ge(st_ao[0], 16 * 8 * NH // 2)
                    g.wait_ge(st_ao[1], 16 * 8 * NH // 2)
                elif u == 2:
                    g.wait_ge(pe2b, 64)
                elif u >= 4:
                    g.wait_ge(pe6, 8 * ((u - 4) // 2) + 8)
                for it in range(32):
                    g.dma_start(out=wo_bufs[u % 4][:, it, :],
                                in_=wor[it][:, 1024 * oc_ + 512 * h_:
                                            1024 * oc_ + 512 * h_ + 512]).then_inc(ld_wo[u % 4], 16)
            wlr = wlag_out.rearrange("(r it p) o -> r it p o", r=8, it=32)
            wug = [wbig[:, 16384 * i:16384 * (i + 1)].rearrange(
                       "p (it f) -> p it f", it=32) for i in range(2)]
            wug.append(wunit3.rearrange("p (it f) -> p it f", it=32))
            for u in range(8):
                if u == 2:
                    g.wait_ge(ve7, 40)
                    g.wait_ge(pe7, 32)
                    g.wait_ge(st_x2[0], 16 * 16)
                    g.wait_ge(st_x2[1], 16 * 16)
                elif u == 0:
                    g.wait_ge(cc_wl, 1)
                    g.wait_ge(pe6, 32)
                elif u >= 3:
                    g.wait_ge(pe8, 8 * ((u - 3) // 2) + 8)
                for it in range(32):
                    g.dma_start(out=wug[u % 3][:, it, :],
                                in_=wlr[u, it]).then_inc(ld_wl[u % 3], 16)

        @block.vector
        def _(v):
            v.wait_ge(ge, 1)
            v.tensor_scalar(out=ident, in0=iota_i, scalar1=0.0, scalar2=None,
                            op0=AluOpType.is_equal)
            v.tensor_scalar(out=identb, in0=iota_i, scalar1=0.0, scalar2=None,
                            op0=AluOpType.is_equal)
            v.memset(onesb[:, 0:1], 1.0)
            v.memset(onesr[0:1, :], 1.0)
            v.memset(eps_t, EPS)
            v.wait_ge(ld_misc, 64)
            v.tensor_scalar(out=bqs, in0=bqs, scalar1=SCALE, scalar2=None,
                            op0=AluOpType.mult).then_inc(ve0, 1)

            def ln_tile(tt, ld_sem, vsem, ssem, psem, x_src):
                v.wait_ge(ld_sem[tt % 2], 16 * (tt // 2 + 1))
                last = None
                for i in range(8):
                    last = v.bn_stats(bn6[:, 6 * i:6 * i + 6], x_src[:, 512 * i:512 * (i + 1)])
                last.then_inc(vsem, 1)
                v.wait_ge(vsem, VE1_TT * tt + 1)
                v.bn_aggr(mv, bn6).then_inc(vsem, 1)
                v.wait_ge(ssem, tt + 1)
                v.reciprocal(rstd, sq).then_inc(vsem, 1)
                v.wait_ge(vsem, VE1_TT * tt + 3)
                v.tensor_scalar(out=nmr, in0=mv[:, 0:1], scalar1=rstd, scalar2=-1.0,
                                op0=AluOpType.mult, op1=AluOpType.mult).then_inc(vsem, 1)
                v.wait_ge(vsem, VE1_TT * tt + 4)
                if tt >= 1:
                    v.wait_ge(psem, 4 * tt)
                v.tensor_scalar(out=xnf, in0=x_src, scalar1=rstd, scalar2=nmr,
                                op0=AluOpType.mult, op1=AluOpType.add).then_inc(vsem, 1)

            xnr = xn_dm.rearrange("p (it t) -> p it t", it=32)
            pdr = PD.rearrange("p (j t) -> p j t", j=8)
            for tt in range(8):
                ln_tile(tt, ld_x1, ve1, se1, pe1, x_io[tt % 2])
                for gq in range(4):
                    G = 4 * tt + gq
                    v.wait_ge(pe1, G + 1)
                    v.tensor_copy(xnr[:, 8 * gq:8 * gq + 8, 128 * tt:128 * (tt + 1)],
                                  pdr).then_inc(ve1b, 1)
            vcnt = 0
            for ot in range(4):
                v.wait_ge(ld_qw, 32 * (ot + 1))
                for pl in range(4):
                    last = None
                    for sh in range(8):
                        last = v.tensor_scalar(
                            out=bits[pl].rearrange("p (b sh) -> p sh b", sh=8)[:, sh, :],
                            in0=qw_u8[:, 512 * pl:512 * (pl + 1)],
                            scalar1=sh, scalar2=1,
                            op0=AluOpType.logical_shift_right,
                            op1=AluOpType.bitwise_and)
                    last.then_inc(ve_dq, 1); vcnt += 1
                v.wait_ge(ve_dq, vcnt)
                v.tensor_tensor(out=dlev,
                                in0=lut_t.rearrange("p (c two) -> p two c", two=2)[:, 1, :],
                                in1=lut_t.rearrange("p (c two) -> p two c", two=2)[:, 0, :],
                                op=AluOpType.subtract).then_inc(ve_dq, 1)
                vcnt += 1
                v.wait_ge(ve_dq, vcnt)
                for chk in range(NCHUNK):
                    C = NCHUNK * ot + chk
                    gidx = C // 2
                    sl = slice(CH * chk, CH * (chk + 1))
                    last = None
                    for c8 in range(8):
                        last = v.tensor_scalar(
                            out=tbuf[:, CH * c8:CH * (c8 + 1)], in0=bits[0][:, sl],
                            scalar1=dlev[:, c8:c8 + 1], scalar2=lut_t[:, 2 * c8:2 * c8 + 1],
                            op0=AluOpType.mult, op1=AluOpType.add)
                    last.then_inc(ve_dq, 1); vcnt += 1
                    v.wait_ge(ve_dq, vcnt)
                    for c4 in range(4):
                        last = v.select(ubuf[:, CH * c4:CH * (c4 + 1)], bits[1][:, sl],
                                        tbuf[:, CH * (2 * c4 + 1):CH * (2 * c4 + 2)],
                                        tbuf[:, CH * 2 * c4:CH * (2 * c4 + 1)], add_drain=True)
                    last.then_inc(ve_dq, 1); vcnt += 1
                    v.wait_ge(ve_dq, vcnt)
                    for c2 in range(2):
                        last = v.select(vbuf[:, CH * c2:CH * (c2 + 1)], bits[2][:, sl],
                                        ubuf[:, CH * (2 * c2 + 1):CH * (2 * c2 + 2)],
                                        ubuf[:, CH * 2 * c2:CH * (2 * c2 + 1)], add_drain=True)
                    last.then_inc(ve_dq, 1); vcnt += 1
                    v.wait_ge(ve_dq, vcnt)
                    if gidx >= 4:
                        v.wait_ge(pe_wt, gidx - 3)
                    v.select(wl_f2[gidx % 4][:, 512 * (C % 2):512 * (C % 2) + 512],
                             bits[3][:, sl], vbuf[:, CH:2 * CH], vbuf[:, 0:CH],
                             add_drain=True).then_inc(ve_dq, 1)
                    vcnt += 1
                    v.wait_ge(ve_dq, vcnt)
                    if C % 2 == 1 and gidx >= 2:
                        G = gidx - 2
                        v.wait_ge(pe_wt, G + 1)
                        if G >= 2:
                            v.wait_ge(exp_wl[G % 2], 16 * 8 * (G // 2))
                        v.tensor_copy(wlt_ev[G % 2], PD[:, 0:1024]).then_inc(ve_wt, 1)
                assert vcnt == (ot + 1) * VE_DQ_OT
            for G in (14, 15):
                v.wait_ge(pe_wt, G + 1)
                v.wait_ge(exp_wl[G % 2], 16 * 8 * (G // 2))
                v.tensor_copy(wlt_ev[G % 2], PD[:, 0:1024]).then_inc(ve_wt, 1)
            for H in range(NH):
                v.wait_ge(pe_avz, 16 * H + 8)
                v.reciprocal(rzrow[0:1, 0:512], PD[0:1, 0:512]).then_inc(ve_rz, 1)
                v.wait_ge(pe_avz, 16 * H + 16)
                v.reciprocal(rzrow[0:1, 512:1024], PD[0:1, 512:1024]).then_inc(ve_rz, 1)
                v.wait_ge(pe_rzb, 2 * H + 1)
                v.tensor_copy(rzb[:, 0:512], PD[:, 0:512]).then_inc(ve_fin, 1)
                v.wait_ge(ve_fin, 3 * H + 1)
                if H >= 2:
                    v.wait_ge(st_ao[H % 2], 16 * 8 * (H // 2))
                v.tensor_tensor(out=ao_sb[H % 2][:, 0:512], in0=PC[:, 0:512],
                                in1=rzb[:, 0:512],
                                op=AluOpType.mult).then_inc(ve_mid, 1)
                v.wait_ge(pe_rzb, 2 * H + 2)
                v.tensor_copy(rzb[:, 512:1024], PD[:, 512:1024]).then_inc(ve_fin, 1)
                v.wait_ge(ve_fin, 3 * H + 2)
                v.tensor_tensor(out=ao_sb[H % 2][:, 512:1024], in0=PC[:, 512:1024],
                                in1=rzb[:, 512:1024],
                                op=AluOpType.mult).then_inc(ve_mid, 1)
                v.wait_ge(ve_mid, 2 * H + 2)
                v.tensor_scalar(out=ao_sb[H % 2], in0=ao_sb[H % 2],
                                scalar1=bv_sb[:, H:H + 1], scalar2=None,
                                op0=AluOpType.add).then_inc(ve_fin, 1)
            for c in range(4):
                v.wait_ge(pe_bb, c + 1)
                v.tensor_copy(biasB[:, 1024 * c:1024 * (c + 1)], PD[:, 0:1024]).then_inc(ve_bb, 1)
            for oc in range(4):
                for tt in range(8):
                    i = 8 * oc + tt
                    v.wait_ge(pe6, i + 1)
                    v.wait_ge(ld_x6[i % 2], 16 * (i // 2 + 1))
                    if i >= 2:
                        v.wait_ge(st_x2[i % 2], 16 * (i // 2))
                    ps = PA if tt % 2 == 0 else PB
                    v.tensor_tensor(out=x2o[tt % 2], in0=ps[:, 0:1024],
                                    in1=biasB[:, 1024 * oc:1024 * (oc + 1)],
                                    op=AluOpType.add).then_inc(ve_mid, 1)
                    v.wait_ge(ve_mid, 64 + i + 1)
                    v.tensor_tensor(out=x2o[tt % 2], in0=x2o[tt % 2], in1=xch[tt % 2],
                                    op=AluOpType.add).then_inc(ve6, 1)
            for tt in range(8):
                ln_tile(tt, ld_x27, ve7, se7, pe7, x_io[tt % 2])
                for gq in range(4):
                    G = 4 * tt + gq
                    v.wait_ge(pe7, G + 1)
                    v.tensor_copy(xnr[:, 8 * gq:8 * gq + 8, 128 * tt:128 * (tt + 1)],
                                  pdr).then_inc(ve7b, 1)
            for c in range(4):
                v.wait_ge(pe_bb, 4 + c + 1)
                v.tensor_copy(biasB[:, 1024 * c:1024 * (c + 1)], PD[:, 0:1024]).then_inc(ve_bb, 1)
            for rp in range(4):
                for tt in range(8):
                    i = 8 * rp + tt
                    v.wait_ge(pe8, i + 1)
                    v.wait_ge(ld_x28[i % 2], 16 * (i // 2 + 1))
                    if i >= 2:
                        v.wait_ge(st_out[i % 2], 16 * (i // 2))
                    ps = PA if tt % 2 == 0 else PB
                    v.tensor_tensor(out=outb[tt % 2], in0=ps[:, 0:1024],
                                    in1=biasB[:, 1024 * rp:1024 * (rp + 1)],
                                    op=AluOpType.add).then_inc(ve_mid, 1)
                    v.wait_ge(ve_mid, 96 + i + 1)
                    v.tensor_tensor(out=outb[tt % 2], in0=outb[tt % 2], in1=x2ch[tt % 2],
                                    op=AluOpType.add).then_inc(ve8, 1)

        @block.scalar
        def _(sc):
            sc.wait_ge(ve0, 1)
            for tt in range(8):
                sc.wait_ge(ve1, VE1_TT * tt + 2)
                sc.activation(sq, mv[:, 1:2], AF.Sqrt, bias=eps_t, scale=1.0).then_inc(se1, 1)
            for mt in range(32):
                sc.wait_ge(pe2a, mt + 1)
                if mt >= 2:
                    sc.wait_ge(pe_vt, mt - 1)
                ps = (PA, PB, PC)[mt % 3]
                sc.activation(xvf[mt % 2], ps[:, 0:1024], AF.Identity).then_inc(se2a, 1)
                sc.wait_ge(pe_vt, mt + 1)
                if mt >= 2:
                    sc.wait_ge(exp_v[mt % 2], 16 * 8 * (mt // 2))
                sc.activation(vexp[mt % 2], PD[:, 0:1024], AF.Identity).then_inc(se_vx, 1)
            for mt in range(32):
                sc.wait_ge(pe2b, mt + 1)
                if mt >= 2:
                    pm = mt - 2
                    sc.wait_ge(exp_k[2 * (pm // 16) + pm % 2], 16 * ((pm % 16) // 2 + 1))
                ps = (PA, PB, PC)[mt % 3]
                sc.activation(kexp[mt % 2], ps[:, 0:1024], AF.Identity,
                              bias=bk_sb[:, mt:mt + 1], scale=1.0).then_inc(se2b, 1)
            for hg in range(NHG):
                for mtl in range(4):
                    mt = 4 * hg + mtl
                    sc.wait_ge(pe2b, 32 + mt + 1)
                    if hg >= 2 and mtl == 0:
                        sc.wait_ge(pe_sc, 16 * 4 * (hg - 1))
                    ps = PA if mt % 2 == 0 else PB
                    sc.activation(Q_hg[hg % 2][:, 1024 * mtl:1024 * (mtl + 1)], ps[:, 0:1024],
                                  AF.Identity, bias=bqs[:, mt:mt + 1], scale=SCALE).then_inc(se5q, 1)
                for hl in range(4):
                    H = 4 * hg + hl
                    for kt in range(16):
                        if kt == 6 and H >= 1:
                            Hp = H - 1
                            sc.wait_ge(ve_fin, 3 * Hp + 3)
                            for tt in range(8):
                                sc.dma_start(out=ao_scr[tt, :, Hp, :],
                                             in_=ao_sb[Hp % 2][:, 128 * tt:128 * (tt + 1)]).then_inc(st_ao[Hp % 2], 16)
                        sc.wait_ge(pe_sc, 16 * H + kt + 1)
                        if H >= 1:
                            sc.wait_ge(pe_avz, 16 * (H - 1) + kt + 1)
                        slot = PA if kt % 2 == 0 else PB
                        q0 = q0_of(kt)
                        sc.activation(attT[:, ATO[kt]:ATO[kt] + ATW[kt]],
                                      slot[:, q0:1024], AF.Exp).then_inc(se_exp, 1)
            sc.wait_ge(ve_fin, 3 * NH)
            for tt in range(8):
                sc.dma_start(out=ao_scr[tt, :, NH - 1, :],
                             in_=ao_sb[(NH - 1) % 2][:, 128 * tt:128 * (tt + 1)]).then_inc(st_ao[(NH - 1) % 2], 16)
            for oc in range(4):
                for tt in range(8):
                    i = 8 * oc + tt
                    sc.wait_ge(ve6, i + 1)
                    sc.dma_start(out=x2_scr[128 * tt:128 * (tt + 1), 1024 * oc:1024 * (oc + 1)],
                                 in_=x2o[tt % 2]).then_inc(st_x2[i % 2], 16)
            for tt in range(8):
                sc.wait_ge(ve7, VE1_TT * tt + 2)
                sc.activation(sq, mv[:, 1:2], AF.Sqrt, bias=eps_t, scale=1.0).then_inc(se7, 1)
            for rp in range(4):
                for tt in range(8):
                    i = 8 * rp + tt
                    sc.wait_ge(ve8, i + 1)
                    sc.dma_start(out=out_own[128 * tt:128 * (tt + 1), 1024 * rp:1024 * (rp + 1)],
                                 in_=outb[tt % 2]).then_inc(st_out[i % 2], 16)
            sc.wait_ge(st_out[0], 16 * 16)
            sc.wait_ge(st_out[1], 16 * 16)

        @block.tensor
        def _(te):
            te.wait_ge(ve0, 1)
            for tt in range(8):
                te.wait_ge(ve1, VE1_TT * (tt + 1))
                for gq in range(4):
                    G = 4 * tt + gq
                    if G > 0:
                        te.wait_ge(ve1b, G)
                    last = None
                    for j in range(8):
                        d = 8 * gq + j
                        last = te.transpose(PD[:, 128 * j:128 * (j + 1)],
                                            xnf[:, 128 * d:128 * (d + 1)], ident)
                    last.then_inc(pe1, 1)
            prev_pd = None
            for mt in range(33):
                if mt < 32:
                    te.wait_ge(ld_wv[mt % 2], 16 * (mt // 2 + 1))
                    if mt == 0:
                        te.wait_ge(ve1b, 32)
                    if mt >= 3:
                        te.wait_ge(se2a, mt - 2)
                    ps = (PA, PB, PC)[mt % 3]
                    last = None
                    for it in range(32):
                        for (c0, c1) in ((0, 512), (512, 1024)):
                            last = te.matmul(ps[:, c0:c1], wcol[mt % 2][:, 128 * it:128 * (it + 1)],
                                             xn_dm[:, 1024 * it + c0:1024 * it + c1],
                                             start=(it == 0), stop=(it == 31))
                    last.then_inc(pe2a, 1)
                if mt >= 1:
                    mp = mt - 1
                    te.wait_ge(se2a, mp + 1)
                    if prev_pd is not None:
                        te.wait_ge(*prev_pd)
                    last = None
                    for j in range(8):
                        last = te.transpose(PD[:, 128 * j:128 * (j + 1)],
                                            xvf[mp % 2][:, 128 * j:128 * (j + 1)], ident)
                    last.then_inc(pe_vt, 1)
                    prev_pd = (se_vx, mp + 1)
                    if mp % 2 == 1 and mp // 2 < 16:
                        G = mp // 2
                        te.wait_ge(ve_dq, dq_chunk_done(G // 4, 2 * (G % 4) + 1))
                        te.wait_ge(*prev_pd)
                        last = None
                        for j in range(8):
                            last = te.transpose(PD[:, 128 * j:128 * (j + 1)],
                                                wl_f2[G % 4][:, 128 * j:128 * (j + 1)], ident)
                        last.then_inc(pe_wt, 1)
                        prev_pd = (ve_wt, G + 1)
            for mt in range(32):
                te.wait_ge(ld_wk[mt % 2], 16 * (mt // 2 + 1))
                if mt < 3:
                    te.wait_ge(se2a, (31, 32, 30)[mt])
                if mt >= 3:
                    te.wait_ge(se2b, mt - 2)
                ps = (PA, PB, PC)[mt % 3]
                last = None
                for it in range(32):
                    for (c0, c1) in ((0, 512), (512, 1024)):
                        last = te.matmul(ps[:, c0:c1], wcol[mt % 2][:, 128 * it:128 * (it + 1)],
                                         xn_dm[:, 1024 * it + c0:1024 * it + c1],
                                         start=(it == 0), stop=(it == 31))
                last.then_inc(pe2b, 1)
            for hg in range(NHG):
                for mtl in range(4):
                    mt = 4 * hg + mtl
                    te.wait_ge(ld_wq[mt % 2], 16 * (mt // 2 + 1))
                    if hg == 0 and mtl < 2:
                        te.wait_ge(se2b, 31 + mtl)
                        te.wait_ge(ve_wt, 16)
                    if hg >= 1 and mtl < 2:
                        te.wait_ge(se_exp, 16 * 4 * hg)
                    if mtl >= 2:
                        te.wait_ge(se5q, 4 * hg + mtl - 1)
                    ps = PA if mt % 2 == 0 else PB
                    last = None
                    for it in range(32):
                        for (c0, c1) in ((0, 512), (512, 1024)):
                            last = te.matmul(ps[:, c0:c1], wcol[mt % 2][:, 128 * it:128 * (it + 1)],
                                             xn_dm[:, 1024 * it + c0:1024 * it + c1],
                                             start=(it == 0), stop=(it == 31))
                    last.then_inc(pe2b, 1)
                def mk_avz(H):
                    def avz_pair(s_):
                        if s_ == 0:
                            if H >= 1:
                                te.wait_ge(ve_mid, 2 * H)
                            else:
                                te.wait_ge(ve_wt, 16)
                        for kt in (2 * s_, 2 * s_ + 1):
                            te.wait_ge(se_exp, 16 * H + 2 * s_ + 2)
                            lastm = None
                            for (c0, c1) in q_ranges(kt):
                                stop = (kt == 7) if c1 <= 512 else (kt == 15)
                                lastm = te.matmul(
                                    PC[:, c0:c1], v_sb[H % 2][:, 128 * kt:128 * (kt + 1)],
                                    attT[:, ATO[kt] + c0 - q0_of(kt):ATO[kt] + c1 - q0_of(kt)],
                                    start=(kt == 0), stop=stop)
                            for (c0, c1) in q_ranges(kt):
                                stop = (kt == 7) if c1 <= 512 else (kt == 15)
                                lastm = te.matmul(
                                    PD[0:1, c0:c1], onesb[:, 0:1],
                                    attT[:, ATO[kt] + c0 - q0_of(kt):ATO[kt] + c1 - q0_of(kt)],
                                    start=(kt == 0), stop=stop)
                            lastm.then_inc(pe_avz, 1)
                    return avz_pair

                def rz_bcast(Hp, h2):
                        te.wait_ge(ve_rz, 2 * Hp + h2 + 1)
                        te.matmul(PD[:, 512 * h2:512 * (h2 + 1)], onesr[0:1, :],
                                  rzrow[0:1, 512 * h2:512 * (h2 + 1)],
                                  start=True, stop=True).then_inc(pe_rzb, 1)

                for hl in range(4):
                    H = 4 * hg + hl
                    te.wait_ge(ld_kv[H % 2], 16 * 8 * (H // 2 + 1))
                    avz_pair = mk_avz(H)
                    for s_ in range(8):
                        for kt in (2 * s_, 2 * s_ + 1):
                            tgt = 16 * H + kt - 1
                            if tgt > 0:
                                te.wait_ge(se_exp, tgt)
                            if kt == 0:
                                te.wait_ge(se5q, 4 * (hg + 1))
                            slot = PA if kt % 2 == 0 else PB
                            s_of = kt // 2
                            j_of = kt % 2
                            last = None
                            for (c0, c1) in q_ranges(kt):
                                has_mask = (c0 == q0_of(kt))
                                last = te.matmul(slot[:, c0:c1],
                                                 k_sb[H % 2][:, 128 * kt:128 * (kt + 1)],
                                                 Q_hg[hg % 2][:, 1024 * hl + c0:1024 * hl + c1],
                                                 start=True, stop=not has_mask)
                                if has_mask:
                                    last = te.matmul(
                                        slot[:, c0:c0 + 128], identb,
                                        masks[:, 256 * s_of + 128 * j_of:256 * s_of + 128 * (j_of + 1)],
                                        start=False, stop=True)
                            last.then_inc(pe_sc, 1)
                        if s_ == 1 and H >= 1:
                            rz_bcast(H - 1, 0)
                        if s_ == 2 and H >= 1:
                            rz_bcast(H - 1, 1)
                        if s_ >= 2:
                            avz_pair(s_ - 2)
                    avz_pair(6)
                    avz_pair(7)
                    if H == NH - 1:
                        rz_bcast(H, 0)
                        rz_bcast(H, 1)
            for c in range(4):
                if c == 0:
                    te.wait_ge(ve_fin, 3 * NH)
                te.wait_ge(ld_misc, 64 + 16 * (c + 1))
                if c > 0:
                    te.wait_ge(ve_bb, c)
                last = None
                for h2 in range(2):
                    last = te.matmul(PD[:, 512 * h2:512 * (h2 + 1)], onesr[0:1, :],
                                     rzrow[0:1, 512 * h2:512 * (h2 + 1)], start=True, stop=True)
                last.then_inc(pe_bb, 1)
            wo_bt = [wbig[:, 0:16384], wbig[:, 16384:32768],
                     xn_dm[:, 0:16384], xn_dm[:, 16384:32768]]
            wo_bt = [b.rearrange("p (it f) -> p it f", it=32) for b in wo_bt]
            for oc in range(4):
                if oc == 0:
                    te.wait_ge(ve_bb, 4)
                for tt in range(8):
                    i = 8 * oc + tt
                    te.wait_ge(ld_ao[i % 2], 16 * (i // 2 + 1))
                    if i < 2:
                        te.wait_ge(se_exp, 16 * NH)
                    if i >= 2:
                        te.wait_ge(ve6, i - 1)
                    ps = PA if tt % 2 == 0 else PB
                    last = None
                    for h in range(2):
                        u = 2 * oc + h
                        if tt == 0:
                            te.wait_ge(ld_wo[u % 4], 16 * 32 * (u // 4 + 1))
                        for it in range(32):
                            last = te.matmul(ps[:, 512 * h:512 * (h + 1)],
                                             aost[tt % 2][:, 128 * it:128 * (it + 1)],
                                             wo_bt[u % 4][:, it, :],
                                             start=(it == 0), stop=(it == 31))
                    last.then_inc(pe6, 1)
            for tt in range(8):
                te.wait_ge(ve7, VE1_TT * (tt + 1))
                for gq in range(4):
                    G = 4 * tt + gq
                    if G > 0:
                        te.wait_ge(ve7b, G)
                    else:
                        te.wait_ge(ve_bb, 4)
                    last = None
                    for j in range(8):
                        d = 8 * gq + j
                        last = te.transpose(PD[:, 128 * j:128 * (j + 1)],
                                            xnf[:, 128 * d:128 * (d + 1)], ident)
                    last.then_inc(pe7, 1)
            for c in range(4):
                if c == 0:
                    te.wait_ge(ve7b, 32)
                te.wait_ge(ld_misc, 128 + 16 * (c + 1))
                if c > 0:
                    te.wait_ge(ve_bb, 4 + c)
                last = None
                for h2 in range(2):
                    last = te.matmul(PD[:, 512 * h2:512 * (h2 + 1)], onesr[0:1, :],
                                     rzrow[0:1, 512 * h2:512 * (h2 + 1)], start=True, stop=True)
                last.then_inc(pe_bb, 1)
            wut = [wbig[:, 16384 * i:16384 * (i + 1)].rearrange(
                       "p (it f) -> p it f", it=32) for i in range(2)]
            wut.append(wunit3.rearrange("p (it f) -> p it f", it=32))
            for rp in range(4):
                if rp == 0:
                    te.wait_ge(ve_bb, 8)
                    te.wait_ge(ve6, 32)
                for tt in range(8):
                    i = 8 * rp + tt
                    ps = PA if tt % 2 == 0 else PB
                    if i >= 2:
                        te.wait_ge(ve8, i - 1)
                    last = None
                    for rb in range(2):
                        u = 2 * rp + rb
                        if tt == 0:
                            te.wait_ge(ld_wl[u % 3], 16 * 32 * (u // 3 + 1))
                        for it in range(32):
                            last = te.matmul(
                                ps[:, 512 * rb:512 * (rb + 1)],
                                xn_dm[:, 1024 * it + 128 * tt:1024 * it + 128 * (tt + 1)],
                                wut[u % 3][:, it, :],
                                start=(it == 0), stop=(it == 31))
                    last.then_inc(pe8, 1)
    return nc


def kernel(L0_attn_x, L0_attn_Wq, L0_attn_bq, L0_attn_Wk, L0_attn_bk,
           L0_attn_Wv, L0_attn_bv, L0_attn_Wo, L0_attn_bo,
           L0_ffn_qweight, L0_ffn_lut4, L0_ffn_lut8, L0_ffn_bias):
    x = np.asarray(L0_attn_x, dtype=np.float32)
    Wq = np.asarray(L0_attn_Wq, dtype=np.float32)
    Wk = np.asarray(L0_attn_Wk, dtype=np.float32)
    Wv = np.asarray(L0_attn_Wv, dtype=np.float32)
    Wo = np.asarray(L0_attn_Wo, dtype=np.float32)
    bq = np.asarray(L0_attn_bq, dtype=np.float32)
    bk = np.asarray(L0_attn_bk, dtype=np.float32)
    bv = np.asarray(L0_attn_bv, dtype=np.float32)
    bo = np.asarray(L0_attn_bo, dtype=np.float32)
    qw = np.asarray(L0_ffn_qweight)
    lut4 = np.asarray(L0_ffn_lut4, dtype=np.float32)
    bfb = np.asarray(L0_ffn_bias, dtype=np.float32)

    wq4 = np.ascontiguousarray(Wq.reshape(32, 128, 32, 128).transpose(0, 3, 2, 1))
    wk4 = np.ascontiguousarray(Wk.reshape(32, 128, 32, 128).transpose(0, 3, 2, 1))
    wv4 = np.ascontiguousarray(Wv.reshape(32, 128, 32, 128).transpose(0, 3, 2, 1))
    wot = np.ascontiguousarray(Wo.T)
    bq_t = np.ascontiguousarray(bq.reshape(32, 128).T)
    bk_t = np.ascontiguousarray(bk.reshape(32, 128).T)
    bv_t = np.ascontiguousarray(bv.reshape(32, 128).T)
    bo_row = np.ascontiguousarray(bo.reshape(1, 4096))
    bf_row = np.ascontiguousarray(bfb.reshape(1, 4096))

    def build_mask(parity):
        QT = QT_EVEN if parity == 0 else QT_ODD
        m = np.zeros((128, 2048), np.float32)
        for s in range(8):
            qt = QT[s]
            for j in range(2):
                kt = 2 * s + j
                if kt < qt:
                    t = np.zeros((128, 128), np.float32)
                elif kt == qt:
                    t = np.where(np.arange(128)[:, None] <= np.arange(128)[None, :],
                                 0.0, -30000.0).astype(np.float32)
                else:
                    t = np.full((128, 128), -30000.0, np.float32)
                m[:, 256 * s + 128 * j:256 * s + 128 * (j + 1)] = t
        return m.astype(BF)

    mask_by_par = [build_mask(0), build_mask(1)]

    nc = build_nc()
    in_maps = []
    for core in range(8):
        b = core // 2
        parity = core % 2
        QT = QT_EVEN if parity == 0 else QT_ODD
        rows = np.concatenate([128 * qt + np.arange(128) for qt in QT])
        in_maps.append({
            "x_own": np.ascontiguousarray(x[b][rows]),
            "wq4": wq4, "wk4": wk4, "wv4": wv4, "wot": wot,
            "bq_t": bq_t, "bk_t": bk_t, "bv_t": bv_t,
            "bo_row": bo_row, "bf_row": bf_row,
            "qw_own": np.ascontiguousarray(qw[0:4, 512 * core:512 * (core + 1), :]),
            "lut_own": np.ascontiguousarray(lut4[512 * core:512 * (core + 1), :]),
            "masks_in": mask_by_par[parity],
        })
    res = run_bass_kernel_spmd(nc, in_maps, core_ids=list(range(8)))
    out = np.empty((4, 2048, 4096), dtype=np.float32)
    for core in range(8):
        b = core // 2
        parity = core % 2
        QT = QT_EVEN if parity == 0 else QT_ODD
        rows = np.concatenate([128 * qt + np.arange(128) for qt in QT])
        out[b][rows] = res.results[core]["out_own"]
    return out
